# version 02
# baseline (speedup 1.0000x reference)
import numpy as np
import ml_dtypes
from contextlib import ExitStack

import concourse.bass as bass
import concourse.mybir as mybir
from concourse.bass_utils import run_bass_kernel_spmd

F32 = mybir.dt.float32
BF16 = mybir.dt.bfloat16
AF = mybir.ActivationFunctionType
ALU = mybir.AluOpType
AX = mybir.AxisListType

D = 1024
L = 4096
DI = 2048
DS = 16
DTR = 64
NLAY = 4
FF = 4096
NFB = FF // 128
LC = 512
NCH = L // LC
NDB = DI // 128
KD = D // 128
EPS = 1e-5
MAGIC = float(np.float32(1.5 * 2 ** 23))

ALL_PHASES = ["ln0", "m0", "m1", "m2", "m3", "mfin", "ffn", "ln1", "attkv", "attq", "rwkv"]

STREAMS = {
    "xn0":  ((D, L), F32, "ln0", ["m0", "ffn"]),
    "R1":   ((D, L), F32, "m0", ["m1"]),
    "R2":   ((D, L), F32, "m1", ["m2"]),
    "R3":   ((D, L), F32, "m2", ["m3"]),
    "R4":   ((D, L), F32, "m3", ["mfin"]),
    "x1p":  ((D, L), F32, "mfin", ["ffn"]),
    "x1":   ((D, L), F32, "ffn", ["ln1", "rwkv"]),
    "xn1":  ((D, L), F32, "ln1", ["attkv", "attq", "rwkv"]),
    "kfm":  ((D, L), BF16, "attkv", ["attq"]),
    "vtm":  ((L, D), BF16, "attkv", ["attq"]),
    "qfm":  ((D, L), BF16, "attkv", ["attq"]),
    "att":  ((D, L), F32, "attq", ["rwkv"]),
}
DEBUG_SCRATCH = False
DEBUG_PHASE_A = False
DEBUG_FFN = False
DEBUG_ATT = False

MSCRATCH = {
    "xc_s": ((DI, L), F32),
    "z_s":  ((DI, L), F32),
    "dt_s": ((DI, L), F32),
    "bc_s": ((32, L), BF16),
    "ym_s": ((DI, L), BF16),
}


class Build:
    def __init__(self, phases):
        self.phases = phases
        self.nc = bass.Bass("TRN2", target_bir_lowering=False)
        self.ex = ExitStack()
        self.sems = {}
        self.cnt = {}
        self._phase_idx = 0
        self._new_sems("g")
        self.inputs = {}
        self.outputs = {}
        self.dram = {}
        self.uid = 0

    def _new_sems(self, tag):
        for s in ["ld", "st", "pe", "act", "dve", "gp"]:
            self.sems[s] = self.ex.enter_context(self.nc.semaphore(f"{s}_{tag}"))
            self.cnt[s] = 0

    def begin_phase(self, name):
        if self._phase_idx > 0:
            self.bar()
        self._phase_idx += 1
        self._new_sems(f"p{self._phase_idx}_{name}")

    def ext_in(self, name, shape, dtype):
        if name not in self.dram:
            t = self.nc.dram_tensor(name, list(shape), dtype, kind="ExternalInput")
            self.inputs[name] = t
            self.dram[name] = t.ap()
        return self.dram[name]

    def ext_out(self, name, shape, dtype):
        if name not in self.dram:
            t = self.nc.dram_tensor(name, list(shape), dtype, kind="ExternalOutput")
            self.outputs[name] = t
            self.dram[name] = t.ap()
        return self.dram[name]

    def internal(self, name, shape, dtype):
        if name not in self.dram:
            t = self.nc.dram_tensor(name, list(shape), dtype, kind="Internal")
            self.dram[name] = t.ap()
        return self.dram[name]

    def stream(self, name):
        shape, dtype, prod, cons = STREAMS[name]
        key = "S_" + name
        if key in self.dram:
            return self.dram[key]
        if prod not in self.phases:
            return self.ext_in(key, shape, dtype)
        if any(c not in self.phases for c in cons):
            return self.ext_out(key, shape, dtype)
        return self.internal(key, shape, dtype)

    def mscratch(self, name):
        shape, dtype = MSCRATCH[name]
        if DEBUG_SCRATCH:
            return self.ext_out("M_" + name, shape, dtype)
        return self.internal("M_" + name, shape, dtype)

    def _bump(self, instr, sem, amt=1):
        instr.then_inc(self.sems[sem], amt)
        self.cnt[sem] += amt

    def ld(self, out_ap, in_ap):
        self._bump(self.nc.sync.dma_start(out=out_ap, in_=in_ap), "ld", 16)

    def st(self, out_ap, in_ap):
        self._bump(self.nc.gpsimd.dma_start(out=out_ap, in_=in_ap), "st", 16)

    def pe(self, meth, *a, **k):
        self._bump(getattr(self.nc.tensor, meth)(*a, **k), "pe")

    @staticmethod
    def _outsz(a, k):
        ap = k.get("out", a[0] if a else None)
        try:
            sz = 1
            for d in ap.shape:
                sz *= d
            return sz
        except Exception:
            return 1 << 30

    def v(self, meth, *a, **k):
        self._bump(getattr(self.nc.vector, meth)(*a, **k), "dve")
        if self._outsz(a, k) <= 8192:
            self.nc.vector.wait_ge(self.sems["dve"], self.cnt["dve"])

    def s(self, meth, *a, **k):
        self._bump(getattr(self.nc.scalar, meth)(*a, **k), "act")
        if self._outsz(a, k) <= 8192:
            self.nc.scalar.wait_ge(self.sems["act"], self.cnt["act"])

    def wait(self, eng_name, *sems):
        eng = getattr(self.nc, eng_name)
        for s in sems:
            if self.cnt[s] > 0:
                eng.wait_ge(self.sems[s], self.cnt[s])

    def bar(self, engines=("tensor", "vector", "scalar", "gpsimd", "sync")):
        for e in engines:
            self.wait(e, "ld", "st", "pe", "act", "dve", "gp")

    def finish(self):
        self.nc.sync.wait_ge(self.sems["st"], self.cnt["st"])
        self.ex.close()



def emit_ln_stream(b, tag, src_ap, dst_ap, w_col=None, b_col=None, add_ap=None):
    nc = b.nc
    with ExitStack() as stk:
        sb = lambda shape, dt, nm: stk.enter_context(nc.sbuf_tensor(f"{tag}_{nm}", list(shape), dt))
        ps = lambda shape, dt, nm: stk.enter_context(nc.psum_tensor(f"{tag}p_{nm}", list(shape), dt))
        xt = [sb([128, LC], F32, f"x{i}") for i in range(KD)]
        sq = sb([128, LC], F32, "sq")
        ot = sb([128, LC], F32, "ot")
        addt = sb([128, LC], F32, "add") if add_ap is not None else None
        mu = sb([1, LC], F32, "mu")
        ex2 = sb([1, LC], F32, "ex2")
        rstd = sb([1, LC], F32, "rstd")
        onesc = sb([128, 1], F32, "onesc")
        onesr = sb([1, 128], F32, "onesr")
        epsT = sb([1, 1], F32, "eps")
        ps_sum = ps([1, LC], F32, "sum")
        ps_sq = ps([1, LC], F32, "sqs")
        ps_mu = ps([128, LC], F32, "mub")
        ps_rs = ps([128, LC], F32, "rsb")
        b.ld(onesc[:], b.dram["C_ones128"])
        b.ld(onesr[:], b.dram["C_ones1x128"])
        b.v("memset", epsT[:], EPS)
        b.bar()
        for c in range(NCH):
            sl = slice(c * LC, (c + 1) * LC)
            b.bar()
            for kb in range(KD):
                b.ld(xt[kb][:], src_ap[kb * 128:(kb + 1) * 128, sl])
            b.wait("tensor", "ld")
            b.wait("scalar", "ld")
            for kb in range(KD):
                b.pe("matmul", ps_sum[:, :], onesc[:], xt[kb][:],
                     start=(kb == 0), stop=(kb == KD - 1))
            for kb in range(KD):
                if kb > 0:
                    b.wait("scalar", "pe")
                b.s("activation", out=sq[:], in_=xt[kb][:], func=AF.Square)
                b.wait("tensor", "act")
                b.pe("matmul", ps_sq[:, :], onesc[:], sq[:],
                     start=(kb == 0), stop=(kb == KD - 1))
            b.wait("vector", "pe")
            b.v("tensor_scalar_mul", mu[:, :], ps_sum[:, :], 1.0 / D)
            b.v("tensor_scalar_mul", ex2[:, :], ps_sq[:, :], 1.0 / D)
            b.v("tensor_tensor", rstd[:, :], mu[:, :], mu[:, :], ALU.mult)
            b.v("tensor_tensor", rstd[:, :], ex2[:, :], rstd[:, :], ALU.subtract)
            b.wait("scalar", "dve")
            b.s("activation", out=rstd[:], in_=rstd[:], func=AF.Sqrt, bias=epsT[0:1, 0:1])
            b.wait("vector", "act")
            b.v("reciprocal", rstd[:], rstd[:])
            b.wait("tensor", "dve")
            b.pe("matmul", ps_mu[:, :], onesr[:], mu[:, :], start=True, stop=True)
            b.pe("matmul", ps_rs[:, :], onesr[:], rstd[:, :], start=True, stop=True)
            b.wait("vector", "pe")
            b.wait("sync", "dve", "act", "st")
            for kb in range(KD):
                if add_ap is not None:
                    b.ld(addt[:], add_ap[kb * 128:(kb + 1) * 128, sl])
                b.v("tensor_tensor", ot[:], xt[kb][:], ps_mu[:, :], ALU.subtract)
                b.v("tensor_tensor", ot[:], ot[:], ps_rs[:, :], ALU.mult)
                if w_col is not None:
                    b.wait("scalar", "dve")
                    b.s("activation", out=ot[:], in_=ot[:], func=AF.Identity,
                        bias=b_col[:, kb:kb + 1], scale=w_col[:, kb:kb + 1])
                if add_ap is not None:
                    b.wait("vector", "ld")
                    if w_col is not None:
                        b.wait("vector", "act")
                    b.v("tensor_tensor", ot[:], ot[:], addt[:], ALU.add)
                b.bar(("gpsimd",))
                b.st(dst_ap[kb * 128:(kb + 1) * 128, sl], ot[:])
                b.wait("vector", "st")
                b.wait("scalar", "st")
                b.wait("sync", "st")
        b.bar()


def emit_w_to_bf16(b, tag, stk, src_ap, nk, width, layout_width=None):
    nc = b.nc
    lw = layout_width or width
    dst = stk.enter_context(nc.sbuf_tensor(f"{tag}_wb", [128, nk * lw], BF16))
    stg = stk.enter_context(nc.sbuf_tensor(f"{tag}_wstg", [128, width], F32))
    for kb in range(nk):
        b.wait("sync", "dve")
        b.ld(stg[:], src_ap[kb * 128:(kb + 1) * 128, :])
        b.wait("vector", "ld")
        b.v("tensor_copy", dst[:, kb * lw:kb * lw + width], stg[:])
    b.bar()
    return dst



def emit_ln0(b):
    x = b.ext_in("x", (D, L), F32)
    emit_ln_stream(b, "ln0", x, b.stream("xn0"))


def emit_ln1(b):
    emit_ln_stream(b, "ln1", b.stream("x1"), b.stream("xn1"))


def emit_mfin(b):
    x = b.ext_in("x", (D, L), F32)
    with ExitStack() as stk:
        nwc = stk.enter_context(b.nc.sbuf_tensor("mfin_nw", [128, KD], F32))
        nbc = stk.enter_context(b.nc.sbuf_tensor("mfin_nb", [128, KD], F32))
        b.ld(nwc[:], b.ext_in("normf_w", (128, KD), F32))
        b.ld(nbc[:], b.ext_in("normf_b", (128, KD), F32))
        b.bar()
        emit_ln_stream(b, "mfin", b.stream("R4"), b.stream("x1p"),
                       w_col=nwc, b_col=nbc, add_ap=x)


def emit_mamba_layer(b, k):
    nc = b.nc
    rin = b.stream("xn0") if k == 0 else b.stream(f"R{k}")
    rout = b.stream(f"R{k + 1}")
    xc_s = b.mscratch("xc_s")
    z_s = b.mscratch("z_s")
    dt_s = b.mscratch("dt_s")
    bc_s = b.mscratch("bc_s")
    ym_s = b.mscratch("ym_s")

    ipW = b.ext_in(f"ipW{k}", (D, 2 * DI), F32)
    xpW = b.ext_in(f"xpW{k}", (DI, 96), F32)
    dpW = b.ext_in(f"dpW{k}", (DTR, DI), F32)
    opW = b.ext_in(f"opW{k}", (DI, D), F32)
    cw = b.ext_in(f"cw{k}", (DI, 4), F32)
    cb = b.ext_in(f"cb{k}", (128, NDB), F32)
    nw = b.ext_in(f"nw{k}", (128, KD), F32)
    nb = b.ext_in(f"nb{k}", (128, KD), F32)
    dtb = b.ext_in(f"dtb{k}", (128, NDB), F32)
    alog = b.ext_in(f"alog{k}", (DI, DS), F32)
    dskip = b.ext_in(f"dskip{k}", (128, NDB), F32)
    selBC = b.ext_in("selBC", (32, 32 * 128), BF16)
    if DEBUG_PHASE_A:
        xnd_dbg = b.ext_out(f"dbg_xn{k}", (D, L), BF16)
        xcpre_dbg = b.ext_out(f"dbg_xcpre{k}", (DI, L), F32)

    with ExitStack() as stk:
        sb = lambda shape, dt, nm: stk.enter_context(nc.sbuf_tensor(f"A{k}_{nm}", list(shape), dt))
        ps = lambda shape, dt, nm: stk.enter_context(nc.psum_tensor(f"A{k}p_{nm}", list(shape), dt))
        ipWb = emit_w_to_bf16(b, f"A{k}ip", stk, ipW, KD, 2 * DI)
        xpWb = emit_w_to_bf16(b, f"A{k}xp", stk, xpW, NDB, 96)
        dpWb = sb([64, DI], BF16, "dpWb")
        wstg = sb([64, DI], F32, "dpstg")
        b.ld(wstg[:], dpW)
        b.wait("vector", "ld")
        b.v("tensor_copy", dpWb[:], wstg[:])
        cwt = sb([128, NDB * 4], F32, "cwt")
        cbt = sb([128, NDB], F32, "cbt")
        nwt = sb([128, KD], F32, "nwt")
        nbt = sb([128, KD], F32, "nbt")
        dtbt = sb([128, NDB], F32, "dtbt")
        onesc = sb([128, 1], F32, "onesc")
        onesr = sb([1, 128], F32, "onesr")
        epsT = sb([1, 1], F32, "eps")
        b.v("memset", epsT[:], EPS)
        b.ld(onesc[:], b.dram["C_ones128"])
        b.ld(onesr[:], b.dram["C_ones1x128"])
        for db in range(NDB):
            b.ld(cwt[:, db * 4:(db + 1) * 4], cw[db * 128:(db + 1) * 128, :])
        b.ld(cbt[:], cb)
        b.ld(nwt[:], nw)
        b.ld(nbt[:], nb)
        b.ld(dtbt[:], dtb)

        rt = [sb([128, LC], F32, f"rt{i}") for i in range(KD)]
        xnb = sb([128, KD * LC], BF16, "xnb")
        sq = sb([128, LC], F32, "sq")
        mu = sb([1, LC], F32, "mu")
        ex2 = sb([1, LC], F32, "ex2")
        rstd = sb([1, LC], F32, "rstd")
        ext = [sb([128, LC + 3], F32, f"ext{i}") for i in range(2)]
        tail = [sb([128, 4], F32, f"tail{i}") for i in range(NDB)]
        xcf = sb([128, LC], F32, "xcf")
        xcb = sb([128, NDB * LC], BF16, "xcb")
        zt = sb([128, LC], F32, "zt")
        xdbb = sb([96, LC], BF16, "xdbb")
        t1 = sb([128, LC], F32, "t1")
        t2 = sb([128, LC], F32, "t2")
        ps_sum = ps([1, LC], F32, "sum")
        ps_sq = ps([1, LC], F32, "sqs")
        ps_mu = ps([128, LC], F32, "mub")
        ps_rs = ps([128, LC], F32, "rsb")
        ps_a = ps([128, LC], F32, "a0")
        ps_b2 = ps([128, LC], F32, "a1")
        ps_x = ps([96, LC], F32, "x96")

        for db in range(NDB):
            b.v("memset", tail[db][:], 0.0)
        b.bar()

        for c in range(NCH):
            sl = slice(c * LC, (c + 1) * LC)
            b.bar()
            for kb in range(KD):
                b.ld(rt[kb][:], rin[kb * 128:(kb + 1) * 128, sl])
            b.wait("tensor", "ld")
            b.wait("scalar", "ld")
            for kb in range(KD):
                b.pe("matmul", ps_sum[:, :], onesc[:], rt[kb][:],
                     start=(kb == 0), stop=(kb == KD - 1))
            for kb in range(KD):
                if kb > 0:
                    b.wait("scalar", "pe")
                b.s("activation", out=sq[:], in_=rt[kb][:], func=AF.Square)
                b.wait("tensor", "act")
                b.pe("matmul", ps_sq[:, :], onesc[:], sq[:],
                     start=(kb == 0), stop=(kb == KD - 1))
            b.wait("vector", "pe")
            b.v("tensor_scalar_mul", mu[:, :], ps_sum[:, :], 1.0 / D)
            b.v("tensor_scalar_mul", ex2[:, :], ps_sq[:, :], 1.0 / D)
            b.v("tensor_tensor", rstd[:, :], mu[:, :], mu[:, :], ALU.mult)
            b.v("tensor_tensor", rstd[:, :], ex2[:, :], rstd[:, :], ALU.subtract)
            b.wait("scalar", "dve")
            b.s("activation", out=rstd[:], in_=rstd[:], func=AF.Sqrt, bias=epsT[0:1, 0:1])
            b.wait("vector", "act")
            b.v("reciprocal", rstd[:], rstd[:])
            b.wait("tensor", "dve")
            b.pe("matmul", ps_mu[:, :], onesr[:], mu[:, :], start=True, stop=True)
            b.pe("matmul", ps_rs[:, :], onesr[:], rstd[:, :], start=True, stop=True)
            b.wait("vector", "pe")
            for kb in range(KD):
                b.wait("vector", "act")
                b.v("tensor_tensor", sq[:], rt[kb][:], ps_mu[:, :], ALU.subtract)
                b.v("tensor_tensor", sq[:], sq[:], ps_rs[:, :], ALU.mult)
                b.wait("scalar", "dve")
                b.s("activation", out=xnb[:, kb * LC:(kb + 1) * LC], in_=sq[:], func=AF.Identity,
                    bias=nbt[:, kb:kb + 1], scale=nwt[:, kb:kb + 1])
            b.bar()
            if DEBUG_PHASE_A:
                for kb in range(KD):
                    b.st(xnd_dbg[kb * 128:(kb + 1) * 128, sl], xnb[:, kb * LC:(kb + 1) * LC])
                b.bar()
            for db in range(NDB):
                pst = ps_a if db % 2 == 0 else ps_b2
                b.wait("tensor", "act", "dve")
                for kb in range(KD):
                    b.pe("matmul", pst[:, :],
                         ipWb[:, kb * 2 * DI + db * 128: kb * 2 * DI + (db + 1) * 128],
                         xnb[:, kb * LC:(kb + 1) * LC],
                         start=(kb == 0), stop=(kb == KD - 1))
                e = ext[db % 2]
                b.wait("scalar", "pe", "dve")
                b.s("copy", e[:, 3:3 + LC], pst[:, :])
                if DEBUG_PHASE_A:
                    b.bar()
                    b.st(xcpre_dbg[db * 128:(db + 1) * 128, sl], e[:, 3:3 + LC])
                    b.bar()
                b.wait("vector", "act", "st")
                b.v("tensor_copy", e[:, 0:3], tail[db][:, 0:3])
                b.v("tensor_scalar", xcf[:], e[:, 0:LC],
                    cwt[:, db * 4:db * 4 + 1], cbt[:, db:db + 1], op0=ALU.mult, op1=ALU.add)
                for tap in range(1, 4):
                    b.v("scalar_tensor_tensor", out=xcf[:], in0=e[:, tap:tap + LC],
                        scalar=cwt[:, db * 4 + tap:db * 4 + tap + 1], in1=xcf[:],
                        op0=ALU.mult, op1=ALU.add)
                b.v("tensor_copy", tail[db][:, 0:3], e[:, LC:LC + 3])
                b.wait("scalar", "dve")
                b.s("activation", out=xcf[:], in_=xcf[:], func=AF.Silu)
                b.wait("vector", "act")
                b.v("tensor_copy", xcb[:, db * LC:(db + 1) * LC], xcf[:])
                b.bar(("gpsimd",))
                b.st(xc_s[db * 128:(db + 1) * 128, sl], xcf[:])
            for db in range(NDB):
                pst = ps_a if db % 2 == 0 else ps_b2
                b.wait("tensor", "act", "dve")
                for kb in range(KD):
                    b.pe("matmul", pst[:, :],
                         ipWb[:, kb * 2 * DI + DI + db * 128: kb * 2 * DI + DI + (db + 1) * 128],
                         xnb[:, kb * LC:(kb + 1) * LC],
                         start=(kb == 0), stop=(kb == KD - 1))
                b.wait("scalar", "pe", "st")
                b.s("copy", zt[:], pst[:, :])
                b.bar(("gpsimd",))
                b.st(z_s[db * 128:(db + 1) * 128, sl], zt[:])
            b.wait("tensor", "act", "dve")
            for db in range(NDB):
                b.pe("matmul", ps_x[:, :], xpWb[:, db * 96:(db + 1) * 96],
                     xcb[:, db * LC:(db + 1) * LC],
                     start=(db == 0), stop=(db == NDB - 1))
            b.wait("scalar", "pe", "st")
            b.s("copy", xdbb[:], ps_x[:, :])
            b.bar(("gpsimd",))
            b.st(bc_s[:, sl], xdbb[64:96, :])
            for db in range(NDB):
                pst = ps_a if db % 2 == 0 else ps_b2
                b.wait("tensor", "act", "dve", "st")
                b.pe("matmul", pst[:, :], dpWb[:, db * 128:(db + 1) * 128],
                     xdbb[0:64, :], start=True, stop=True)
                b.wait("scalar", "pe")
                b.s("activation", out=t1[:], in_=pst[:, :], func=AF.Abs,
                    bias=dtbt[:, db:db + 1])
                b.wait("vector", "act")
                b.v("tensor_scalar", t2[:], pst[:, :], dtbt[:, db:db + 1], 0.0,
                    op0=ALU.add, op1=ALU.max)
                b.s("activation", out=t1[:], in_=t1[:], func=AF.Exp, scale=-1.0)
                b.s("activation", out=t1[:], in_=t1[:], func=AF.Ln, bias=1.0)
                b.wait("vector", "act")
                b.v("tensor_tensor", t2[:], t2[:], t1[:], ALU.add)
                b.bar(("gpsimd",))
                b.st(dt_s[db * 128:(db + 1) * 128, sl], t2[:])
                b.wait("scalar", "st")
                b.wait("vector", "st")
        b.bar()

    with ExitStack() as stk:
        sb = lambda shape, dt, nm: stk.enter_context(nc.sbuf_tensor(f"B{k}_{nm}", list(shape), dt))
        ps = lambda shape, dt, nm: stk.enter_context(nc.psum_tensor(f"B{k}p_{nm}", list(shape), dt))
        dtt = sb([128, L], F32, "dt")
        xct = sb([128, L], F32, "xc")
        ut = sb([128, L], F32, "u")
        dA = sb([128, L], F32, "dA")
        G = sb([128, L], F32, "G")
        ht = sb([128, L], F32, "h")
        hc = sb([128, L], F32, "hc")
        ya = sb([128, L], F32, "ya")
        zt = sb([128, L], F32, "z")
        bct = sb([32, L], BF16, "bc")
        selt = sb([32, 32 * 128], BF16, "sel")
        at = sb([128, DS * NDB], F32, "A")
        dst_ = sb([128, NDB], F32, "dsk")
        ymb = sb([128, L], BF16, "ymb")
        psb = [ps([128, LC], F32, f"b{i}") for i in range(2)]
        psc = [ps([128, LC], F32, f"c{i}") for i in range(2)]
        b.ld(selt[:], selBC)
        b.ld(bct[:], bc_s)
        b.ld(dst_[:], dskip)
        for db in range(NDB):
            b.ld(at[:, db * DS:(db + 1) * DS], alog[db * 128:(db + 1) * 128, :])
        b.bar()
        b.s("activation", out=at[:], in_=at[:], func=AF.Exp)
        b.wait("vector", "act")
        b.v("tensor_scalar_mul", at[:], at[:], -1.0)
        b.bar()
        for db in range(NDB):
            b.bar()
            b.ld(dtt[:], dt_s[db * 128:(db + 1) * 128, :])
            b.ld(xct[:], xc_s[db * 128:(db + 1) * 128, :])
            b.ld(zt[:], z_s[db * 128:(db + 1) * 128, :])
            b.wait("vector", "ld")
            b.v("tensor_tensor", ut[:], dtt[:], xct[:], ALU.mult)
            b.v("memset", ya[:], 0.0)
            for n in range(DS):
                col = db * DS + n
                b.wait("scalar", "dve" if (n > 0 or db > 0) else "ld")
                b.s("activation", out=dA[:], in_=dtt[:], func=AF.Exp,
                    scale=at[:, col:col + 1])
                for j in range(NCH):
                    pb = psb[j % 2]
                    b.wait("tensor", "dve")
                    b.pe("matmul", pb[:, :], selt[:, n * 128:(n + 1) * 128],
                         bct[:, j * LC:(j + 1) * LC], start=True, stop=True)
                    b.wait("vector", "pe", "act")
                    b.v("tensor_tensor", G[:, j * LC:(j + 1) * LC],
                        ut[:, j * LC:(j + 1) * LC], pb[:, :], ALU.mult)
                b.v("tensor_tensor_scan", out=ht[:], data0=dA[:], data1=G[:],
                    initial=0.0, op0=ALU.mult, op1=ALU.add)
                for j in range(NCH):
                    pc = psc[j % 2]
                    b.wait("tensor", "dve")
                    b.pe("matmul", pc[:, :], selt[:, (DS + n) * 128:(DS + n + 1) * 128],
                         bct[:, j * LC:(j + 1) * LC], start=True, stop=True)
                    b.wait("vector", "pe")
                    b.v("tensor_tensor", hc[:, j * LC:(j + 1) * LC],
                        ht[:, j * LC:(j + 1) * LC], pc[:, :], ALU.mult)
                b.v("tensor_tensor", ya[:], ya[:], hc[:], ALU.add)
            b.wait("scalar", "dve", "ld")
            b.s("activation", out=zt[:], in_=zt[:], func=AF.Silu)
            b.v("scalar_tensor_tensor", out=ya[:], in0=xct[:], scalar=dst_[:, db:db + 1],
                in1=ya[:], op0=ALU.mult, op1=ALU.add)
            b.wait("vector", "act")
            b.v("tensor_tensor", ymb[:], ya[:], zt[:], ALU.mult)
            b.bar(("gpsimd",))
            b.st(ym_s[db * 128:(db + 1) * 128, :], ymb[:])
            b.wait("sync", "st")
            b.wait("scalar", "st")
        b.bar()

    with ExitStack() as stk:
        sb = lambda shape, dt, nm: stk.enter_context(nc.sbuf_tensor(f"C{k}_{nm}", list(shape), dt))
        ps = lambda shape, dt, nm: stk.enter_context(nc.psum_tensor(f"C{k}p_{nm}", list(shape), dt))
        opWb = emit_w_to_bf16(b, f"C{k}op", stk, opW, NDB, D)
        ymt = sb([128, NDB * LC], BF16, "ym")
        rt = sb([128, LC], F32, "rt")
        ot = sb([128, LC], F32, "ot")
        pso = [ps([128, LC], F32, f"o{i}") for i in range(2)]
        for c in range(NCH):
            sl = slice(c * LC, (c + 1) * LC)
            b.bar()
            for db in range(NDB):
                b.ld(ymt[:, db * LC:(db + 1) * LC], ym_s[db * 128:(db + 1) * 128, sl])
            b.wait("tensor", "ld")
            for ob in range(KD):
                pst = pso[ob % 2]
                b.wait("tensor", "dve")
                for db in range(NDB):
                    b.pe("matmul", pst[:, :], opWb[:, db * D + ob * 128:db * D + (ob + 1) * 128],
                         ymt[:, db * LC:(db + 1) * LC],
                         start=(db == 0), stop=(db == NDB - 1))
                b.wait("sync", "dve")
                b.ld(rt[:], rin[ob * 128:(ob + 1) * 128, sl])
                b.wait("vector", "pe", "ld", "st")
                b.v("tensor_tensor", ot[:], pst[:, :], rt[:], ALU.add)
                b.bar(("gpsimd",))
                b.st(rout[ob * 128:(ob + 1) * 128, sl], ot[:])
        b.bar()


def emit_ffn(b):
    nc = b.nc
    xn0 = b.stream("xn0")
    x1p = b.stream("x1p")
    x1 = b.stream("x1")
    WfcT = b.ext_in("WfcT", (D, FF), F32)
    WprT = b.ext_in("WprT", (FF, D), F32)
    AfcT = b.ext_in("AfcT", (D, 16), F32)
    BfcT = b.ext_in("BfcT", (16, FF), F32)
    AprT = b.ext_in("AprT", (FF, 16), F32)
    BprT = b.ext_in("BprT", (16, D), F32)
    bfc = b.ext_in("bfc", (128, NFB), F32)
    bpr = b.ext_in("bpr", (128, KD), F32)
    smax_d = b.internal("FF_smax", (1, 128), F32)
    if DEBUG_FFN:
        rfc_dbg = b.ext_out("dbg_rfc", (D, FF), BF16)
        sc_dbg = b.ext_out("dbg_sc", (128, 4), F32)
        h1_dbg = b.ext_out("dbg_h1", (FF, L), BF16)

    with ExitStack() as stk:
        sb = lambda shape, dt, nm: stk.enter_context(nc.sbuf_tensor(f"F_{nm}", list(shape), dt))
        ps = lambda shape, dt, nm: stk.enter_context(nc.psum_tensor(f"Fp_{nm}", list(shape), dt))
        rfc = sb([128, KD * FF], BF16, "rfc")
        rpr = sb([128, NFB * D], BF16, "rpr")
        sval = sb([1, 4], F32, "sval")
        svals = sb([1, 2], F32, "svals")
        sfcb = sb([128, 1], F32, "sfcb")
        sprb = sb([128, 1], F32, "sprb")
        rfcb = sb([128, 1], F32, "rfcb")
        rprb = sb([128, 1], F32, "rprb")
        rsval = sb([1, 2], F32, "rsval")
        onesr = sb([1, 128], F32, "onesr")
        AfcTb = sb([128, KD * 16], BF16, "AfcTb")
        BfcTb = sb([16, FF], BF16, "BfcTb")
        AprTb = sb([128, NFB * 16], BF16, "AprTb")
        BprTb = sb([16, D], BF16, "BprTb")
        bfct = sb([128, NFB], F32, "bfct")
        bprt = sb([128, KD], F32, "bprt")
        ps_bc = ps([128, 1], F32, "bc")
        stk2 = ExitStack()
        sb2 = lambda shape, dt, nm: stk2.enter_context(nc.sbuf_tensor(f"F2_{nm}", list(shape), dt))
        wstg = sb2([128, FF], F32, "wstg")
        rmax = sb2([128, 2], F32, "rmax")
        permax = sb2([128, 32], F32, "permax")
        rowall = sb2([1, 128 * 32], F32, "rowall")
        b.ld(onesr[:], b.dram["C_ones1x128"])
        b.ld(bfct[:], bfc)
        b.ld(bprt[:], bpr)
        for a in range(KD):
            b.ld(wstg[:, a * 16:(a + 1) * 16], AfcT[a * 128:(a + 1) * 128, :])
        for a in range(NFB):
            b.ld(wstg[:, 128 + a * 16:128 + (a + 1) * 16], AprT[a * 128:(a + 1) * 128, :])
        b.ld(wstg[0:16, 1024:1024 + D], BprT)
        b.wait("vector", "ld")
        b.v("tensor_copy", AfcTb[:], wstg[:, 0:KD * 16])
        b.v("tensor_copy", AprTb[:], wstg[:, 128:128 + NFB * 16])
        b.v("tensor_copy", BprTb[:], wstg[0:16, 1024:1024 + D])
        b.wait("sync", "dve")
        b.ld(wstg[0:16, 0:FF], BfcT)
        b.wait("vector", "ld")
        b.v("tensor_copy", BfcTb[:], wstg[0:16, 0:FF])
        b.bar()
        smax2 = b.internal("FF_smax2", (128, 32), F32)
        for which, W, nk, width, dstn in ((0, WfcT, KD, FF, rfc), (1, WprT, NFB, D, rpr)):
            b.v("memset", permax[:], 0.0)
            for kb in range(nk):
                b.wait("sync", "dve")
                b.ld(wstg[:, 0:width], W[kb * 128:(kb + 1) * 128, :])
                b.wait("vector", "ld")
                b.v("reduce_max", permax[:, kb:kb + 1], wstg[:, 0:width], axis=AX.X,
                    apply_absolute_value=True)
            b.bar(("gpsimd",))
            b.st(smax2, permax[:])
            b.wait("sync", "st")
            b.ld(rowall[:], smax2.rearrange("p n -> (p n)").rearrange("(o f) -> o f", o=1))
            b.wait("vector", "ld")
            b.v("reduce_max", sval[:, which:which + 1], rowall[:], axis=AX.X)
        b.v("tensor_scalar", svals[:, 0:2], sval[:, 0:2], 1.0 / 127.0, None, op0=ALU.mult)
        b.v("reciprocal", rsval[:, 0:2], svals[:, 0:2])
        b.wait("tensor", "dve")
        for src_c, dstb in ((svals[0:1, 0:1], sfcb), (svals[0:1, 1:2], sprb),
                            (rsval[0:1, 0:1], rfcb), (rsval[0:1, 1:2], rprb)):
            b.pe("matmul", ps_bc[:, :], onesr[:], src_c, start=True, stop=True)
            b.wait("scalar", "pe")
            b.s("copy", dstb[:], ps_bc[:, :])
            b.wait("tensor", "act")
        b.bar()
        for which, W, nk, width, dstn, scol in (
                (0, WfcT, KD, FF, rfc, rfcb), (1, WprT, NFB, D, rpr, rprb)):
            for kb in range(nk):
                b.wait("sync", "dve")
                b.ld(wstg[:, 0:width], W[kb * 128:(kb + 1) * 128, :])
                b.wait("vector", "ld")
                b.v("tensor_scalar", wstg[:, 0:width], wstg[:, 0:width], scol[:], None,
                    op0=ALU.mult)
                b.v("tensor_scalar", wstg[:, 0:width], wstg[:, 0:width], MAGIC, None,
                    op0=ALU.add)
                b.v("tensor_scalar", wstg[:, 0:width], wstg[:, 0:width], MAGIC, None,
                    op0=ALU.subtract)
                b.v("tensor_scalar", wstg[:, 0:width], wstg[:, 0:width], 127.0, -128.0,
                    op0=ALU.min, op1=ALU.max)
                b.v("tensor_copy", dstn[:, kb * width:(kb + 1) * width], wstg[:, 0:width])
        b.bar()
        stk2.close()
        if DEBUG_FFN:
            for kb in range(KD):
                b.st(rfc_dbg[kb * 128:(kb + 1) * 128, :], rfc[:, kb * FF:(kb + 1) * FF])
            for _i, _t in enumerate((sfcb, sprb, rfcb, rprb)):
                with b.nc.allow_non_contiguous_dma(reason="debug tap"):
                    b.st(sc_dbg[:, _i:_i + 1], _t[:])
            b.bar()
        xnt = sb([128, LC], F32, "xnt")
        xnbf = sb([128, KD * LC], BF16, "xnbf")
        t2b = sb([16, LC], BF16, "t2b")
        t4b = sb([16, LC], BF16, "t4b")
        h1b = sb([128, NFB * LC], BF16, "h1b")
        x1t = sb([128, LC], F32, "x1t")
        ot = sb([128, LC], F32, "ot")
        ps_l = ps([16, LC], F32, "l")
        ps_m = [ps([128, LC], F32, f"m{i}") for i in range(2)]
        for c in range(NCH):
            sl = slice(c * LC, (c + 1) * LC)
            b.bar()
            for kb in range(KD):
                b.ld(xnt[:], xn0[kb * 128:(kb + 1) * 128, sl])
                b.wait("vector", "ld")
                b.v("tensor_copy", xnbf[:, kb * LC:(kb + 1) * LC], xnt[:])
                b.wait("sync", "dve")
            b.wait("tensor", "dve")
            for kb in range(KD):
                b.pe("matmul", ps_l[:, :], AfcTb[:, kb * 16:(kb + 1) * 16],
                     xnbf[:, kb * LC:(kb + 1) * LC], start=(kb == 0), stop=(kb == KD - 1))
            b.wait("vector", "pe")
            b.v("tensor_scalar", t2b[:], ps_l[:, :], rfcb[0:16, :], 2.0,
                op0=ALU.mult, op1=ALU.mult)
            for fb in range(NFB):
                pst = ps_m[fb % 2]
                b.wait("tensor", "act", "dve")
                for kb in range(KD):
                    b.pe("matmul", pst[:, :], rfc[:, kb * FF + fb * 128:kb * FF + (fb + 1) * 128],
                         xnbf[:, kb * LC:(kb + 1) * LC], start=(kb == 0), stop=False)
                b.pe("matmul", pst[:, :], BfcTb[:, fb * 128:(fb + 1) * 128], t2b[:],
                     start=False, stop=True)
                b.wait("scalar", "pe")
                b.s("activation", out=h1b[:, fb * LC:(fb + 1) * LC], in_=pst[:, :],
                    func=AF.Gelu, bias=bfct[:, fb:fb + 1], scale=sfcb[:])
            b.wait("tensor", "act")
            if DEBUG_FFN:
                b.bar()
                for fb in range(NFB):
                    b.st(h1_dbg[fb * 128:(fb + 1) * 128, sl], h1b[:, fb * LC:(fb + 1) * LC])
                b.bar()
            for fb in range(NFB):
                b.pe("matmul", ps_l[:, :], AprTb[:, fb * 16:(fb + 1) * 16],
                     h1b[:, fb * LC:(fb + 1) * LC], start=(fb == 0), stop=(fb == NFB - 1))
            b.wait("vector", "pe", "st")
            b.v("tensor_scalar", t4b[:], ps_l[:, :], rprb[0:16, :], 2.0,
                op0=ALU.mult, op1=ALU.mult)
            for ob in range(KD):
                pst = ps_m[ob % 2]
                b.wait("tensor", "act", "dve")
                for fb in range(NFB):
                    b.pe("matmul", pst[:, :], rpr[:, fb * D + ob * 128:fb * D + (ob + 1) * 128],
                         h1b[:, fb * LC:(fb + 1) * LC], start=(fb == 0), stop=False)
                b.pe("matmul", pst[:, :], BprTb[:, ob * 128:(ob + 1) * 128], t4b[:],
                     start=False, stop=True)
                b.wait("scalar", "pe", "st")
                b.s("activation", out=ot[:], in_=pst[:, :], func=AF.Identity,
                    bias=bprt[:, ob:ob + 1], scale=sprb[:])
                b.wait("sync", "dve")
                b.ld(x1t[:], x1p[ob * 128:(ob + 1) * 128, sl])
                b.wait("vector", "ld", "act", "st")
                b.v("tensor_tensor", ot[:], ot[:], x1t[:], ALU.add)
                b.bar(("gpsimd",))
                b.st(x1[ob * 128:(ob + 1) * 128, sl], ot[:])
        b.bar()


def emit_attkv(b):
    nc = b.nc
    xn1 = b.stream("xn1")
    kfm = b.stream("kfm")
    vtm = b.stream("vtm")
    qfm = b.stream("qfm")
    WqT = b.ext_in("WqT", (D, D), F32)
    WkT = b.ext_in("WkT", (D, D), F32)
    WvT = b.ext_in("WvT", (D, D), F32)
    bq = b.ext_in("bq", (128, KD), F32)
    bk = b.ext_in("bk", (128, KD), F32)

    with ExitStack() as stk:
        sb = lambda shape, dt, nm: stk.enter_context(nc.sbuf_tensor(f"KV_{nm}", list(shape), dt))
        ps = lambda shape, dt, nm: stk.enter_context(nc.psum_tensor(f"KVp_{nm}", list(shape), dt))
        xnbf = sb([128, KD * LC], BF16, "xnbf")
        xnt = sb([128, LC], F32, "xnt")
        bqt = sb([128, KD], F32, "bqt")
        bkt = sb([128, KD], F32, "bkt")
        ob_ = sb([128, LC], BF16, "ob")
        ps_m = [ps([128, LC], F32, f"m{i}") for i in range(2)]
        b.ld(bqt[:], bq)
        b.ld(bkt[:], bk)
        wqb = emit_w_to_bf16(b, "KVq", stk, WqT, KD, D)
        wkb = emit_w_to_bf16(b, "KVk", stk, WkT, KD, D)
        wvb = emit_w_to_bf16(b, "KVv", stk, WvT, KD, D)
        for c in range(NCH):
            sl = slice(c * LC, (c + 1) * LC)
            b.bar()
            for kb in range(KD):
                b.ld(xnt[:], xn1[kb * 128:(kb + 1) * 128, sl])
                b.wait("vector", "ld")
                b.v("tensor_copy", xnbf[:, kb * LC:(kb + 1) * LC], xnt[:])
                b.wait("sync", "dve")
            b.wait("tensor", "dve")
            for wb, bcol, dstm in ((wqb, bqt, qfm), (wkb, bkt, kfm)):
                for eb in range(KD):
                    pst = ps_m[eb % 2]
                    b.wait("tensor", "act", "st")
                    for kb in range(KD):
                        b.pe("matmul", pst[:, :], wb[:, kb * D + eb * 128:kb * D + (eb + 1) * 128],
                             xnbf[:, kb * LC:(kb + 1) * LC], start=(kb == 0), stop=(kb == KD - 1))
                    b.wait("scalar", "pe")
                    b.s("activation", out=ob_[:], in_=pst[:, :], func=AF.Identity,
                        bias=bcol[:, eb:eb + 1])
                    b.bar(("gpsimd",))
                    b.st(dstm[eb * 128:(eb + 1) * 128, sl], ob_[:])
                    b.wait("scalar", "st")
            for lb in range(LC // 128):
                labs = c * LC + lb * 128
                for ec in range(2):
                    pst = ps_m[ec % 2]
                    b.wait("tensor", "act", "st")
                    for kb in range(KD):
                        b.pe("matmul", pst[:, :],
                             xnbf[:, kb * LC + lb * 128:kb * LC + (lb + 1) * 128],
                             wvb[:, kb * D + ec * LC:kb * D + (ec + 1) * LC],
                             start=(kb == 0), stop=(kb == KD - 1))
                    b.wait("scalar", "pe")
                    b.s("copy", ob_[:], pst[:, :])
                    b.bar(("gpsimd",))
                    b.st(vtm[labs:labs + 128, ec * LC:(ec + 1) * LC], ob_[:])
                    b.wait("scalar", "st")
        b.bar()


def emit_attq(b):
    nc = b.nc
    kfm = b.stream("kfm")
    vtm = b.stream("vtm")
    qfm = b.stream("qfm")
    att = b.stream("att")
    bv = b.ext_in("bv", (128, KD), F32)
    amask = b.ext_in("amask", (4, 128 * LC), F32)
    identb = b.ext_in("identb", (128, 128), BF16)
    NLB = L // 128
    if DEBUG_ATT:
        p_dbg = b.ext_out("dbg_p", (L, L), BF16)
        rs_dbg = b.ext_out("dbg_rs", (L, 2), F32)
        sc_dbg2 = b.ext_out("dbg_sc2", (256, L), F32)

    with ExitStack() as stk:
        sb = lambda shape, dt, nm: stk.enter_context(nc.sbuf_tensor(f"AQ_{nm}", list(shape), dt))
        ps = lambda shape, dt, nm: stk.enter_context(nc.psum_tensor(f"AQp_{nm}", list(shape), dt))
        kf = sb([128, KD * L], BF16, "kf")
        vt = sb([128, NLB * D], BF16, "vt")
        qt = sb([128, KD * 128], BF16, "qt")
        sc = sb([128, L], F32, "sc")
        pb_ = sb([128, L], BF16, "pb")
        pT = sb([128, NLB * 128], BF16, "pT")
        mask = sb([128, 4 * LC], F32, "mask")
        idt = sb([128, 128], BF16, "idt")
        bvt = sb([128, KD], F32, "bvt")
        negm = sb([128, 1], F32, "negm")
        negm2 = sb([128, 1], F32, "negm2")
        rsum = sb([128, 1], F32, "rsum")
        rsum2 = sb([128, 1], F32, "rsum2")
        ot = sb([128, 128], F32, "ot")
        ps_s = [ps([128, LC], F32, f"s{i}") for i in range(2)]
        ps_t = ps([128, 128], BF16, "t")
        ps_av = [ps([128, 128], F32, f"av{i}") for i in range(2)]
        for kb in range(KD):
            b.ld(kf[:, kb * L:(kb + 1) * L], kfm[kb * 128:(kb + 1) * 128, :])
        for lb in range(NLB):
            b.ld(vt[:, lb * D:(lb + 1) * D], vtm[lb * 128:(lb + 1) * 128, :])
        am = amask.rearrange("a (p f) -> (a p) f", p=128)
        for w in range(4):
            b.ld(mask[:, w * LC:(w + 1) * LC], am[w * 128:(w + 1) * 128, :])
        b.ld(idt[:], identb)
        b.ld(bvt[:], bv)
        b.bar()
        for i in range(NLB):
            nchunks = i // 4 + 1
            kcols = nchunks * LC
            w = i % 4
            b.bar()
            for kb in range(KD):
                b.ld(qt[:, kb * 128:(kb + 1) * 128], qfm[kb * 128:(kb + 1) * 128, i * 128:(i + 1) * 128])
            b.wait("tensor", "ld")
            for j in range(nchunks):
                pst = ps_s[j % 2]
                b.wait("tensor", "act")
                for kb in range(KD):
                    b.pe("matmul", pst[:, :], qt[:, kb * 128:(kb + 1) * 128],
                         kf[:, kb * L + j * LC:kb * L + (j + 1) * LC],
                         start=(kb == 0), stop=(kb == KD - 1))
                b.wait("scalar", "pe", "dve")
                b.s("copy", sc[:, j * LC:(j + 1) * LC], pst[:, :])
            b.wait("vector", "act")
            b.v("tensor_tensor", sc[:, (nchunks - 1) * LC:kcols],
                sc[:, (nchunks - 1) * LC:kcols], mask[:, w * LC:(w + 1) * LC], ALU.add)
            b.v("reduce_max", negm[:], sc[:, 0:kcols], axis=AX.X, negate=True)
            b.v("tensor_scalar_mul", negm2[:], negm[:], 1.0 / 32.0)
            b.wait("scalar", "dve")
            b.s("activation", out=pb_[:, 0:kcols], in_=sc[:, 0:kcols], func=AF.Exp,
                scale=1.0 / 32.0, bias=negm2[:], accum_out=rsum[:])
            b.wait("vector", "act")
            if DEBUG_ATT:
                b.bar()
                with b.nc.allow_non_contiguous_dma(reason="debug tap"):
                    b.st(rs_dbg[i * 128:(i + 1) * 128, 0:1], rsum[:])
                    b.st(rs_dbg[i * 128:(i + 1) * 128, 1:2], negm[:])
                if i < 2:
                    b.st(sc_dbg2[i * 128:(i + 1) * 128, 0:kcols], sc[:, 0:kcols])
                b.bar()
            b.v("reciprocal", rsum2[:], rsum[:])
            b.v("tensor_scalar", pb_[:, 0:kcols], pb_[:, 0:kcols], rsum2[:], None,
                op0=ALU.mult)
            if DEBUG_ATT:
                b.bar()
                b.st(p_dbg[i * 128:(i + 1) * 128, 0:kcols], pb_[:, 0:kcols])
                b.bar()
            for kb2 in range(4 * nchunks):
                b.wait("tensor", "dve", "act")
                b.pe("transpose", ps_t[:, :], pb_[:, kb2 * 128:(kb2 + 1) * 128], idt[:])
                b.wait("scalar", "pe")
                b.s("copy", pT[:, kb2 * 128:(kb2 + 1) * 128], ps_t[:, :])
            for db in range(KD):
                pst = ps_av[db % 2]
                b.wait("tensor", "act", "st")
                for kb2 in range(4 * nchunks):
                    b.pe("matmul", pst[:, :], vt[:, kb2 * D + db * 128:kb2 * D + (db + 1) * 128],
                         pT[:, kb2 * 128:(kb2 + 1) * 128],
                         start=(kb2 == 0), stop=(kb2 == 4 * nchunks - 1))
                b.wait("scalar", "pe")
                b.s("activation", out=ot[:], in_=pst[:, :], func=AF.Identity,
                    bias=bvt[:, db:db + 1])
                b.bar(("gpsimd",))
                b.st(att[db * 128:(db + 1) * 128, i * 128:(i + 1) * 128], ot[:])
                b.wait("scalar", "st")
        b.bar()


def emit_rwkv(b):
    nc = b.nc
    xn1 = b.stream("xn1")
    x1 = b.stream("x1")
    att = b.stream("att")
    out = b.ext_out("out", (D, L), F32)
    WrT = b.ext_in("WrT", (D, D), F32)
    Wk2T = b.ext_in("Wk2T", (D, D), F32)
    Wv2T = b.ext_in("Wv2T", (D, D), F32)
    mur = b.ext_in("mur", (128, KD), F32)
    muk = b.ext_in("muk", (128, KD), F32)

    with ExitStack() as stk:
        sb = lambda shape, dt, nm: stk.enter_context(nc.sbuf_tensor(f"RW_{nm}", list(shape), dt))
        ps = lambda shape, dt, nm: stk.enter_context(nc.psum_tensor(f"RWp_{nm}", list(shape), dt))
        wrb = emit_w_to_bf16(b, "RWr", stk, WrT, KD, D)
        wkb = emit_w_to_bf16(b, "RWk", stk, Wk2T, KD, D)
        wvb = emit_w_to_bf16(b, "RWv", stk, Wv2T, KD, D)
        murt = sb([128, KD], F32, "murt")
        mukt = sb([128, KD], F32, "mukt")
        b.ld(murt[:], mur)
        b.ld(mukt[:], muk)
        xnt = sb([128, LC], F32, "xnt")
        xpt = sb([128, LC], F32, "xpt")
        dt_ = sb([128, LC], F32, "d")
        xrb = sb([128, KD * LC], BF16, "xrb")
        xkb = sb([128, KD * LC], BF16, "xkb")
        rsig = sb([128, KD * LC], F32, "rsig")
        katb = sb([128, KD * LC], BF16, "katb")
        t1 = sb([128, LC], F32, "t1")
        x1t = sb([128, LC], F32, "x1t")
        att_t = sb([128, LC], F32, "attt")
        ot = sb([128, LC], F32, "ot")
        ps_m = [ps([128, LC], F32, f"m{i}") for i in range(2)]
        b.bar()
        for c in range(NCH):
            sl = slice(c * LC, (c + 1) * LC)
            b.bar()
            for kb in range(KD):
                b.ld(xnt[:], xn1[kb * 128:(kb + 1) * 128, sl])
                if c == 0:
                    b.v("memset", xpt[:, 0:1], 0.0)
                    b.ld(xpt[:, 1:LC], xn1[kb * 128:(kb + 1) * 128, 0:LC - 1])
                else:
                    b.ld(xpt[:], xn1[kb * 128:(kb + 1) * 128, c * LC - 1:(c + 1) * LC - 1])
                b.wait("vector", "ld")
                b.v("tensor_tensor", dt_[:], xnt[:], xpt[:], ALU.subtract)
                b.v("scalar_tensor_tensor", out=xrb[:, kb * LC:(kb + 1) * LC], in0=dt_[:],
                    scalar=murt[:, kb:kb + 1], in1=xpt[:], op0=ALU.mult, op1=ALU.add)
                b.v("scalar_tensor_tensor", out=xkb[:, kb * LC:(kb + 1) * LC], in0=dt_[:],
                    scalar=mukt[:, kb:kb + 1], in1=xpt[:], op0=ALU.mult, op1=ALU.add)
                b.wait("sync", "dve")
            b.wait("tensor", "dve")
            for eb in range(KD):
                pst = ps_m[eb % 2]
                b.wait("tensor", "act")
                for kb in range(KD):
                    b.pe("matmul", pst[:, :], wrb[:, kb * D + eb * 128:kb * D + (eb + 1) * 128],
                         xrb[:, kb * LC:(kb + 1) * LC], start=(kb == 0), stop=(kb == KD - 1))
                b.wait("scalar", "pe")
                b.s("activation", out=rsig[:, eb * LC:(eb + 1) * LC], in_=pst[:, :],
                    func=AF.Sigmoid)
            for eb in range(KD):
                pst = ps_m[eb % 2]
                b.wait("tensor", "act")
                for kb in range(KD):
                    b.pe("matmul", pst[:, :], wkb[:, kb * D + eb * 128:kb * D + (eb + 1) * 128],
                         xkb[:, kb * LC:(kb + 1) * LC], start=(kb == 0), stop=(kb == KD - 1))
                b.wait("scalar", "pe")
                b.s("activation", out=t1[:], in_=pst[:, :], func=AF.Relu)
                b.s("activation", out=katb[:, eb * LC:(eb + 1) * LC], in_=t1[:],
                    func=AF.Square)
            b.wait("tensor", "act")
            for eb in range(KD):
                pst = ps_m[eb % 2]
                b.wait("tensor", "dve")
                for kb in range(KD):
                    b.pe("matmul", pst[:, :], wvb[:, kb * D + eb * 128:kb * D + (eb + 1) * 128],
                         katb[:, kb * LC:(kb + 1) * LC], start=(kb == 0), stop=(kb == KD - 1))
                b.wait("sync", "dve")
                b.ld(x1t[:], x1[eb * 128:(eb + 1) * 128, sl])
                b.ld(att_t[:], att[eb * 128:(eb + 1) * 128, sl])
                b.wait("vector", "pe", "ld", "st", "act")
                b.v("tensor_tensor", ot[:], pst[:, :], rsig[:, eb * LC:(eb + 1) * LC], ALU.mult)
                b.v("tensor_tensor", ot[:], ot[:], x1t[:], ALU.add)
                b.v("tensor_tensor", ot[:], ot[:], att_t[:], ALU.add)
                b.bar(("gpsimd",))
                b.st(out[eb * 128:(eb + 1) * 128, sl], ot[:])
        b.bar()


PHASE_FN = {
    "ln0": emit_ln0,
    "m0": lambda b: emit_mamba_layer(b, 0),
    "m1": lambda b: emit_mamba_layer(b, 1),
    "m2": lambda b: emit_mamba_layer(b, 2),
    "m3": lambda b: emit_mamba_layer(b, 3),
    "mfin": emit_mfin,
    "ffn": emit_ffn,
    "ln1": emit_ln1,
    "attkv": emit_attkv,
    "attq": emit_attq,
    "rwkv": emit_rwkv,
}


def build_launch(phases):
    b = Build(phases)
    b.ext_in("C_ones128", (128, 1), F32)
    b.ext_in("C_ones1x128", (1, 128), F32)
    for p in phases:
        b.begin_phase(p)
        PHASE_FN[p](b)
    b.finish()
    return b



def _prep_shared(I):
    asc = np.ascontiguousarray
    d = {}
    d["C_ones128"] = np.ones((128, 1), np.float32)
    d["C_ones1x128"] = np.ones((1, 128), np.float32)
    sel = np.zeros((32, 32 * 128), np.float32)
    for q in range(32):
        sel[q, q * 128:(q + 1) * 128] = 1.0
    d["selBC"] = sel.astype(ml_dtypes.bfloat16)
    d["identb"] = np.eye(128, dtype=np.float32).astype(ml_dtypes.bfloat16)
    d["identf"] = np.eye(128, dtype=np.float32)
    for k in range(NLAY):
        d[f"ipW{k}"] = asc(I["L0_attn_in_proj_w"][k].T)
        d[f"xpW{k}"] = asc(I["L0_attn_x_proj_w"][k].T)
        d[f"dpW{k}"] = asc(I["L0_attn_dt_proj_w"][k].T)
        d[f"opW{k}"] = asc(I["L0_attn_out_proj_w"][k].T)
        d[f"cw{k}"] = asc(I["L0_attn_conv_w"][k])
        d[f"cb{k}"] = asc(I["L0_attn_conv_b"][k].reshape(NDB, 128).T)
        d[f"nw{k}"] = asc(I["L0_attn_norm_w"][k].reshape(KD, 128).T)
        d[f"nb{k}"] = asc(I["L0_attn_norm_b"][k].reshape(KD, 128).T)
        d[f"dtb{k}"] = asc(I["L0_attn_dt_proj_b"][k].reshape(NDB, 128).T)
        d[f"alog{k}"] = asc(I["L0_attn_A_log"][k])
        d[f"dskip{k}"] = asc(I["L0_attn_D_skip"][k].reshape(NDB, 128).T)
    d["normf_w"] = asc(I["L0_attn_normf_w"].reshape(KD, 128).T)
    d["normf_b"] = asc(I["L0_attn_normf_b"].reshape(KD, 128).T)
    d["WfcT"] = asc(I["L0_ffn_W_fc"].T)
    d["WprT"] = asc(I["L0_ffn_W_proj"].T)
    d["AfcT"] = asc(I["L0_ffn_A_fc"].T)
    d["BfcT"] = asc(I["L0_ffn_B_fc"].T)
    d["AprT"] = asc(I["L0_ffn_A_proj"].T)
    d["BprT"] = asc(I["L0_ffn_B_proj"].T)
    d["bfc"] = asc(I["L0_ffn_b_fc"].reshape(NFB, 128).T)
    d["bpr"] = asc(I["L0_ffn_b_proj"].reshape(KD, 128).T)
    d["WqT"] = asc(I["L1_attn_Wq"].T)
    d["WkT"] = asc(I["L1_attn_Wk"].T)
    d["WvT"] = asc(I["L1_attn_Wv"].T)
    d["bq"] = asc(I["L1_attn_bq"].reshape(KD, 128).T)
    d["bk"] = asc(I["L1_attn_bk"].reshape(KD, 128).T)
    d["bv"] = asc(I["L1_attn_bv"].reshape(KD, 128).T)
    d["WrT"] = asc(I["L1_ffn_W_r"].T)
    d["Wk2T"] = asc(I["L1_ffn_W_k"].T)
    d["Wv2T"] = asc(I["L1_ffn_W_v"].T)
    d["mur"] = asc(I["L1_ffn_mu_r"].reshape(KD, 128).T)
    d["muk"] = asc(I["L1_ffn_mu_k"].reshape(KD, 128).T)
    w = np.arange(4)[:, None, None]
    r = np.arange(128)[None, :, None]
    cc = np.arange(LC)[None, None, :]
    d["amask"] = np.ascontiguousarray(
        np.where(cc <= 128 * w + r, 0.0, -1e30).astype(np.float32).reshape(4, 128 * LC))
    return d


FUSED = True
LAUNCH_PLAN_UNFUSED = [["ln0"], ["m0"], ["m1"], ["m2"], ["m3"], ["mfin"],
                       ["ffn"], ["ln1"], ["attkv"], ["attq"], ["rwkv"]]
LAUNCH_PLAN = [list(ALL_PHASES)] if FUSED else LAUNCH_PLAN_UNFUSED


def kernel(**inputs):
    shared = _prep_shared(inputs)
    x_all = inputs["L0_attn_hidden_states"]
    ncores = 8
    xs = [np.ascontiguousarray(x_all[i].T) for i in range(ncores)]
    state = [dict() for _ in range(ncores)]

    for phases in LAUNCH_PLAN:
        b = build_launch(phases)
        in_maps = []
        for i in range(ncores):
            m = {}
            for name in b.inputs:
                if name.startswith("S_"):
                    m[name] = state[i][name]
                elif name == "x":
                    m[name] = xs[i]
                else:
                    m[name] = shared[name]
            in_maps.append(m)
        res = run_bass_kernel_spmd(b.nc, in_maps, core_ids=list(range(ncores)))
        for i in range(ncores):
            for name in b.outputs:
                state[i][name] = np.asarray(res.results[i][name])

    outs = np.stack([state[i]["out"].astype(np.float32).T for i in range(ncores)], axis=0)
    return np.ascontiguousarray(outs)


# version 03
# speedup vs baseline: 1.3255x; 1.3255x over previous
import numpy as np
import ml_dtypes
from contextlib import ExitStack

import concourse.bass as bass
import concourse.mybir as mybir
from concourse.bass_utils import run_bass_kernel_spmd

F32 = mybir.dt.float32
BF16 = mybir.dt.bfloat16
AF = mybir.ActivationFunctionType
ALU = mybir.AluOpType
AX = mybir.AxisListType

D = 1024
L = 4096
DI = 2048
DS = 16
DTR = 64
NLAY = 4
FF = 4096
NFB = FF // 128
LC = 512
NCH = L // LC
NDB = DI // 128
KD = D // 128
EPS = 1e-5
MAGIC = float(np.float32(1.5 * 2 ** 23))

ALL_PHASES = ["ln0", "m0", "m1", "m2", "m3", "mfin", "ffn", "ln1", "attkv", "attq", "rwkv"]

STREAMS = {
    "xn0":  ((D, L), F32, "ln0", ["m0", "ffn"]),
    "R1":   ((D, L), F32, "m0", ["m1"]),
    "R2":   ((D, L), F32, "m1", ["m2"]),
    "R3":   ((D, L), F32, "m2", ["m3"]),
    "R4":   ((D, L), F32, "m3", ["mfin"]),
    "x1p":  ((D, L), F32, "mfin", ["ffn"]),
    "x1":   ((D, L), F32, "ffn", ["ln1", "rwkv"]),
    "xn1":  ((D, L), F32, "ln1", ["attkv", "attq", "rwkv"]),
    "kfm":  ((D, L), BF16, "attkv", ["attq"]),
    "vtm":  ((L, D), BF16, "attkv", ["attq"]),
    "qfm":  ((D, L), BF16, "attkv", ["attq"]),
    "att":  ((D, L), F32, "attq", ["rwkv"]),
}
DEBUG_SCRATCH = False
DEBUG_PHASE_A = False
DEBUG_FFN = False
DEBUG_ATT = False

MSCRATCH = {
    "xc_s": ((DI, L), BF16),
    "z_s":  ((DI, L), BF16),
    "dt_s": ((DI, L), BF16),
    "bc_s": ((32, L), BF16),
    "ym_s": ((DI, L), BF16),
}


class Build:
    def __init__(self, phases):
        self.phases = phases
        self.nc = bass.Bass("TRN2", target_bir_lowering=False)
        self.ex = ExitStack()
        self.sems = {}
        self.cnt = {}
        self._phase_idx = 0
        self._new_sems("g")
        self.inputs = {}
        self.outputs = {}
        self.dram = {}
        self.uid = 0

    def _new_sems(self, tag):
        for s in ["ld", "st", "pe", "act", "dve", "gp"]:
            self.sems[s] = self.ex.enter_context(self.nc.semaphore(f"{s}_{tag}"))
            self.cnt[s] = 0

    def begin_phase(self, name):
        if self._phase_idx > 0:
            self.bar()
        self._phase_idx += 1
        self._new_sems(f"p{self._phase_idx}_{name}")

    def ext_in(self, name, shape, dtype):
        if name not in self.dram:
            t = self.nc.dram_tensor(name, list(shape), dtype, kind="ExternalInput")
            self.inputs[name] = t
            self.dram[name] = t.ap()
        return self.dram[name]

    def ext_out(self, name, shape, dtype):
        if name not in self.dram:
            t = self.nc.dram_tensor(name, list(shape), dtype, kind="ExternalOutput")
            self.outputs[name] = t
            self.dram[name] = t.ap()
        return self.dram[name]

    def internal(self, name, shape, dtype):
        if name not in self.dram:
            t = self.nc.dram_tensor(name, list(shape), dtype, kind="Internal")
            self.dram[name] = t.ap()
        return self.dram[name]

    def stream(self, name):
        shape, dtype, prod, cons = STREAMS[name]
        key = "S_" + name
        if key in self.dram:
            return self.dram[key]
        if prod not in self.phases:
            return self.ext_in(key, shape, dtype)
        if any(c not in self.phases for c in cons):
            return self.ext_out(key, shape, dtype)
        return self.internal(key, shape, dtype)

    def mscratch(self, name):
        shape, dtype = MSCRATCH[name]
        if DEBUG_SCRATCH:
            return self.ext_out("M_" + name, shape, dtype)
        return self.internal("M_" + name, shape, dtype)

    def _bump(self, instr, sem, amt=1):
        instr.then_inc(self.sems[sem], amt)
        self.cnt[sem] += amt

    def ld(self, out_ap, in_ap):
        self._bump(self.nc.sync.dma_start(out=out_ap, in_=in_ap), "ld", 16)

    def st(self, out_ap, in_ap):
        self._bump(self.nc.gpsimd.dma_start(out=out_ap, in_=in_ap), "st", 16)

    def pe(self, meth, *a, **k):
        self._bump(getattr(self.nc.tensor, meth)(*a, **k), "pe")

    @staticmethod
    def _outsz(a, k):
        ap = k.get("out", a[0] if a else None)
        try:
            sz = 1
            for d in ap.shape:
                sz *= d
            return sz
        except Exception:
            return 1 << 30

    def v(self, meth, *a, **k):
        self._bump(getattr(self.nc.vector, meth)(*a, **k), "dve")
        if self._outsz(a, k) <= 8192:
            self.nc.vector.wait_ge(self.sems["dve"], self.cnt["dve"])

    def s(self, meth, *a, **k):
        self._bump(getattr(self.nc.scalar, meth)(*a, **k), "act")
        if self._outsz(a, k) <= 8192:
            self.nc.scalar.wait_ge(self.sems["act"], self.cnt["act"])

    def g(self, meth, *a, **k):
        self._bump(getattr(self.nc.gpsimd, meth)(*a, **k), "gp")
        if self._outsz(a, k) <= 8192:
            self.nc.gpsimd.wait_ge(self.sems["gp"], self.cnt["gp"])

    def wait(self, eng_name, *sems):
        eng = getattr(self.nc, eng_name)
        for s in sems:
            if self.cnt[s] > 0:
                eng.wait_ge(self.sems[s], self.cnt[s])

    def bar(self, engines=("tensor", "vector", "scalar", "gpsimd", "sync")):
        for e in engines:
            self.wait(e, "ld", "st", "pe", "act", "dve", "gp")

    def finish(self):
        self.nc.sync.wait_ge(self.sems["st"], self.cnt["st"])
        self.ex.close()



def emit_ln_stream(b, tag, src_ap, dst_ap, w_col=None, b_col=None, add_ap=None):
    nc = b.nc
    with ExitStack() as stk:
        sb = lambda shape, dt, nm: stk.enter_context(nc.sbuf_tensor(f"{tag}_{nm}", list(shape), dt))
        ps = lambda shape, dt, nm: stk.enter_context(nc.psum_tensor(f"{tag}p_{nm}", list(shape), dt))
        xt = [sb([128, LC], F32, f"x{i}") for i in range(KD)]
        sq = sb([128, LC], F32, "sq")
        ot = sb([128, LC], F32, "ot")
        addt = sb([128, LC], F32, "add") if add_ap is not None else None
        mu = sb([1, LC], F32, "mu")
        ex2 = sb([1, LC], F32, "ex2")
        rstd = sb([1, LC], F32, "rstd")
        onesc = sb([128, 1], F32, "onesc")
        onesr = sb([1, 128], F32, "onesr")
        epsT = sb([1, 1], F32, "eps")
        ps_sum = ps([1, LC], F32, "sum")
        ps_sq = ps([1, LC], F32, "sqs")
        ps_mu = ps([128, LC], F32, "mub")
        ps_rs = ps([128, LC], F32, "rsb")
        b.ld(onesc[:], b.dram["C_ones128"])
        b.ld(onesr[:], b.dram["C_ones1x128"])
        b.v("memset", epsT[:], EPS)
        b.bar()
        for c in range(NCH):
            sl = slice(c * LC, (c + 1) * LC)
            b.bar()
            for kb in range(KD):
                b.ld(xt[kb][:], src_ap[kb * 128:(kb + 1) * 128, sl])
            b.wait("tensor", "ld")
            b.wait("scalar", "ld")
            for kb in range(KD):
                b.pe("matmul", ps_sum[:, :], onesc[:], xt[kb][:],
                     start=(kb == 0), stop=(kb == KD - 1))
            for kb in range(KD):
                if kb > 0:
                    b.wait("scalar", "pe")
                b.s("activation", out=sq[:], in_=xt[kb][:], func=AF.Square)
                b.wait("tensor", "act")
                b.pe("matmul", ps_sq[:, :], onesc[:], sq[:],
                     start=(kb == 0), stop=(kb == KD - 1))
            b.wait("vector", "pe")
            b.v("tensor_scalar_mul", mu[:, :], ps_sum[:, :], 1.0 / D)
            b.v("tensor_scalar_mul", ex2[:, :], ps_sq[:, :], 1.0 / D)
            b.v("tensor_tensor", rstd[:, :], mu[:, :], mu[:, :], ALU.mult)
            b.v("tensor_tensor", rstd[:, :], ex2[:, :], rstd[:, :], ALU.subtract)
            b.wait("scalar", "dve")
            b.s("activation", out=rstd[:], in_=rstd[:], func=AF.Sqrt, bias=epsT[0:1, 0:1])
            b.wait("vector", "act")
            b.v("reciprocal", rstd[:], rstd[:])
            b.wait("tensor", "dve")
            b.pe("matmul", ps_mu[:, :], onesr[:], mu[:, :], start=True, stop=True)
            b.pe("matmul", ps_rs[:, :], onesr[:], rstd[:, :], start=True, stop=True)
            b.wait("vector", "pe")
            b.wait("sync", "dve", "act", "st")
            for kb in range(KD):
                if add_ap is not None:
                    b.ld(addt[:], add_ap[kb * 128:(kb + 1) * 128, sl])
                b.v("tensor_tensor", ot[:], xt[kb][:], ps_mu[:, :], ALU.subtract)
                b.v("tensor_tensor", ot[:], ot[:], ps_rs[:, :], ALU.mult)
                if w_col is not None:
                    b.wait("scalar", "dve")
                    b.s("activation", out=ot[:], in_=ot[:], func=AF.Identity,
                        bias=b_col[:, kb:kb + 1], scale=w_col[:, kb:kb + 1])
                if add_ap is not None:
                    b.wait("vector", "ld")
                    if w_col is not None:
                        b.wait("vector", "act")
                    b.v("tensor_tensor", ot[:], ot[:], addt[:], ALU.add)
                b.bar(("gpsimd",))
                b.st(dst_ap[kb * 128:(kb + 1) * 128, sl], ot[:])
                b.wait("vector", "st")
                b.wait("scalar", "st")
                b.wait("sync", "st")
        b.bar()


def emit_w_to_bf16(b, tag, stk, src_ap, nk, width, layout_width=None):
    nc = b.nc
    lw = layout_width or width
    dst = stk.enter_context(nc.sbuf_tensor(f"{tag}_wb", [128, nk * lw], BF16))
    stg = stk.enter_context(nc.sbuf_tensor(f"{tag}_wstg", [128, width], F32))
    for kb in range(nk):
        b.wait("sync", "dve")
        b.ld(stg[:], src_ap[kb * 128:(kb + 1) * 128, :])
        b.wait("vector", "ld")
        b.v("tensor_copy", dst[:, kb * lw:kb * lw + width], stg[:])
    b.bar()
    return dst



def emit_ln0(b):
    x = b.ext_in("x", (D, L), F32)
    emit_ln_stream(b, "ln0", x, b.stream("xn0"))


def emit_ln1(b):
    emit_ln_stream(b, "ln1", b.stream("x1"), b.stream("xn1"))


def emit_mfin(b):
    x = b.ext_in("x", (D, L), F32)
    with ExitStack() as stk:
        nwc = stk.enter_context(b.nc.sbuf_tensor("mfin_nw", [128, KD], F32))
        nbc = stk.enter_context(b.nc.sbuf_tensor("mfin_nb", [128, KD], F32))
        b.ld(nwc[:], b.ext_in("normf_w", (128, KD), F32))
        b.ld(nbc[:], b.ext_in("normf_b", (128, KD), F32))
        b.bar()
        emit_ln_stream(b, "mfin", b.stream("R4"), b.stream("x1p"),
                       w_col=nwc, b_col=nbc, add_ap=x)


def emit_mamba_layer(b, k):
    nc = b.nc
    rin = b.stream("xn0") if k == 0 else b.stream(f"R{k}")
    rout = b.stream(f"R{k + 1}")
    xc_s = b.mscratch("xc_s")
    z_s = b.mscratch("z_s")
    dt_s = b.mscratch("dt_s")
    bc_s = b.mscratch("bc_s")
    ym_s = b.mscratch("ym_s")

    ipW = b.ext_in(f"ipW{k}", (D, 2 * DI), F32)
    xpW = b.ext_in(f"xpW{k}", (DI, 96), F32)
    dpW = b.ext_in(f"dpW{k}", (DTR, DI), F32)
    opW = b.ext_in(f"opW{k}", (DI, D), F32)
    cw = b.ext_in(f"cw{k}", (DI, 4), F32)
    cb = b.ext_in(f"cb{k}", (128, NDB), F32)
    nw = b.ext_in(f"nw{k}", (128, KD), F32)
    nb = b.ext_in(f"nb{k}", (128, KD), F32)
    dtb = b.ext_in(f"dtb{k}", (128, NDB), F32)
    alog = b.ext_in(f"alog{k}", (DI, DS), F32)
    dskip = b.ext_in(f"dskip{k}", (128, NDB), F32)
    selBC = b.ext_in("selBC", (32, 32 * 128), BF16)
    if DEBUG_PHASE_A:
        xnd_dbg = b.ext_out(f"dbg_xn{k}", (D, L), BF16)
        xcpre_dbg = b.ext_out(f"dbg_xcpre{k}", (DI, L), F32)

    with ExitStack() as stk:
        sb = lambda shape, dt, nm: stk.enter_context(nc.sbuf_tensor(f"A{k}_{nm}", list(shape), dt))
        ps = lambda shape, dt, nm: stk.enter_context(nc.psum_tensor(f"A{k}p_{nm}", list(shape), dt))
        ipWb = emit_w_to_bf16(b, f"A{k}ip", stk, ipW, KD, 2 * DI)
        xpWb = emit_w_to_bf16(b, f"A{k}xp", stk, xpW, NDB, 96)
        dpWb = sb([64, DI], BF16, "dpWb")
        wstg = sb([64, DI], F32, "dpstg")
        b.ld(wstg[:], dpW)
        b.wait("vector", "ld")
        b.v("tensor_copy", dpWb[:], wstg[:])
        cwt = sb([128, NDB * 4], F32, "cwt")
        cbt = sb([128, NDB], F32, "cbt")
        nwt = sb([128, KD], F32, "nwt")
        nbt = sb([128, KD], F32, "nbt")
        dtbt = sb([128, NDB], F32, "dtbt")
        onesc = sb([128, 1], F32, "onesc")
        onesr = sb([1, 128], F32, "onesr")
        epsT = sb([1, 1], F32, "eps")
        b.v("memset", epsT[:], EPS)
        b.ld(onesc[:], b.dram["C_ones128"])
        b.ld(onesr[:], b.dram["C_ones1x128"])
        for db in range(NDB):
            b.ld(cwt[:, db * 4:(db + 1) * 4], cw[db * 128:(db + 1) * 128, :])
        b.ld(cbt[:], cb)
        b.ld(nwt[:], nw)
        b.ld(nbt[:], nb)
        b.ld(dtbt[:], dtb)

        rt = [sb([128, LC], F32, f"rt{i}") for i in range(KD)]
        xnb = sb([128, KD * LC], BF16, "xnb")
        sq = sb([128, LC], F32, "sq")
        mu = sb([1, LC], F32, "mu")
        ex2 = sb([1, LC], F32, "ex2")
        rstd = sb([1, LC], F32, "rstd")
        ext = [sb([128, LC + 3], F32, f"ext{i}") for i in range(2)]
        tail = [sb([128, 4], F32, f"tail{i}") for i in range(NDB)]
        xcf = sb([128, LC], F32, "xcf")
        xcb = sb([128, NDB * LC], BF16, "xcb")
        zt = sb([128, LC], BF16, "zt")
        xdbb = sb([96, LC], BF16, "xdbb")
        t1 = sb([128, LC], F32, "t1")
        t2 = sb([128, LC], F32, "t2")
        t2b_ = sb([128, LC], BF16, "t2b")
        ps_sum = ps([1, LC], F32, "sum")
        ps_sq = ps([1, LC], F32, "sqs")
        ps_mu = ps([128, LC], F32, "mub")
        ps_rs = ps([128, LC], F32, "rsb")
        ps_a = ps([128, LC], F32, "a0")
        ps_b2 = ps([128, LC], F32, "a1")
        ps_x = ps([96, LC], F32, "x96")

        for db in range(NDB):
            b.v("memset", tail[db][:], 0.0)
        b.bar()

        for c in range(NCH):
            sl = slice(c * LC, (c + 1) * LC)
            b.bar()
            for kb in range(KD):
                b.ld(rt[kb][:], rin[kb * 128:(kb + 1) * 128, sl])
            b.wait("tensor", "ld")
            b.wait("scalar", "ld")
            for kb in range(KD):
                b.pe("matmul", ps_sum[:, :], onesc[:], rt[kb][:],
                     start=(kb == 0), stop=(kb == KD - 1))
            for kb in range(KD):
                if kb > 0:
                    b.wait("scalar", "pe")
                b.s("activation", out=sq[:], in_=rt[kb][:], func=AF.Square)
                b.wait("tensor", "act")
                b.pe("matmul", ps_sq[:, :], onesc[:], sq[:],
                     start=(kb == 0), stop=(kb == KD - 1))
            b.wait("vector", "pe")
            b.v("tensor_scalar_mul", mu[:, :], ps_sum[:, :], 1.0 / D)
            b.v("tensor_scalar_mul", ex2[:, :], ps_sq[:, :], 1.0 / D)
            b.v("tensor_tensor", rstd[:, :], mu[:, :], mu[:, :], ALU.mult)
            b.v("tensor_tensor", rstd[:, :], ex2[:, :], rstd[:, :], ALU.subtract)
            b.wait("scalar", "dve")
            b.s("activation", out=rstd[:], in_=rstd[:], func=AF.Sqrt, bias=epsT[0:1, 0:1])
            b.wait("vector", "act")
            b.v("reciprocal", rstd[:], rstd[:])
            b.wait("tensor", "dve")
            b.pe("matmul", ps_mu[:, :], onesr[:], mu[:, :], start=True, stop=True)
            b.pe("matmul", ps_rs[:, :], onesr[:], rstd[:, :], start=True, stop=True)
            b.wait("vector", "pe")
            for kb in range(KD):
                b.wait("vector", "act")
                b.v("tensor_tensor", sq[:], rt[kb][:], ps_mu[:, :], ALU.subtract)
                b.v("tensor_tensor", sq[:], sq[:], ps_rs[:, :], ALU.mult)
                b.wait("scalar", "dve")
                b.s("activation", out=xnb[:, kb * LC:(kb + 1) * LC], in_=sq[:], func=AF.Identity,
                    bias=nbt[:, kb:kb + 1], scale=nwt[:, kb:kb + 1])
            b.bar()
            if DEBUG_PHASE_A:
                for kb in range(KD):
                    b.st(xnd_dbg[kb * 128:(kb + 1) * 128, sl], xnb[:, kb * LC:(kb + 1) * LC])
                b.bar()
            for db in range(NDB):
                pst = ps_a if db % 2 == 0 else ps_b2
                b.wait("tensor", "act", "dve")
                for kb in range(KD):
                    b.pe("matmul", pst[:, :],
                         ipWb[:, kb * 2 * DI + db * 128: kb * 2 * DI + (db + 1) * 128],
                         xnb[:, kb * LC:(kb + 1) * LC],
                         start=(kb == 0), stop=(kb == KD - 1))
                e = ext[db % 2]
                b.wait("scalar", "pe", "dve")
                b.s("copy", e[:, 3:3 + LC], pst[:, :])
                if DEBUG_PHASE_A:
                    b.bar()
                    b.st(xcpre_dbg[db * 128:(db + 1) * 128, sl], e[:, 3:3 + LC])
                    b.bar()
                b.wait("vector", "act", "st")
                b.v("tensor_copy", e[:, 0:3], tail[db][:, 0:3])
                b.v("tensor_scalar", xcf[:], e[:, 0:LC],
                    cwt[:, db * 4:db * 4 + 1], cbt[:, db:db + 1], op0=ALU.mult, op1=ALU.add)
                for tap in range(1, 4):
                    b.v("scalar_tensor_tensor", out=xcf[:], in0=e[:, tap:tap + LC],
                        scalar=cwt[:, db * 4 + tap:db * 4 + tap + 1], in1=xcf[:],
                        op0=ALU.mult, op1=ALU.add)
                b.v("tensor_copy", tail[db][:, 0:3], e[:, LC:LC + 3])
                b.wait("scalar", "dve")
                b.s("activation", out=xcf[:], in_=xcf[:], func=AF.Silu)
                b.wait("vector", "act")
                b.v("tensor_copy", xcb[:, db * LC:(db + 1) * LC], xcf[:])
                b.bar(("gpsimd",))
                b.st(xc_s[db * 128:(db + 1) * 128, sl], xcb[:, db * LC:(db + 1) * LC])
            for db in range(NDB):
                pst = ps_a if db % 2 == 0 else ps_b2
                b.wait("tensor", "act", "dve")
                for kb in range(KD):
                    b.pe("matmul", pst[:, :],
                         ipWb[:, kb * 2 * DI + DI + db * 128: kb * 2 * DI + DI + (db + 1) * 128],
                         xnb[:, kb * LC:(kb + 1) * LC],
                         start=(kb == 0), stop=(kb == KD - 1))
                b.wait("scalar", "pe", "st")
                b.s("copy", zt[:], pst[:, :])
                b.bar(("gpsimd",))
                b.st(z_s[db * 128:(db + 1) * 128, sl], zt[:])
            b.wait("tensor", "act", "dve")
            for db in range(NDB):
                b.pe("matmul", ps_x[:, :], xpWb[:, db * 96:(db + 1) * 96],
                     xcb[:, db * LC:(db + 1) * LC],
                     start=(db == 0), stop=(db == NDB - 1))
            b.wait("scalar", "pe", "st")
            b.s("copy", xdbb[:], ps_x[:, :])
            b.bar(("gpsimd",))
            b.st(bc_s[:, sl], xdbb[64:96, :])
            for db in range(NDB):
                pst = ps_a if db % 2 == 0 else ps_b2
                b.wait("tensor", "act", "dve", "st")
                b.pe("matmul", pst[:, :], dpWb[:, db * 128:(db + 1) * 128],
                     xdbb[0:64, :], start=True, stop=True)
                b.wait("scalar", "pe")
                b.s("activation", out=t1[:], in_=pst[:, :], func=AF.Abs,
                    bias=dtbt[:, db:db + 1])
                b.wait("vector", "act")
                b.v("tensor_scalar", t2[:], pst[:, :], dtbt[:, db:db + 1], 0.0,
                    op0=ALU.add, op1=ALU.max)
                b.s("activation", out=t1[:], in_=t1[:], func=AF.Exp, scale=-1.0)
                b.s("activation", out=t1[:], in_=t1[:], func=AF.Ln, bias=1.0)
                b.wait("vector", "act")
                b.v("tensor_tensor", t2b_[:], t2[:], t1[:], ALU.add)
                b.bar(("gpsimd",))
                b.st(dt_s[db * 128:(db + 1) * 128, sl], t2b_[:])
                b.wait("scalar", "st")
                b.wait("vector", "st")
        b.bar()

    with ExitStack() as stk:
        sb = lambda shape, dt, nm: stk.enter_context(nc.sbuf_tensor(f"B{k}_{nm}", list(shape), dt))
        ps = lambda shape, dt, nm: stk.enter_context(nc.psum_tensor(f"B{k}p_{nm}", list(shape), dt))
        dtb = [sb([128, L], BF16, f"dt{m}") for m in range(2)]
        ub = [sb([128, L], BF16, f"u{m}") for m in range(2)]
        ya = [sb([128, L], F32, f"ya{m}") for m in range(2)]
        dAt = [sb([128, L], F32, f"dA{m}") for m in range(2)]
        Gt = [sb([128, L], BF16, f"G{m}") for m in range(2)]
        ht = sb([128, L], BF16, "h")
        hct = sb([128, L], BF16, "hc")
        brep = [sb([128, L], BF16, f"br{p}") for p in range(2)]
        crep = [sb([128, L], BF16, f"cr{p}") for p in range(2)]
        xcz = sb([128, L], BF16, "xcz")
        zt = sb([128, L], BF16, "z")
        bct = sb([32, L], BF16, "bc")
        selt = sb([32, 32 * 128], BF16, "sel")
        at = sb([128, DS * NDB], F32, "A")
        dst_ = sb([128, NDB], F32, "dsk")
        psB = [ps([128, LC], F32, f"b{i}") for i in range(2)]
        psC = [ps([128, LC], F32, f"c{i}") for i in range(2)]
        b.ld(selt[:], selBC)
        b.ld(bct[:], bc_s)
        b.ld(dst_[:], dskip)
        for db in range(NDB):
            b.ld(at[:, db * DS:(db + 1) * DS], alog[db * 128:(db + 1) * 128, :])
        b.bar()
        b.s("activation", out=at[:], in_=at[:], func=AF.Exp)
        b.wait("vector", "act")
        b.v("tensor_scalar_mul", at[:], at[:], -1.0)
        b.bar()
        NJ = L // LC
        nct = b.cnt
        sem = b.sems
        for g in range(NDB // 2):
            b.bar()
            for m in range(2):
                db = 2 * g + m
                b.ld(dtb[m][:], dt_s[db * 128:(db + 1) * 128, :])
                b.ld(xcz[:] if m == 1 else zt[:], xc_s[db * 128:(db + 1) * 128, :])
                b.wait("vector", "ld")
                b.v("tensor_tensor", ub[m][:], dtb[m][:], zt[:] if m == 0 else xcz[:], ALU.mult)
                if m == 0:
                    b.wait("sync", "dve")
            b.g("memset", ya[0][:], 0.0)
            b.g("memset", ya[1][:], 0.0)
            umark = nct["dve"]
            gp_ms = nct["gp"]
            scan_mark = [umark, umark]
            dA_mark = [0, 0]
            G_mark = [gp_ms, gp_ms]
            evB_bank = [0, 0]
            evC_bank = [0, 0]
            brepR = [gp_ms, gp_ms]
            crepR = [umark, umark]
            for n in range(DS):
                par = n % 2
                for m in range(2):
                    col = (2 * g + m) * DS + n
                    nc.scalar.wait_ge(sem["dve"], scan_mark[m])
                    b.s("activation", out=dAt[m][:], in_=dtb[m][:], func=AF.Exp,
                        scale=at[:, col:col + 1])
                    dA_mark[m] = nct["act"]
                for j in range(NJ):
                    bank = j % 2
                    nc.tensor.wait_ge(sem["act"], evB_bank[bank])
                    b.pe("matmul", psB[bank][:, :], selt[:, n * 128:(n + 1) * 128],
                         bct[:, j * LC:(j + 1) * LC], start=True, stop=True)
                    mmk = nct["pe"]
                    nc.scalar.wait_ge(sem["pe"], mmk)
                    nc.scalar.wait_ge(sem["gp"], brepR[par])
                    b.s("copy", brep[par][:, j * LC:(j + 1) * LC], psB[bank][:, :])
                    evB_bank[bank] = nct["act"]
                evB_all = nct["act"]
                for j in range(NJ):
                    bank = j % 2
                    nc.tensor.wait_ge(sem["act"], evC_bank[bank])
                    b.pe("matmul", psC[bank][:, :], selt[:, (DS + n) * 128:(DS + n + 1) * 128],
                         bct[:, j * LC:(j + 1) * LC], start=True, stop=True)
                    mmk = nct["pe"]
                    nc.scalar.wait_ge(sem["pe"], mmk)
                    nc.scalar.wait_ge(sem["dve"], crepR[par])
                    b.s("copy", crep[par][:, j * LC:(j + 1) * LC], psC[bank][:, :])
                    evC_bank[bank] = nct["act"]
                evC_all = nct["act"]
                for m in range(2):
                    nc.gpsimd.wait_ge(sem["act"], evB_all)
                    nc.gpsimd.wait_ge(sem["dve"], scan_mark[m])
                    b.g("tensor_tensor", Gt[m][:], ub[m][:], brep[par][:], ALU.mult)
                    G_mark[m] = nct["gp"]
                brepR[par] = G_mark[1]
                for m in range(2):
                    nc.vector.wait_ge(sem["act"], dA_mark[m])
                    nc.vector.wait_ge(sem["gp"], G_mark[m])
                    b.v("tensor_tensor_scan", out=ht[:], data0=dAt[m][:], data1=Gt[m][:],
                        initial=0.0, op0=ALU.mult, op1=ALU.add)
                    scan_mark[m] = nct["dve"]
                    nc.vector.wait_ge(sem["act"], evC_all)
                    b.v("tensor_tensor", hct[:], ht[:], crep[par][:], ALU.mult)
                    b.v("tensor_tensor", ya[m][:], ya[m][:], hct[:], ALU.add)
                crepR[par] = nct["dve"]
            for m in range(2):
                db = 2 * g + m
                nc.sync.wait_ge(sem["dve"], nct["dve"])
                b.ld(xcz[:], xc_s[db * 128:(db + 1) * 128, :])
                b.ld(zt[:], z_s[db * 128:(db + 1) * 128, :])
                b.wait("scalar", "ld")
                b.s("activation", out=zt[:], in_=zt[:], func=AF.Silu)
                b.wait("vector", "ld")
                ymf = dAt[0]
                b.v("scalar_tensor_tensor", out=ymf[:], in0=xcz[:], scalar=dst_[:, db:db + 1],
                    in1=ya[m][:], op0=ALU.mult, op1=ALU.add)
                b.wait("vector", "act")
                b.v("tensor_tensor", hct[:], ymf[:], zt[:], ALU.mult)
                b.bar(("gpsimd",))
                b.st(ym_s[db * 128:(db + 1) * 128, :], hct[:])
                b.wait("sync", "st")
                b.wait("vector", "st")
        b.bar()

    with ExitStack() as stk:
        sb = lambda shape, dt, nm: stk.enter_context(nc.sbuf_tensor(f"C{k}_{nm}", list(shape), dt))
        ps = lambda shape, dt, nm: stk.enter_context(nc.psum_tensor(f"C{k}p_{nm}", list(shape), dt))
        opWb = emit_w_to_bf16(b, f"C{k}op", stk, opW, NDB, D)
        ymt = sb([128, NDB * LC], BF16, "ym")
        rt = sb([128, LC], F32, "rt")
        ot = sb([128, LC], F32, "ot")
        pso = [ps([128, LC], F32, f"o{i}") for i in range(2)]
        for c in range(NCH):
            sl = slice(c * LC, (c + 1) * LC)
            b.bar()
            for db in range(NDB):
                b.ld(ymt[:, db * LC:(db + 1) * LC], ym_s[db * 128:(db + 1) * 128, sl])
            b.wait("tensor", "ld")
            for ob in range(KD):
                pst = pso[ob % 2]
                b.wait("tensor", "dve")
                for db in range(NDB):
                    b.pe("matmul", pst[:, :], opWb[:, db * D + ob * 128:db * D + (ob + 1) * 128],
                         ymt[:, db * LC:(db + 1) * LC],
                         start=(db == 0), stop=(db == NDB - 1))
                b.wait("sync", "dve")
                b.ld(rt[:], rin[ob * 128:(ob + 1) * 128, sl])
                b.wait("vector", "pe", "ld", "st")
                b.v("tensor_tensor", ot[:], pst[:, :], rt[:], ALU.add)
                b.bar(("gpsimd",))
                b.st(rout[ob * 128:(ob + 1) * 128, sl], ot[:])
        b.bar()


def emit_ffn(b):
    nc = b.nc
    xn0 = b.stream("xn0")
    x1p = b.stream("x1p")
    x1 = b.stream("x1")
    WfcT = b.ext_in("WfcT", (D, FF), F32)
    WprT = b.ext_in("WprT", (FF, D), F32)
    AfcT = b.ext_in("AfcT", (D, 16), F32)
    BfcT = b.ext_in("BfcT", (16, FF), F32)
    AprT = b.ext_in("AprT", (FF, 16), F32)
    BprT = b.ext_in("BprT", (16, D), F32)
    bfc = b.ext_in("bfc", (128, NFB), F32)
    bpr = b.ext_in("bpr", (128, KD), F32)
    smax_d = b.internal("FF_smax", (1, 128), F32)
    if DEBUG_FFN:
        rfc_dbg = b.ext_out("dbg_rfc", (D, FF), BF16)
        sc_dbg = b.ext_out("dbg_sc", (128, 4), F32)
        h1_dbg = b.ext_out("dbg_h1", (FF, L), BF16)

    with ExitStack() as stk:
        sb = lambda shape, dt, nm: stk.enter_context(nc.sbuf_tensor(f"F_{nm}", list(shape), dt))
        ps = lambda shape, dt, nm: stk.enter_context(nc.psum_tensor(f"Fp_{nm}", list(shape), dt))
        rfc = sb([128, KD * FF], BF16, "rfc")
        rpr = sb([128, NFB * D], BF16, "rpr")
        sval = sb([1, 4], F32, "sval")
        svals = sb([1, 2], F32, "svals")
        sfcb = sb([128, 1], F32, "sfcb")
        sprb = sb([128, 1], F32, "sprb")
        rfcb = sb([128, 1], F32, "rfcb")
        rprb = sb([128, 1], F32, "rprb")
        rsval = sb([1, 2], F32, "rsval")
        onesr = sb([1, 128], F32, "onesr")
        AfcTb = sb([128, KD * 16], BF16, "AfcTb")
        BfcTb = sb([16, FF], BF16, "BfcTb")
        AprTb = sb([128, NFB * 16], BF16, "AprTb")
        BprTb = sb([16, D], BF16, "BprTb")
        bfct = sb([128, NFB], F32, "bfct")
        bprt = sb([128, KD], F32, "bprt")
        ps_bc = ps([128, 1], F32, "bc")
        stk2 = ExitStack()
        sb2 = lambda shape, dt, nm: stk2.enter_context(nc.sbuf_tensor(f"F2_{nm}", list(shape), dt))
        wstg = sb2([128, FF], F32, "wstg")
        rmax = sb2([128, 2], F32, "rmax")
        permax = sb2([128, 32], F32, "permax")
        rowall = sb2([1, 128 * 32], F32, "rowall")
        b.ld(onesr[:], b.dram["C_ones1x128"])
        b.ld(bfct[:], bfc)
        b.ld(bprt[:], bpr)
        for a in range(KD):
            b.ld(wstg[:, a * 16:(a + 1) * 16], AfcT[a * 128:(a + 1) * 128, :])
        for a in range(NFB):
            b.ld(wstg[:, 128 + a * 16:128 + (a + 1) * 16], AprT[a * 128:(a + 1) * 128, :])
        b.ld(wstg[0:16, 1024:1024 + D], BprT)
        b.wait("vector", "ld")
        b.v("tensor_copy", AfcTb[:], wstg[:, 0:KD * 16])
        b.v("tensor_copy", AprTb[:], wstg[:, 128:128 + NFB * 16])
        b.v("tensor_copy", BprTb[:], wstg[0:16, 1024:1024 + D])
        b.wait("sync", "dve")
        b.ld(wstg[0:16, 0:FF], BfcT)
        b.wait("vector", "ld")
        b.v("tensor_copy", BfcTb[:], wstg[0:16, 0:FF])
        b.bar()
        smax2 = b.internal("FF_smax2", (128, 32), F32)
        for which, W, nk, width, dstn in ((0, WfcT, KD, FF, rfc), (1, WprT, NFB, D, rpr)):
            b.v("memset", permax[:], 0.0)
            for kb in range(nk):
                b.wait("sync", "dve")
                b.ld(wstg[:, 0:width], W[kb * 128:(kb + 1) * 128, :])
                b.wait("vector", "ld")
                b.v("reduce_max", permax[:, kb:kb + 1], wstg[:, 0:width], axis=AX.X,
                    apply_absolute_value=True)
            b.bar(("gpsimd",))
            b.st(smax2, permax[:])
            b.wait("sync", "st")
            b.ld(rowall[:], smax2.rearrange("p n -> (p n)").rearrange("(o f) -> o f", o=1))
            b.wait("vector", "ld")
            b.v("reduce_max", sval[:, which:which + 1], rowall[:], axis=AX.X)
        b.v("tensor_scalar", svals[:, 0:2], sval[:, 0:2], 1.0 / 127.0, None, op0=ALU.mult)
        b.v("reciprocal", rsval[:, 0:2], svals[:, 0:2])
        b.wait("tensor", "dve")
        for src_c, dstb in ((svals[0:1, 0:1], sfcb), (svals[0:1, 1:2], sprb),
                            (rsval[0:1, 0:1], rfcb), (rsval[0:1, 1:2], rprb)):
            b.pe("matmul", ps_bc[:, :], onesr[:], src_c, start=True, stop=True)
            b.wait("scalar", "pe")
            b.s("copy", dstb[:], ps_bc[:, :])
            b.wait("tensor", "act")
        b.bar()
        for which, W, nk, width, dstn, scol in (
                (0, WfcT, KD, FF, rfc, rfcb), (1, WprT, NFB, D, rpr, rprb)):
            for kb in range(nk):
                b.wait("sync", "dve")
                b.ld(wstg[:, 0:width], W[kb * 128:(kb + 1) * 128, :])
                b.wait("vector", "ld")
                b.v("tensor_scalar", wstg[:, 0:width], wstg[:, 0:width], scol[:], None,
                    op0=ALU.mult)
                b.v("tensor_scalar", wstg[:, 0:width], wstg[:, 0:width], MAGIC, None,
                    op0=ALU.add)
                b.v("tensor_scalar", wstg[:, 0:width], wstg[:, 0:width], MAGIC, None,
                    op0=ALU.subtract)
                b.v("tensor_scalar", wstg[:, 0:width], wstg[:, 0:width], 127.0, -128.0,
                    op0=ALU.min, op1=ALU.max)
                b.v("tensor_copy", dstn[:, kb * width:(kb + 1) * width], wstg[:, 0:width])
        b.bar()
        stk2.close()
        if DEBUG_FFN:
            for kb in range(KD):
                b.st(rfc_dbg[kb * 128:(kb + 1) * 128, :], rfc[:, kb * FF:(kb + 1) * FF])
            for _i, _t in enumerate((sfcb, sprb, rfcb, rprb)):
                with b.nc.allow_non_contiguous_dma(reason="debug tap"):
                    b.st(sc_dbg[:, _i:_i + 1], _t[:])
            b.bar()
        xnt = sb([128, LC], F32, "xnt")
        xnbf = sb([128, KD * LC], BF16, "xnbf")
        t2b = sb([16, LC], BF16, "t2b")
        t4b = sb([16, LC], BF16, "t4b")
        h1b = sb([128, NFB * LC], BF16, "h1b")
        x1t = sb([128, LC], F32, "x1t")
        ot = sb([128, LC], F32, "ot")
        ps_l = ps([16, LC], F32, "l")
        ps_m = [ps([128, LC], F32, f"m{i}") for i in range(2)]
        for c in range(NCH):
            sl = slice(c * LC, (c + 1) * LC)
            b.bar()
            for kb in range(KD):
                b.ld(xnt[:], xn0[kb * 128:(kb + 1) * 128, sl])
                b.wait("vector", "ld")
                b.v("tensor_copy", xnbf[:, kb * LC:(kb + 1) * LC], xnt[:])
                b.wait("sync", "dve")
            b.wait("tensor", "dve")
            for kb in range(KD):
                b.pe("matmul", ps_l[:, :], AfcTb[:, kb * 16:(kb + 1) * 16],
                     xnbf[:, kb * LC:(kb + 1) * LC], start=(kb == 0), stop=(kb == KD - 1))
            b.wait("vector", "pe")
            b.v("tensor_scalar", t2b[:], ps_l[:, :], rfcb[0:16, :], 2.0,
                op0=ALU.mult, op1=ALU.mult)
            for fb in range(NFB):
                pst = ps_m[fb % 2]
                b.wait("tensor", "act", "dve")
                for kb in range(KD):
                    b.pe("matmul", pst[:, :], rfc[:, kb * FF + fb * 128:kb * FF + (fb + 1) * 128],
                         xnbf[:, kb * LC:(kb + 1) * LC], start=(kb == 0), stop=False)
                b.pe("matmul", pst[:, :], BfcTb[:, fb * 128:(fb + 1) * 128], t2b[:],
                     start=False, stop=True)
                b.wait("scalar", "pe")
                b.s("activation", out=h1b[:, fb * LC:(fb + 1) * LC], in_=pst[:, :],
                    func=AF.Gelu, bias=bfct[:, fb:fb + 1], scale=sfcb[:])
            b.wait("tensor", "act")
            if DEBUG_FFN:
                b.bar()
                for fb in range(NFB):
                    b.st(h1_dbg[fb * 128:(fb + 1) * 128, sl], h1b[:, fb * LC:(fb + 1) * LC])
                b.bar()
            for fb in range(NFB):
                b.pe("matmul", ps_l[:, :], AprTb[:, fb * 16:(fb + 1) * 16],
                     h1b[:, fb * LC:(fb + 1) * LC], start=(fb == 0), stop=(fb == NFB - 1))
            b.wait("vector", "pe", "st")
            b.v("tensor_scalar", t4b[:], ps_l[:, :], rprb[0:16, :], 2.0,
                op0=ALU.mult, op1=ALU.mult)
            for ob in range(KD):
                pst = ps_m[ob % 2]
                b.wait("tensor", "act", "dve")
                for fb in range(NFB):
                    b.pe("matmul", pst[:, :], rpr[:, fb * D + ob * 128:fb * D + (ob + 1) * 128],
                         h1b[:, fb * LC:(fb + 1) * LC], start=(fb == 0), stop=False)
                b.pe("matmul", pst[:, :], BprTb[:, ob * 128:(ob + 1) * 128], t4b[:],
                     start=False, stop=True)
                b.wait("scalar", "pe", "st")
                b.s("activation", out=ot[:], in_=pst[:, :], func=AF.Identity,
                    bias=bprt[:, ob:ob + 1], scale=sprb[:])
                b.wait("sync", "dve")
                b.ld(x1t[:], x1p[ob * 128:(ob + 1) * 128, sl])
                b.wait("vector", "ld", "act", "st")
                b.v("tensor_tensor", ot[:], ot[:], x1t[:], ALU.add)
                b.bar(("gpsimd",))
                b.st(x1[ob * 128:(ob + 1) * 128, sl], ot[:])
        b.bar()


def emit_attkv(b):
    nc = b.nc
    xn1 = b.stream("xn1")
    kfm = b.stream("kfm")
    vtm = b.stream("vtm")
    qfm = b.stream("qfm")
    WqT = b.ext_in("WqT", (D, D), F32)
    WkT = b.ext_in("WkT", (D, D), F32)
    WvT = b.ext_in("WvT", (D, D), F32)
    bq = b.ext_in("bq", (128, KD), F32)
    bk = b.ext_in("bk", (128, KD), F32)

    with ExitStack() as stk:
        sb = lambda shape, dt, nm: stk.enter_context(nc.sbuf_tensor(f"KV_{nm}", list(shape), dt))
        ps = lambda shape, dt, nm: stk.enter_context(nc.psum_tensor(f"KVp_{nm}", list(shape), dt))
        xnbf = sb([128, KD * LC], BF16, "xnbf")
        xnt = sb([128, LC], F32, "xnt")
        bqt = sb([128, KD], F32, "bqt")
        bkt = sb([128, KD], F32, "bkt")
        ob_ = sb([128, LC], BF16, "ob")
        ps_m = [ps([128, LC], F32, f"m{i}") for i in range(2)]
        b.ld(bqt[:], bq)
        b.ld(bkt[:], bk)
        wqb = emit_w_to_bf16(b, "KVq", stk, WqT, KD, D)
        wkb = emit_w_to_bf16(b, "KVk", stk, WkT, KD, D)
        wvb = emit_w_to_bf16(b, "KVv", stk, WvT, KD, D)
        for c in range(NCH):
            sl = slice(c * LC, (c + 1) * LC)
            b.bar()
            for kb in range(KD):
                b.ld(xnt[:], xn1[kb * 128:(kb + 1) * 128, sl])
                b.wait("vector", "ld")
                b.v("tensor_copy", xnbf[:, kb * LC:(kb + 1) * LC], xnt[:])
                b.wait("sync", "dve")
            b.wait("tensor", "dve")
            for wb, bcol, dstm in ((wqb, bqt, qfm), (wkb, bkt, kfm)):
                for eb in range(KD):
                    pst = ps_m[eb % 2]
                    b.wait("tensor", "act", "st")
                    for kb in range(KD):
                        b.pe("matmul", pst[:, :], wb[:, kb * D + eb * 128:kb * D + (eb + 1) * 128],
                             xnbf[:, kb * LC:(kb + 1) * LC], start=(kb == 0), stop=(kb == KD - 1))
                    b.wait("scalar", "pe")
                    b.s("activation", out=ob_[:], in_=pst[:, :], func=AF.Identity,
                        bias=bcol[:, eb:eb + 1])
                    b.bar(("gpsimd",))
                    b.st(dstm[eb * 128:(eb + 1) * 128, sl], ob_[:])
                    b.wait("scalar", "st")
            for lb in range(LC // 128):
                labs = c * LC + lb * 128
                for ec in range(2):
                    pst = ps_m[ec % 2]
                    b.wait("tensor", "act", "st")
                    for kb in range(KD):
                        b.pe("matmul", pst[:, :],
                             xnbf[:, kb * LC + lb * 128:kb * LC + (lb + 1) * 128],
                             wvb[:, kb * D + ec * LC:kb * D + (ec + 1) * LC],
                             start=(kb == 0), stop=(kb == KD - 1))
                    b.wait("scalar", "pe")
                    b.s("copy", ob_[:], pst[:, :])
                    b.bar(("gpsimd",))
                    b.st(vtm[labs:labs + 128, ec * LC:(ec + 1) * LC], ob_[:])
                    b.wait("scalar", "st")
        b.bar()


def emit_attq(b):
    nc = b.nc
    kfm = b.stream("kfm")
    vtm = b.stream("vtm")
    qfm = b.stream("qfm")
    att = b.stream("att")
    bv = b.ext_in("bv", (128, KD), F32)
    amask = b.ext_in("amask", (4, 128 * LC), F32)
    identb = b.ext_in("identb", (128, 128), BF16)
    NLB = L // 128
    if DEBUG_ATT:
        p_dbg = b.ext_out("dbg_p", (L, L), BF16)
        rs_dbg = b.ext_out("dbg_rs", (L, 2), F32)
        sc_dbg2 = b.ext_out("dbg_sc2", (256, L), F32)

    with ExitStack() as stk:
        sb = lambda shape, dt, nm: stk.enter_context(nc.sbuf_tensor(f"AQ_{nm}", list(shape), dt))
        ps = lambda shape, dt, nm: stk.enter_context(nc.psum_tensor(f"AQp_{nm}", list(shape), dt))
        kf = sb([128, KD * L], BF16, "kf")
        vt = sb([128, NLB * D], BF16, "vt")
        qt = sb([128, KD * 128], BF16, "qt")
        sc = sb([128, L], F32, "sc")
        pb_ = sb([128, L], BF16, "pb")
        pT = sb([128, NLB * 128], BF16, "pT")
        mask = sb([128, 4 * LC], F32, "mask")
        idt = sb([128, 128], BF16, "idt")
        bvt = sb([128, KD], F32, "bvt")
        negm = sb([128, 1], F32, "negm")
        negm2 = sb([128, 1], F32, "negm2")
        rsum = sb([128, 1], F32, "rsum")
        rsum2 = sb([128, 1], F32, "rsum2")
        ot = sb([128, 128], F32, "ot")
        ps_s = [ps([128, LC], F32, f"s{i}") for i in range(2)]
        ps_t = ps([128, 128], BF16, "t")
        ps_av = [ps([128, 128], F32, f"av{i}") for i in range(2)]
        for kb in range(KD):
            b.ld(kf[:, kb * L:(kb + 1) * L], kfm[kb * 128:(kb + 1) * 128, :])
        for lb in range(NLB):
            b.ld(vt[:, lb * D:(lb + 1) * D], vtm[lb * 128:(lb + 1) * 128, :])
        am = amask.rearrange("a (p f) -> (a p) f", p=128)
        for w in range(4):
            b.ld(mask[:, w * LC:(w + 1) * LC], am[w * 128:(w + 1) * 128, :])
        b.ld(idt[:], identb)
        b.ld(bvt[:], bv)
        b.bar()
        for i in range(NLB):
            nchunks = i // 4 + 1
            kcols = nchunks * LC
            w = i % 4
            b.bar()
            for kb in range(KD):
                b.ld(qt[:, kb * 128:(kb + 1) * 128], qfm[kb * 128:(kb + 1) * 128, i * 128:(i + 1) * 128])
            b.wait("tensor", "ld")
            for j in range(nchunks):
                pst = ps_s[j % 2]
                b.wait("tensor", "act")
                for kb in range(KD):
                    b.pe("matmul", pst[:, :], qt[:, kb * 128:(kb + 1) * 128],
                         kf[:, kb * L + j * LC:kb * L + (j + 1) * LC],
                         start=(kb == 0), stop=(kb == KD - 1))
                b.wait("scalar", "pe", "dve")
                b.s("copy", sc[:, j * LC:(j + 1) * LC], pst[:, :])
            b.wait("vector", "act")
            b.v("tensor_tensor", sc[:, (nchunks - 1) * LC:kcols],
                sc[:, (nchunks - 1) * LC:kcols], mask[:, w * LC:(w + 1) * LC], ALU.add)
            b.v("reduce_max", negm[:], sc[:, 0:kcols], axis=AX.X, negate=True)
            b.v("tensor_scalar_mul", negm2[:], negm[:], 1.0 / 32.0)
            b.wait("scalar", "dve")
            b.s("activation", out=pb_[:, 0:kcols], in_=sc[:, 0:kcols], func=AF.Exp,
                scale=1.0 / 32.0, bias=negm2[:], accum_out=rsum[:])
            b.wait("vector", "act")
            if DEBUG_ATT:
                b.bar()
                with b.nc.allow_non_contiguous_dma(reason="debug tap"):
                    b.st(rs_dbg[i * 128:(i + 1) * 128, 0:1], rsum[:])
                    b.st(rs_dbg[i * 128:(i + 1) * 128, 1:2], negm[:])
                if i < 2:
                    b.st(sc_dbg2[i * 128:(i + 1) * 128, 0:kcols], sc[:, 0:kcols])
                b.bar()
            b.v("reciprocal", rsum2[:], rsum[:])
            b.v("tensor_scalar", pb_[:, 0:kcols], pb_[:, 0:kcols], rsum2[:], None,
                op0=ALU.mult)
            if DEBUG_ATT:
                b.bar()
                b.st(p_dbg[i * 128:(i + 1) * 128, 0:kcols], pb_[:, 0:kcols])
                b.bar()
            for kb2 in range(4 * nchunks):
                b.wait("tensor", "dve", "act")
                b.pe("transpose", ps_t[:, :], pb_[:, kb2 * 128:(kb2 + 1) * 128], idt[:])
                b.wait("scalar", "pe")
                b.s("copy", pT[:, kb2 * 128:(kb2 + 1) * 128], ps_t[:, :])
            for db in range(KD):
                pst = ps_av[db % 2]
                b.wait("tensor", "act", "st")
                for kb2 in range(4 * nchunks):
                    b.pe("matmul", pst[:, :], vt[:, kb2 * D + db * 128:kb2 * D + (db + 1) * 128],
                         pT[:, kb2 * 128:(kb2 + 1) * 128],
                         start=(kb2 == 0), stop=(kb2 == 4 * nchunks - 1))
                b.wait("scalar", "pe")
                b.s("activation", out=ot[:], in_=pst[:, :], func=AF.Identity,
                    bias=bvt[:, db:db + 1])
                b.bar(("gpsimd",))
                b.st(att[db * 128:(db + 1) * 128, i * 128:(i + 1) * 128], ot[:])
                b.wait("scalar", "st")
        b.bar()


def emit_rwkv(b):
    nc = b.nc
    xn1 = b.stream("xn1")
    x1 = b.stream("x1")
    att = b.stream("att")
    out = b.ext_out("out", (D, L), F32)
    WrT = b.ext_in("WrT", (D, D), F32)
    Wk2T = b.ext_in("Wk2T", (D, D), F32)
    Wv2T = b.ext_in("Wv2T", (D, D), F32)
    mur = b.ext_in("mur", (128, KD), F32)
    muk = b.ext_in("muk", (128, KD), F32)

    with ExitStack() as stk:
        sb = lambda shape, dt, nm: stk.enter_context(nc.sbuf_tensor(f"RW_{nm}", list(shape), dt))
        ps = lambda shape, dt, nm: stk.enter_context(nc.psum_tensor(f"RWp_{nm}", list(shape), dt))
        wrb = emit_w_to_bf16(b, "RWr", stk, WrT, KD, D)
        wkb = emit_w_to_bf16(b, "RWk", stk, Wk2T, KD, D)
        wvb = emit_w_to_bf16(b, "RWv", stk, Wv2T, KD, D)
        murt = sb([128, KD], F32, "murt")
        mukt = sb([128, KD], F32, "mukt")
        b.ld(murt[:], mur)
        b.ld(mukt[:], muk)
        xnt = sb([128, LC], F32, "xnt")
        xpt = sb([128, LC], F32, "xpt")
        dt_ = sb([128, LC], F32, "d")
        xrb = sb([128, KD * LC], BF16, "xrb")
        xkb = sb([128, KD * LC], BF16, "xkb")
        rsig = sb([128, KD * LC], F32, "rsig")
        katb = sb([128, KD * LC], BF16, "katb")
        t1 = sb([128, LC], F32, "t1")
        x1t = sb([128, LC], F32, "x1t")
        att_t = sb([128, LC], F32, "attt")
        ot = sb([128, LC], F32, "ot")
        ps_m = [ps([128, LC], F32, f"m{i}") for i in range(2)]
        b.bar()
        for c in range(NCH):
            sl = slice(c * LC, (c + 1) * LC)
            b.bar()
            for kb in range(KD):
                b.ld(xnt[:], xn1[kb * 128:(kb + 1) * 128, sl])
                if c == 0:
                    b.v("memset", xpt[:, 0:1], 0.0)
                    b.ld(xpt[:, 1:LC], xn1[kb * 128:(kb + 1) * 128, 0:LC - 1])
                else:
                    b.ld(xpt[:], xn1[kb * 128:(kb + 1) * 128, c * LC - 1:(c + 1) * LC - 1])
                b.wait("vector", "ld")
                b.v("tensor_tensor", dt_[:], xnt[:], xpt[:], ALU.subtract)
                b.v("scalar_tensor_tensor", out=xrb[:, kb * LC:(kb + 1) * LC], in0=dt_[:],
                    scalar=murt[:, kb:kb + 1], in1=xpt[:], op0=ALU.mult, op1=ALU.add)
                b.v("scalar_tensor_tensor", out=xkb[:, kb * LC:(kb + 1) * LC], in0=dt_[:],
                    scalar=mukt[:, kb:kb + 1], in1=xpt[:], op0=ALU.mult, op1=ALU.add)
                b.wait("sync", "dve")
            b.wait("tensor", "dve")
            for eb in range(KD):
                pst = ps_m[eb % 2]
                b.wait("tensor", "act")
                for kb in range(KD):
                    b.pe("matmul", pst[:, :], wrb[:, kb * D + eb * 128:kb * D + (eb + 1) * 128],
                         xrb[:, kb * LC:(kb + 1) * LC], start=(kb == 0), stop=(kb == KD - 1))
                b.wait("scalar", "pe")
                b.s("activation", out=rsig[:, eb * LC:(eb + 1) * LC], in_=pst[:, :],
                    func=AF.Sigmoid)
            for eb in range(KD):
                pst = ps_m[eb % 2]
                b.wait("tensor", "act")
                for kb in range(KD):
                    b.pe("matmul", pst[:, :], wkb[:, kb * D + eb * 128:kb * D + (eb + 1) * 128],
                         xkb[:, kb * LC:(kb + 1) * LC], start=(kb == 0), stop=(kb == KD - 1))
                b.wait("scalar", "pe")
                b.s("activation", out=t1[:], in_=pst[:, :], func=AF.Relu)
                b.s("activation", out=katb[:, eb * LC:(eb + 1) * LC], in_=t1[:],
                    func=AF.Square)
            b.wait("tensor", "act")
            for eb in range(KD):
                pst = ps_m[eb % 2]
                b.wait("tensor", "dve")
                for kb in range(KD):
                    b.pe("matmul", pst[:, :], wvb[:, kb * D + eb * 128:kb * D + (eb + 1) * 128],
                         katb[:, kb * LC:(kb + 1) * LC], start=(kb == 0), stop=(kb == KD - 1))
                b.wait("sync", "dve")
                b.ld(x1t[:], x1[eb * 128:(eb + 1) * 128, sl])
                b.ld(att_t[:], att[eb * 128:(eb + 1) * 128, sl])
                b.wait("vector", "pe", "ld", "st", "act")
                b.v("tensor_tensor", ot[:], pst[:, :], rsig[:, eb * LC:(eb + 1) * LC], ALU.mult)
                b.v("tensor_tensor", ot[:], ot[:], x1t[:], ALU.add)
                b.v("tensor_tensor", ot[:], ot[:], att_t[:], ALU.add)
                b.bar(("gpsimd",))
                b.st(out[eb * 128:(eb + 1) * 128, sl], ot[:])
        b.bar()


PHASE_FN = {
    "ln0": emit_ln0,
    "m0": lambda b: emit_mamba_layer(b, 0),
    "m1": lambda b: emit_mamba_layer(b, 1),
    "m2": lambda b: emit_mamba_layer(b, 2),
    "m3": lambda b: emit_mamba_layer(b, 3),
    "mfin": emit_mfin,
    "ffn": emit_ffn,
    "ln1": emit_ln1,
    "attkv": emit_attkv,
    "attq": emit_attq,
    "rwkv": emit_rwkv,
}


def build_launch(phases):
    b = Build(phases)
    b.ext_in("C_ones128", (128, 1), F32)
    b.ext_in("C_ones1x128", (1, 128), F32)
    for p in phases:
        b.begin_phase(p)
        PHASE_FN[p](b)
    b.finish()
    return b



def _prep_shared(I):
    asc = np.ascontiguousarray
    d = {}
    d["C_ones128"] = np.ones((128, 1), np.float32)
    d["C_ones1x128"] = np.ones((1, 128), np.float32)
    sel = np.zeros((32, 32 * 128), np.float32)
    for q in range(32):
        sel[q, q * 128:(q + 1) * 128] = 1.0
    d["selBC"] = sel.astype(ml_dtypes.bfloat16)
    d["identb"] = np.eye(128, dtype=np.float32).astype(ml_dtypes.bfloat16)
    d["identf"] = np.eye(128, dtype=np.float32)
    for k in range(NLAY):
        d[f"ipW{k}"] = asc(I["L0_attn_in_proj_w"][k].T)
        d[f"xpW{k}"] = asc(I["L0_attn_x_proj_w"][k].T)
        d[f"dpW{k}"] = asc(I["L0_attn_dt_proj_w"][k].T)
        d[f"opW{k}"] = asc(I["L0_attn_out_proj_w"][k].T)
        d[f"cw{k}"] = asc(I["L0_attn_conv_w"][k])
        d[f"cb{k}"] = asc(I["L0_attn_conv_b"][k].reshape(NDB, 128).T)
        d[f"nw{k}"] = asc(I["L0_attn_norm_w"][k].reshape(KD, 128).T)
        d[f"nb{k}"] = asc(I["L0_attn_norm_b"][k].reshape(KD, 128).T)
        d[f"dtb{k}"] = asc(I["L0_attn_dt_proj_b"][k].reshape(NDB, 128).T)
        d[f"alog{k}"] = asc(I["L0_attn_A_log"][k])
        d[f"dskip{k}"] = asc(I["L0_attn_D_skip"][k].reshape(NDB, 128).T)
    d["normf_w"] = asc(I["L0_attn_normf_w"].reshape(KD, 128).T)
    d["normf_b"] = asc(I["L0_attn_normf_b"].reshape(KD, 128).T)
    d["WfcT"] = asc(I["L0_ffn_W_fc"].T)
    d["WprT"] = asc(I["L0_ffn_W_proj"].T)
    d["AfcT"] = asc(I["L0_ffn_A_fc"].T)
    d["BfcT"] = asc(I["L0_ffn_B_fc"].T)
    d["AprT"] = asc(I["L0_ffn_A_proj"].T)
    d["BprT"] = asc(I["L0_ffn_B_proj"].T)
    d["bfc"] = asc(I["L0_ffn_b_fc"].reshape(NFB, 128).T)
    d["bpr"] = asc(I["L0_ffn_b_proj"].reshape(KD, 128).T)
    d["WqT"] = asc(I["L1_attn_Wq"].T)
    d["WkT"] = asc(I["L1_attn_Wk"].T)
    d["WvT"] = asc(I["L1_attn_Wv"].T)
    d["bq"] = asc(I["L1_attn_bq"].reshape(KD, 128).T)
    d["bk"] = asc(I["L1_attn_bk"].reshape(KD, 128).T)
    d["bv"] = asc(I["L1_attn_bv"].reshape(KD, 128).T)
    d["WrT"] = asc(I["L1_ffn_W_r"].T)
    d["Wk2T"] = asc(I["L1_ffn_W_k"].T)
    d["Wv2T"] = asc(I["L1_ffn_W_v"].T)
    d["mur"] = asc(I["L1_ffn_mu_r"].reshape(KD, 128).T)
    d["muk"] = asc(I["L1_ffn_mu_k"].reshape(KD, 128).T)
    w = np.arange(4)[:, None, None]
    r = np.arange(128)[None, :, None]
    cc = np.arange(LC)[None, None, :]
    d["amask"] = np.ascontiguousarray(
        np.where(cc <= 128 * w + r, 0.0, -1e30).astype(np.float32).reshape(4, 128 * LC))
    return d


FUSED = True
LAUNCH_PLAN_UNFUSED = [["ln0"], ["m0"], ["m1"], ["m2"], ["m3"], ["mfin"],
                       ["ffn"], ["ln1"], ["attkv"], ["attq"], ["rwkv"]]
LAUNCH_PLAN = [list(ALL_PHASES)] if FUSED else LAUNCH_PLAN_UNFUSED


def kernel(**inputs):
    shared = _prep_shared(inputs)
    x_all = inputs["L0_attn_hidden_states"]
    ncores = 8
    xs = [np.ascontiguousarray(x_all[i].T) for i in range(ncores)]
    state = [dict() for _ in range(ncores)]

    for phases in LAUNCH_PLAN:
        b = build_launch(phases)
        in_maps = []
        for i in range(ncores):
            m = {}
            for name in b.inputs:
                if name.startswith("S_"):
                    m[name] = state[i][name]
                elif name == "x":
                    m[name] = xs[i]
                else:
                    m[name] = shared[name]
            in_maps.append(m)
        res = run_bass_kernel_spmd(b.nc, in_maps, core_ids=list(range(ncores)))
        for i in range(ncores):
            for name in b.outputs:
                state[i][name] = np.asarray(res.results[i][name])

    outs = np.stack([state[i]["out"].astype(np.float32).T for i in range(ncores)], axis=0)
    return np.ascontiguousarray(outs)


# version 04
# speedup vs baseline: 1.5011x; 1.1325x over previous
import numpy as np
import ml_dtypes
from contextlib import ExitStack

import concourse.bass as bass
import concourse.mybir as mybir
from concourse.bass_utils import run_bass_kernel_spmd

F32 = mybir.dt.float32
BF16 = mybir.dt.bfloat16
AF = mybir.ActivationFunctionType
ALU = mybir.AluOpType
AX = mybir.AxisListType

D = 1024
L = 4096
DI = 2048
DS = 16
DTR = 64
NLAY = 4
FF = 4096
NFB = FF // 128
LC = 512
NCH = L // LC
NDB = DI // 128
KD = D // 128
EPS = 1e-5
MAGIC = float(np.float32(1.5 * 2 ** 23))

ALL_PHASES = ["ln0", "m0", "m1", "m2", "m3", "mfin", "ffn", "ln1", "attkv", "attq", "rwkv"]

STREAMS = {
    "xn0":  ((D, L), F32, "ln0", ["m0", "ffn"]),
    "R1":   ((D, L), F32, "m0", ["m1"]),
    "R2":   ((D, L), F32, "m1", ["m2"]),
    "R3":   ((D, L), F32, "m2", ["m3"]),
    "R4":   ((D, L), F32, "m3", ["mfin"]),
    "x1p":  ((D, L), F32, "mfin", ["ffn"]),
    "x1":   ((D, L), F32, "ffn", ["ln1", "rwkv"]),
    "xn1":  ((D, L), F32, "ln1", ["attkv", "attq", "rwkv"]),
    "kfm":  ((D, L), BF16, "attkv", ["attq"]),
    "vtm":  ((L, D), BF16, "attkv", ["attq"]),
    "qfm":  ((D, L), BF16, "attkv", ["attq"]),
    "att":  ((D, L), F32, "attq", ["rwkv"]),
}
DEBUG_SCRATCH = False
DEBUG_PHASE_A = False
DEBUG_FFN = False
DEBUG_ATT = False

MSCRATCH = {
    "xc_s": ((DI, L), BF16),
    "z_s":  ((DI, L), BF16),
    "dt_s": ((DI, L), BF16),
    "bc_s": ((32, L), BF16),
    "ym_s": ((DI, L), BF16),
}


class Build:
    def __init__(self, phases):
        self.phases = phases
        self.nc = bass.Bass("TRN2", target_bir_lowering=False)
        self.ex = ExitStack()
        self.sems = {}
        self.cnt = {}
        self._phase_idx = 0
        self._new_sems("g")
        self.inputs = {}
        self.outputs = {}
        self.dram = {}
        self.uid = 0

    def _new_sems(self, tag):
        for s in ["ld", "st", "pe", "act", "dve", "gp"]:
            self.sems[s] = self.ex.enter_context(self.nc.semaphore(f"{s}_{tag}"))
            self.cnt[s] = 0

    def begin_phase(self, name):
        if self._phase_idx > 0:
            self.bar()
        self._phase_idx += 1
        self._new_sems(f"p{self._phase_idx}_{name}")

    def ext_in(self, name, shape, dtype):
        if name not in self.dram:
            t = self.nc.dram_tensor(name, list(shape), dtype, kind="ExternalInput")
            self.inputs[name] = t
            self.dram[name] = t.ap()
        return self.dram[name]

    def ext_out(self, name, shape, dtype):
        if name not in self.dram:
            t = self.nc.dram_tensor(name, list(shape), dtype, kind="ExternalOutput")
            self.outputs[name] = t
            self.dram[name] = t.ap()
        return self.dram[name]

    def internal(self, name, shape, dtype):
        if name not in self.dram:
            t = self.nc.dram_tensor(name, list(shape), dtype, kind="Internal")
            self.dram[name] = t.ap()
        return self.dram[name]

    def stream(self, name):
        shape, dtype, prod, cons = STREAMS[name]
        key = "S_" + name
        if key in self.dram:
            return self.dram[key]
        if prod not in self.phases:
            return self.ext_in(key, shape, dtype)
        if any(c not in self.phases for c in cons):
            return self.ext_out(key, shape, dtype)
        return self.internal(key, shape, dtype)

    def mscratch(self, name):
        shape, dtype = MSCRATCH[name]
        if DEBUG_SCRATCH:
            return self.ext_out("M_" + name, shape, dtype)
        return self.internal("M_" + name, shape, dtype)

    def _bump(self, instr, sem, amt=1):
        instr.then_inc(self.sems[sem], amt)
        self.cnt[sem] += amt

    def ld(self, out_ap, in_ap):
        self._bump(self.nc.sync.dma_start(out=out_ap, in_=in_ap), "ld", 16)

    def st(self, out_ap, in_ap):
        self._bump(self.nc.gpsimd.dma_start(out=out_ap, in_=in_ap), "st", 16)

    def pe(self, meth, *a, **k):
        self._bump(getattr(self.nc.tensor, meth)(*a, **k), "pe")

    @staticmethod
    def _outsz(a, k):
        ap = k.get("out", a[0] if a else None)
        try:
            sz = 1
            for d in ap.shape:
                sz *= d
            return sz
        except Exception:
            return 1 << 30

    def v(self, meth, *a, **k):
        self._bump(getattr(self.nc.vector, meth)(*a, **k), "dve")
        if self._outsz(a, k) <= 8192:
            self.nc.vector.wait_ge(self.sems["dve"], self.cnt["dve"])

    def s(self, meth, *a, **k):
        self._bump(getattr(self.nc.scalar, meth)(*a, **k), "act")
        if self._outsz(a, k) <= 8192:
            self.nc.scalar.wait_ge(self.sems["act"], self.cnt["act"])

    def g(self, meth, *a, **k):
        self._bump(getattr(self.nc.gpsimd, meth)(*a, **k), "gp")
        if self._outsz(a, k) <= 8192:
            self.nc.gpsimd.wait_ge(self.sems["gp"], self.cnt["gp"])

    def wait(self, eng_name, *sems):
        eng = getattr(self.nc, eng_name)
        for s in sems:
            if self.cnt[s] > 0:
                eng.wait_ge(self.sems[s], self.cnt[s])

    def bar(self, engines=("tensor", "vector", "scalar", "gpsimd", "sync")):
        for e in engines:
            self.wait(e, "ld", "st", "pe", "act", "dve", "gp")

    def finish(self):
        self.nc.sync.wait_ge(self.sems["st"], self.cnt["st"])
        self.ex.close()



def emit_ln_stream(b, tag, src_ap, dst_ap, w_col=None, b_col=None, add_ap=None):
    nc = b.nc
    with ExitStack() as stk:
        sb = lambda shape, dt, nm: stk.enter_context(nc.sbuf_tensor(f"{tag}_{nm}", list(shape), dt))
        ps = lambda shape, dt, nm: stk.enter_context(nc.psum_tensor(f"{tag}p_{nm}", list(shape), dt))
        xt = [sb([128, LC], F32, f"x{i}") for i in range(KD)]
        sq = sb([128, LC], F32, "sq")
        ot = sb([128, LC], F32, "ot")
        addt = sb([128, LC], F32, "add") if add_ap is not None else None
        mu = sb([1, LC], F32, "mu")
        ex2 = sb([1, LC], F32, "ex2")
        rstd = sb([1, LC], F32, "rstd")
        onesc = sb([128, 1], F32, "onesc")
        onesr = sb([1, 128], F32, "onesr")
        epsT = sb([1, 1], F32, "eps")
        ps_sum = ps([1, LC], F32, "sum")
        ps_sq = ps([1, LC], F32, "sqs")
        ps_mu = ps([128, LC], F32, "mub")
        ps_rs = ps([128, LC], F32, "rsb")
        b.ld(onesc[:], b.dram["C_ones128"])
        b.ld(onesr[:], b.dram["C_ones1x128"])
        b.v("memset", epsT[:], EPS)
        b.bar()
        for c in range(NCH):
            sl = slice(c * LC, (c + 1) * LC)
            b.bar()
            for kb in range(KD):
                b.ld(xt[kb][:], src_ap[kb * 128:(kb + 1) * 128, sl])
            b.wait("tensor", "ld")
            b.wait("scalar", "ld")
            for kb in range(KD):
                b.pe("matmul", ps_sum[:, :], onesc[:], xt[kb][:],
                     start=(kb == 0), stop=(kb == KD - 1))
            for kb in range(KD):
                if kb > 0:
                    b.wait("scalar", "pe")
                b.s("activation", out=sq[:], in_=xt[kb][:], func=AF.Square)
                b.wait("tensor", "act")
                b.pe("matmul", ps_sq[:, :], onesc[:], sq[:],
                     start=(kb == 0), stop=(kb == KD - 1))
            b.wait("vector", "pe")
            b.v("tensor_scalar_mul", mu[:, :], ps_sum[:, :], 1.0 / D)
            b.v("tensor_scalar_mul", ex2[:, :], ps_sq[:, :], 1.0 / D)
            b.v("tensor_tensor", rstd[:, :], mu[:, :], mu[:, :], ALU.mult)
            b.v("tensor_tensor", rstd[:, :], ex2[:, :], rstd[:, :], ALU.subtract)
            b.wait("scalar", "dve")
            b.s("activation", out=rstd[:], in_=rstd[:], func=AF.Sqrt, bias=epsT[0:1, 0:1])
            b.wait("vector", "act")
            b.v("reciprocal", rstd[:], rstd[:])
            b.wait("tensor", "dve")
            b.pe("matmul", ps_mu[:, :], onesr[:], mu[:, :], start=True, stop=True)
            b.pe("matmul", ps_rs[:, :], onesr[:], rstd[:, :], start=True, stop=True)
            b.wait("vector", "pe")
            b.wait("sync", "dve", "act", "st")
            for kb in range(KD):
                if add_ap is not None:
                    b.ld(addt[:], add_ap[kb * 128:(kb + 1) * 128, sl])
                b.v("tensor_tensor", ot[:], xt[kb][:], ps_mu[:, :], ALU.subtract)
                b.v("tensor_tensor", ot[:], ot[:], ps_rs[:, :], ALU.mult)
                if w_col is not None:
                    b.wait("scalar", "dve")
                    b.s("activation", out=ot[:], in_=ot[:], func=AF.Identity,
                        bias=b_col[:, kb:kb + 1], scale=w_col[:, kb:kb + 1])
                if add_ap is not None:
                    b.wait("vector", "ld")
                    if w_col is not None:
                        b.wait("vector", "act")
                    b.v("tensor_tensor", ot[:], ot[:], addt[:], ALU.add)
                b.bar(("gpsimd",))
                b.st(dst_ap[kb * 128:(kb + 1) * 128, sl], ot[:])
                b.wait("vector", "st")
                b.wait("scalar", "st")
                b.wait("sync", "st")
        b.bar()


def emit_w_to_bf16(b, tag, stk, src_ap, nk, width, layout_width=None):
    nc = b.nc
    lw = layout_width or width
    dst = stk.enter_context(nc.sbuf_tensor(f"{tag}_wb", [128, nk * lw], BF16))
    stg = stk.enter_context(nc.sbuf_tensor(f"{tag}_wstg", [128, width], F32))
    for kb in range(nk):
        b.wait("sync", "dve")
        b.ld(stg[:], src_ap[kb * 128:(kb + 1) * 128, :])
        b.wait("vector", "ld")
        b.v("tensor_copy", dst[:, kb * lw:kb * lw + width], stg[:])
    b.bar()
    return dst



def emit_ln0(b):
    x = b.ext_in("x", (D, L), F32)
    emit_ln_stream(b, "ln0", x, b.stream("xn0"))


def emit_ln1(b):
    emit_ln_stream(b, "ln1", b.stream("x1"), b.stream("xn1"))


def emit_mfin(b):
    x = b.ext_in("x", (D, L), F32)
    with ExitStack() as stk:
        nwc = stk.enter_context(b.nc.sbuf_tensor("mfin_nw", [128, KD], F32))
        nbc = stk.enter_context(b.nc.sbuf_tensor("mfin_nb", [128, KD], F32))
        b.ld(nwc[:], b.ext_in("normf_w", (128, KD), F32))
        b.ld(nbc[:], b.ext_in("normf_b", (128, KD), F32))
        b.bar()
        emit_ln_stream(b, "mfin", b.stream("R4"), b.stream("x1p"),
                       w_col=nwc, b_col=nbc, add_ap=x)


def emit_mamba_layer(b, k):
    nc = b.nc
    rin = b.stream("xn0") if k == 0 else b.stream(f"R{k}")
    rout = b.stream(f"R{k + 1}")
    xc_s = b.mscratch("xc_s")
    z_s = b.mscratch("z_s")
    dt_s = b.mscratch("dt_s")
    bc_s = b.mscratch("bc_s")
    ym_s = b.mscratch("ym_s")

    ipW = b.ext_in(f"ipW{k}", (D, 2 * DI), F32)
    xpW = b.ext_in(f"xpW{k}", (DI, 96), F32)
    dpW = b.ext_in(f"dpW{k}", (DTR, DI), F32)
    opW = b.ext_in(f"opW{k}", (DI, D), F32)
    cw = b.ext_in(f"cw{k}", (DI, 4), F32)
    cb = b.ext_in(f"cb{k}", (128, NDB), F32)
    nw = b.ext_in(f"nw{k}", (128, KD), F32)
    nb = b.ext_in(f"nb{k}", (128, KD), F32)
    dtb = b.ext_in(f"dtb{k}", (128, NDB), F32)
    alog = b.ext_in(f"alog{k}", (DI, DS), F32)
    dskip = b.ext_in(f"dskip{k}", (128, NDB), F32)
    selBC = b.ext_in("selBC", (32, 32 * 128), BF16)
    if DEBUG_PHASE_A:
        xnd_dbg = b.ext_out(f"dbg_xn{k}", (D, L), BF16)
        xcpre_dbg = b.ext_out(f"dbg_xcpre{k}", (DI, L), F32)

    with ExitStack() as stk:
        sb = lambda shape, dt, nm: stk.enter_context(nc.sbuf_tensor(f"A{k}_{nm}", list(shape), dt))
        ps = lambda shape, dt, nm: stk.enter_context(nc.psum_tensor(f"A{k}p_{nm}", list(shape), dt))
        ipWb = emit_w_to_bf16(b, f"A{k}ip", stk, ipW, KD, 2 * DI)
        xpWb = emit_w_to_bf16(b, f"A{k}xp", stk, xpW, NDB, 96)
        dpWb = sb([64, DI], BF16, "dpWb")
        wstg = sb([64, DI], F32, "dpstg")
        b.ld(wstg[:], dpW)
        b.wait("vector", "ld")
        b.v("tensor_copy", dpWb[:], wstg[:])
        cwt = sb([128, NDB * 4], F32, "cwt")
        cbt = sb([128, NDB], F32, "cbt")
        nwt = sb([128, KD], F32, "nwt")
        nbt = sb([128, KD], F32, "nbt")
        dtbt = sb([128, NDB], F32, "dtbt")
        onesc = sb([128, 1], F32, "onesc")
        onesr = sb([1, 128], F32, "onesr")
        epsT = sb([1, 1], F32, "eps")
        b.v("memset", epsT[:], EPS)
        b.ld(onesc[:], b.dram["C_ones128"])
        b.ld(onesr[:], b.dram["C_ones1x128"])
        for db in range(NDB):
            b.ld(cwt[:, db * 4:(db + 1) * 4], cw[db * 128:(db + 1) * 128, :])
        b.ld(cbt[:], cb)
        b.ld(nwt[:], nw)
        b.ld(nbt[:], nb)
        b.ld(dtbt[:], dtb)

        rt = [sb([128, LC], F32, f"rt{i}") for i in range(KD)]
        xnb = sb([128, KD * LC], BF16, "xnb")
        sq = [sb([128, LC], F32, f"sq{i}") for i in range(2)]
        mu = sb([1, LC], F32, "mu")
        ex2 = sb([1, LC], F32, "ex2")
        rstd = sb([1, LC], F32, "rstd")
        ext = [sb([128, LC + 3], F32, f"ext{i}") for i in range(2)]
        tail = [sb([128, 4], F32, f"tail{i}") for i in range(NDB)]
        xcf = [sb([128, LC], F32, f"xcf{i}") for i in range(2)]
        xcb = sb([128, NDB * LC], BF16, "xcb")
        zt = [sb([128, LC], BF16, f"zt{i}") for i in range(2)]
        xdbb = sb([96, LC], BF16, "xdbb")
        t1 = [sb([128, LC], F32, f"t1{i}") for i in range(2)]
        t2 = [sb([128, LC], F32, f"t2{i}") for i in range(2)]
        t2b_ = [sb([128, LC], BF16, f"t2b{i}") for i in range(2)]
        ps_sum = ps([1, LC], F32, "sum")
        ps_sq = ps([1, LC], F32, "sqs")
        ps_mu = ps([128, LC], F32, "mub")
        ps_rs = ps([128, LC], F32, "rsb")
        ps_a = [ps([128, LC], F32, f"a{i}") for i in range(2)]
        ps_x = ps([96, LC], F32, "x96")

        for db in range(NDB):
            b.v("memset", tail[db][:], 0.0)
        b.bar()
        nct = b.cnt
        sem = b.sems
        rt_reader = 0
        rt_dve = 0
        xcbR_pe = 0
        
        ps_evac = [0, 0]
        extR = [0, 0]
        siluM = [0, 0]
        convM = [0, 0]
        castM = 0
        ztR = [0, 0]
        t1R = [0, 0]
        t2R = [0, 0]
        t2stR = [0, 0]
        stat_reader = 0
        xdbbR_pe = 0
        xdbbR_st = 0
        xnb_pe = 0
        for c in range(NCH):
            sl = slice(c * LC, (c + 1) * LC)
            nc.sync.wait_ge(sem["pe"], rt_reader)
            nc.sync.wait_ge(sem["dve"], rt_dve)
            for kb in range(KD):
                b.ld(rt[kb][:], rin[kb * 128:(kb + 1) * 128, sl])
            ld_rt = nct["ld"]
            nc.tensor.wait_ge(sem["ld"], ld_rt)
            nc.tensor.wait_ge(sem["dve"], stat_reader)
            for kb in range(KD):
                b.pe("matmul", ps_sum[:, :], onesc[:], rt[kb][:],
                     start=(kb == 0), stop=(kb == KD - 1))
            nc.scalar.wait_ge(sem["ld"], ld_rt)
            sq_pe = [0, 0]
            for kb in range(KD):
                par = kb % 2
                nc.scalar.wait_ge(sem["pe"], sq_pe[par])
                b.s("activation", out=sq[par][:], in_=rt[kb][:], func=AF.Square)
                sq_m = nct["act"]
                nc.tensor.wait_ge(sem["act"], sq_m)
                b.pe("matmul", ps_sq[:, :], onesc[:], sq[par][:],
                     start=(kb == 0), stop=(kb == KD - 1))
                sq_pe[par] = nct["pe"]
            pe_stats = nct["pe"]
            nc.vector.wait_ge(sem["pe"], pe_stats)
            b.v("tensor_scalar_mul", mu[:, :], ps_sum[:, :], 1.0 / D)
            b.v("tensor_scalar_mul", ex2[:, :], ps_sq[:, :], 1.0 / D)
            b.v("tensor_tensor", rstd[:, :], mu[:, :], mu[:, :], ALU.mult)
            b.v("tensor_tensor", rstd[:, :], ex2[:, :], rstd[:, :], ALU.subtract)
            dvem = nct["dve"]
            nc.scalar.wait_ge(sem["dve"], dvem)
            b.s("activation", out=rstd[:], in_=rstd[:], func=AF.Sqrt, bias=epsT[0:1, 0:1])
            nc.vector.wait_ge(sem["act"], nct["act"])
            b.v("reciprocal", rstd[:], rstd[:])
            dvem = nct["dve"]
            nc.tensor.wait_ge(sem["dve"], dvem)
            b.pe("matmul", ps_mu[:, :], onesr[:], mu[:, :], start=True, stop=True)
            b.pe("matmul", ps_rs[:, :], onesr[:], rstd[:, :], start=True, stop=True)
            bc_m = nct["pe"]
            nc.vector.wait_ge(sem["pe"], bc_m)
            for kb in range(KD):
                par = kb % 2
                nc.vector.wait_ge(sem["act"], nct["act"])
                b.v("tensor_tensor", sq[par][:], rt[kb][:], ps_mu[:, :], ALU.subtract)
                b.v("tensor_tensor", sq[par][:], sq[par][:], ps_rs[:, :], ALU.mult)
                ap_m = nct["dve"]
                nc.scalar.wait_ge(sem["dve"], ap_m)
                if kb == 0:
                    nc.scalar.wait_ge(sem["pe"], xnb_pe)
                b.s("activation", out=xnb[:, kb * LC:(kb + 1) * LC], in_=sq[par][:], func=AF.Identity,
                    bias=nbt[:, kb:kb + 1], scale=nwt[:, kb:kb + 1])
            xnb_ready = nct["act"]
            rt_dve = nct["dve"]
            rt_reader = nct["pe"]
            stat_reader = nct["dve"]
            nc.tensor.wait_ge(sem["act"], xnb_ready)
            for db in range(NDB):
                par = db % 2
                nc.tensor.wait_ge(sem["act"], ps_evac[par])
                for kb in range(KD):
                    b.pe("matmul", ps_a[par][:, :],
                         ipWb[:, kb * 2 * DI + db * 128: kb * 2 * DI + (db + 1) * 128],
                         xnb[:, kb * LC:(kb + 1) * LC],
                         start=(kb == 0), stop=(kb == KD - 1))
                mm_m = nct["pe"]
                nc.scalar.wait_ge(sem["pe"], mm_m)
                nc.scalar.wait_ge(sem["dve"], extR[par])
                b.s("copy", ext[par][:, 3:3 + LC], ps_a[par][:, :])
                ps_evac[par] = nct["act"]
                nc.vector.wait_ge(sem["act"], ps_evac[par])
                nc.vector.wait_ge(sem["act"], siluM[par])
                b.v("tensor_copy", ext[par][:, 0:3], tail[db][:, 0:3])
                b.v("tensor_scalar", xcf[par][:], ext[par][:, 0:LC],
                    cwt[:, db * 4:db * 4 + 1], cbt[:, db:db + 1], op0=ALU.mult, op1=ALU.add)
                for tap in range(1, 4):
                    b.v("scalar_tensor_tensor", out=xcf[par][:], in0=ext[par][:, tap:tap + LC],
                        scalar=cwt[:, db * 4 + tap:db * 4 + tap + 1], in1=xcf[par][:],
                        op0=ALU.mult, op1=ALU.add)
                b.v("tensor_copy", tail[db][:, 0:3], ext[par][:, LC:LC + 3])
                convM[par] = nct["dve"]
                extR[par] = nct["dve"]
                nc.scalar.wait_ge(sem["dve"], convM[par])
                b.s("activation", out=xcf[par][:], in_=xcf[par][:], func=AF.Silu)
                siluM[par] = nct["act"]
                nc.vector.wait_ge(sem["act"], siluM[par])
                if c > 0 and db == 0:
                    nc.vector.wait_ge(sem["pe"], xcbR_pe)
                b.v("tensor_copy", xcb[:, db * LC:(db + 1) * LC], xcf[par][:])
                cast_m = nct["dve"]
                nc.gpsimd.wait_ge(sem["dve"], cast_m)
                b.st(xc_s[db * 128:(db + 1) * 128, sl], xcb[:, db * LC:(db + 1) * LC])
            for db in range(NDB):
                par = db % 2
                nc.tensor.wait_ge(sem["act"], ps_evac[par])
                for kb in range(KD):
                    b.pe("matmul", ps_a[par][:, :],
                         ipWb[:, kb * 2 * DI + DI + db * 128: kb * 2 * DI + DI + (db + 1) * 128],
                         xnb[:, kb * LC:(kb + 1) * LC],
                         start=(kb == 0), stop=(kb == KD - 1))
                mm_m = nct["pe"]
                nc.scalar.wait_ge(sem["pe"], mm_m)
                nc.scalar.wait_ge(sem["st"], ztR[par])
                b.s("copy", zt[par][:], ps_a[par][:, :])
                ps_evac[par] = nct["act"]
                nc.gpsimd.wait_ge(sem["act"], ps_evac[par])
                b.st(z_s[db * 128:(db + 1) * 128, sl], zt[par][:])
                ztR[par] = nct["st"]
                if db == NDB - 1:
                    xnb_pe = nct["pe"]
            cast_all = nct["dve"]
            nc.tensor.wait_ge(sem["dve"], cast_all)
            for db in range(NDB):
                b.pe("matmul", ps_x[:, :], xpWb[:, db * 96:(db + 1) * 96],
                     xcb[:, db * LC:(db + 1) * LC],
                     start=(db == 0), stop=(db == NDB - 1))
            mm_m = nct["pe"]
            xcbR_pe = mm_m
            nc.scalar.wait_ge(sem["pe"], mm_m)
            nc.scalar.wait_ge(sem["pe"], xdbbR_pe)
            nc.scalar.wait_ge(sem["st"], xdbbR_st)
            b.s("copy", xdbb[:], ps_x[:, :])
            xdbb_ready = nct["act"]
            nc.gpsimd.wait_ge(sem["act"], xdbb_ready)
            b.st(bc_s[:, sl], xdbb[64:96, :])
            xdbbR_st = nct["st"]
            for db in range(NDB):
                par = db % 2
                nc.tensor.wait_ge(sem["act"], xdbb_ready)
                nc.tensor.wait_ge(sem["act"], ps_evac[par])
                nc.tensor.wait_ge(sem["dve"], t2R[par])
                b.pe("matmul", ps_a[par][:, :], dpWb[:, db * 128:(db + 1) * 128],
                     xdbb[0:64, :], start=True, stop=True)
                mm_m = nct["pe"]
                nc.scalar.wait_ge(sem["pe"], mm_m)
                nc.scalar.wait_ge(sem["dve"], t1R[par])
                b.s("activation", out=t1[par][:], in_=ps_a[par][:, :], func=AF.Abs,
                    bias=dtbt[:, db:db + 1])
                nc.vector.wait_ge(sem["act"], nct["act"])
                b.v("tensor_scalar", t2[par][:], ps_a[par][:, :], dtbt[:, db:db + 1], 0.0,
                    op0=ALU.add, op1=ALU.max)
                ps_evac[par] = nct["act"]
                b.s("activation", out=t1[par][:], in_=t1[par][:], func=AF.Exp, scale=-1.0)
                b.s("activation", out=t1[par][:], in_=t1[par][:], func=AF.Ln, bias=1.0)
                t1_m = nct["act"]
                nc.vector.wait_ge(sem["act"], t1_m)
                nc.vector.wait_ge(sem["st"], t2stR[par])
                b.v("tensor_tensor", t2b_[par][:], t2[par][:], t1[par][:], ALU.add)
                t2R[par] = nct["dve"]
                t1R[par] = nct["dve"]
                nc.gpsimd.wait_ge(sem["dve"], t2R[par])
                b.st(dt_s[db * 128:(db + 1) * 128, sl], t2b_[par][:])
                t2stR[par] = nct["st"]
            xdbbR_pe = nct["pe"]
        b.bar()

    with ExitStack() as stk:
        sb = lambda shape, dt, nm: stk.enter_context(nc.sbuf_tensor(f"B{k}_{nm}", list(shape), dt))
        ps = lambda shape, dt, nm: stk.enter_context(nc.psum_tensor(f"B{k}p_{nm}", list(shape), dt))
        dtb = [sb([128, L], BF16, f"dt{m}") for m in range(2)]
        ub = [sb([128, L], BF16, f"u{m}") for m in range(2)]
        ya = [sb([128, L], F32, f"ya{m}") for m in range(2)]
        dAt = [sb([128, L], F32, f"dA{m}") for m in range(2)]
        Gt = [sb([128, L], BF16, f"G{m}") for m in range(2)]
        ht = sb([128, L], BF16, "h")
        hct = sb([128, L], BF16, "hc")
        brep = [sb([128, L], BF16, f"br{p}") for p in range(2)]
        crep = [sb([128, L], BF16, f"cr{p}") for p in range(2)]
        xcz = sb([128, L], BF16, "xcz")
        zt = sb([128, L], BF16, "z")
        bct = sb([32, L], BF16, "bc")
        selt = sb([32, 32 * 128], BF16, "sel")
        at = sb([128, DS * NDB], F32, "A")
        dst_ = sb([128, NDB], F32, "dsk")
        psB = [ps([128, LC], F32, f"b{i}") for i in range(2)]
        psC = [ps([128, LC], F32, f"c{i}") for i in range(2)]
        b.ld(selt[:], selBC)
        b.ld(bct[:], bc_s)
        b.ld(dst_[:], dskip)
        for db in range(NDB):
            b.ld(at[:, db * DS:(db + 1) * DS], alog[db * 128:(db + 1) * 128, :])
        b.bar()
        b.s("activation", out=at[:], in_=at[:], func=AF.Exp)
        b.wait("vector", "act")
        b.v("tensor_scalar_mul", at[:], at[:], -1.0)
        b.bar()
        NJ = L // LC
        nct = b.cnt
        sem = b.sems
        for g in range(NDB // 2):
            b.bar()
            for m in range(2):
                db = 2 * g + m
                b.ld(dtb[m][:], dt_s[db * 128:(db + 1) * 128, :])
                b.ld(xcz[:] if m == 1 else zt[:], xc_s[db * 128:(db + 1) * 128, :])
                b.wait("vector", "ld")
                b.v("tensor_tensor", ub[m][:], dtb[m][:], zt[:] if m == 0 else xcz[:], ALU.mult)
                if m == 0:
                    b.wait("sync", "dve")
            b.g("memset", ya[0][:], 0.0)
            b.g("memset", ya[1][:], 0.0)
            umark = nct["dve"]
            gp_ms = nct["gp"]
            scan_mark = [umark, umark]
            dA_mark = [0, 0]
            G_mark = [gp_ms, gp_ms]
            evB_bank = [0, 0]
            evC_bank = [0, 0]
            brepR = [gp_ms, gp_ms]
            crepR = [umark, umark]
            for n in range(DS):
                par = n % 2
                for m in range(2):
                    col = (2 * g + m) * DS + n
                    nc.scalar.wait_ge(sem["dve"], scan_mark[m])
                    b.s("activation", out=dAt[m][:], in_=dtb[m][:], func=AF.Exp,
                        scale=at[:, col:col + 1])
                    dA_mark[m] = nct["act"]
                for j in range(NJ):
                    bank = j % 2
                    nc.tensor.wait_ge(sem["act"], evB_bank[bank])
                    b.pe("matmul", psB[bank][:, :], selt[:, n * 128:(n + 1) * 128],
                         bct[:, j * LC:(j + 1) * LC], start=True, stop=True)
                    mmk = nct["pe"]
                    nc.scalar.wait_ge(sem["pe"], mmk)
                    nc.scalar.wait_ge(sem["gp"], brepR[par])
                    b.s("copy", brep[par][:, j * LC:(j + 1) * LC], psB[bank][:, :])
                    evB_bank[bank] = nct["act"]
                evB_all = nct["act"]
                for j in range(NJ):
                    bank = j % 2
                    nc.tensor.wait_ge(sem["act"], evC_bank[bank])
                    b.pe("matmul", psC[bank][:, :], selt[:, (DS + n) * 128:(DS + n + 1) * 128],
                         bct[:, j * LC:(j + 1) * LC], start=True, stop=True)
                    mmk = nct["pe"]
                    nc.scalar.wait_ge(sem["pe"], mmk)
                    nc.scalar.wait_ge(sem["dve"], crepR[par])
                    b.s("copy", crep[par][:, j * LC:(j + 1) * LC], psC[bank][:, :])
                    evC_bank[bank] = nct["act"]
                evC_all = nct["act"]
                for m in range(2):
                    nc.gpsimd.wait_ge(sem["act"], evB_all)
                    nc.gpsimd.wait_ge(sem["dve"], scan_mark[m])
                    b.g("tensor_tensor", Gt[m][:], ub[m][:], brep[par][:], ALU.mult)
                    G_mark[m] = nct["gp"]
                brepR[par] = G_mark[1]
                for m in range(2):
                    nc.vector.wait_ge(sem["act"], dA_mark[m])
                    nc.vector.wait_ge(sem["gp"], G_mark[m])
                    b.v("tensor_tensor_scan", out=ht[:], data0=dAt[m][:], data1=Gt[m][:],
                        initial=0.0, op0=ALU.mult, op1=ALU.add)
                    scan_mark[m] = nct["dve"]
                    nc.vector.wait_ge(sem["act"], evC_all)
                    b.v("tensor_tensor", hct[:], ht[:], crep[par][:], ALU.mult)
                    b.v("tensor_tensor", ya[m][:], ya[m][:], hct[:], ALU.add)
                crepR[par] = nct["dve"]
            for m in range(2):
                db = 2 * g + m
                nc.sync.wait_ge(sem["dve"], nct["dve"])
                b.ld(xcz[:], xc_s[db * 128:(db + 1) * 128, :])
                b.ld(zt[:], z_s[db * 128:(db + 1) * 128, :])
                b.wait("scalar", "ld")
                b.s("activation", out=zt[:], in_=zt[:], func=AF.Silu)
                b.wait("vector", "ld")
                ymf = dAt[0]
                b.v("scalar_tensor_tensor", out=ymf[:], in0=xcz[:], scalar=dst_[:, db:db + 1],
                    in1=ya[m][:], op0=ALU.mult, op1=ALU.add)
                b.wait("vector", "act")
                b.v("tensor_tensor", hct[:], ymf[:], zt[:], ALU.mult)
                b.bar(("gpsimd",))
                b.st(ym_s[db * 128:(db + 1) * 128, :], hct[:])
                b.wait("sync", "st")
                b.wait("vector", "st")
        b.bar()

    with ExitStack() as stk:
        sb = lambda shape, dt, nm: stk.enter_context(nc.sbuf_tensor(f"C{k}_{nm}", list(shape), dt))
        ps = lambda shape, dt, nm: stk.enter_context(nc.psum_tensor(f"C{k}p_{nm}", list(shape), dt))
        opWb = emit_w_to_bf16(b, f"C{k}op", stk, opW, NDB, D)
        ymt = sb([128, NDB * LC], BF16, "ym")
        rt = sb([128, LC], F32, "rt")
        ot = sb([128, LC], F32, "ot")
        pso = [ps([128, LC], F32, f"o{i}") for i in range(2)]
        for c in range(NCH):
            sl = slice(c * LC, (c + 1) * LC)
            b.bar()
            for db in range(NDB):
                b.ld(ymt[:, db * LC:(db + 1) * LC], ym_s[db * 128:(db + 1) * 128, sl])
            b.wait("tensor", "ld")
            for ob in range(KD):
                pst = pso[ob % 2]
                b.wait("tensor", "dve")
                for db in range(NDB):
                    b.pe("matmul", pst[:, :], opWb[:, db * D + ob * 128:db * D + (ob + 1) * 128],
                         ymt[:, db * LC:(db + 1) * LC],
                         start=(db == 0), stop=(db == NDB - 1))
                b.wait("sync", "dve")
                b.ld(rt[:], rin[ob * 128:(ob + 1) * 128, sl])
                b.wait("vector", "pe", "ld", "st")
                b.v("tensor_tensor", ot[:], pst[:, :], rt[:], ALU.add)
                b.bar(("gpsimd",))
                b.st(rout[ob * 128:(ob + 1) * 128, sl], ot[:])
        b.bar()


def emit_ffn(b):
    nc = b.nc
    xn0 = b.stream("xn0")
    x1p = b.stream("x1p")
    x1 = b.stream("x1")
    WfcT = b.ext_in("WfcT", (D, FF), F32)
    WprT = b.ext_in("WprT", (FF, D), F32)
    AfcT = b.ext_in("AfcT", (D, 16), F32)
    BfcT = b.ext_in("BfcT", (16, FF), F32)
    AprT = b.ext_in("AprT", (FF, 16), F32)
    BprT = b.ext_in("BprT", (16, D), F32)
    bfc = b.ext_in("bfc", (128, NFB), F32)
    bpr = b.ext_in("bpr", (128, KD), F32)
    smax_d = b.internal("FF_smax", (1, 128), F32)
    if DEBUG_FFN:
        rfc_dbg = b.ext_out("dbg_rfc", (D, FF), BF16)
        sc_dbg = b.ext_out("dbg_sc", (128, 4), F32)
        h1_dbg = b.ext_out("dbg_h1", (FF, L), BF16)

    with ExitStack() as stk:
        sb = lambda shape, dt, nm: stk.enter_context(nc.sbuf_tensor(f"F_{nm}", list(shape), dt))
        ps = lambda shape, dt, nm: stk.enter_context(nc.psum_tensor(f"Fp_{nm}", list(shape), dt))
        rfc = sb([128, KD * FF], BF16, "rfc")
        rpr = sb([128, NFB * D], BF16, "rpr")
        sval = sb([1, 4], F32, "sval")
        svals = sb([1, 2], F32, "svals")
        sfcb = sb([128, 1], F32, "sfcb")
        sprb = sb([128, 1], F32, "sprb")
        rfcb = sb([128, 1], F32, "rfcb")
        rprb = sb([128, 1], F32, "rprb")
        rsval = sb([1, 2], F32, "rsval")
        onesr = sb([1, 128], F32, "onesr")
        AfcTb = sb([128, KD * 16], BF16, "AfcTb")
        BfcTb = sb([16, FF], BF16, "BfcTb")
        AprTb = sb([128, NFB * 16], BF16, "AprTb")
        BprTb = sb([16, D], BF16, "BprTb")
        bfct = sb([128, NFB], F32, "bfct")
        bprt = sb([128, KD], F32, "bprt")
        ps_bc = ps([128, 1], F32, "bc")
        stk2 = ExitStack()
        sb2 = lambda shape, dt, nm: stk2.enter_context(nc.sbuf_tensor(f"F2_{nm}", list(shape), dt))
        wstg = sb2([128, FF], F32, "wstg")
        rmax = sb2([128, 2], F32, "rmax")
        permax = sb2([128, 32], F32, "permax")
        rowall = sb2([1, 128 * 32], F32, "rowall")
        b.ld(onesr[:], b.dram["C_ones1x128"])
        b.ld(bfct[:], bfc)
        b.ld(bprt[:], bpr)
        for a in range(KD):
            b.ld(wstg[:, a * 16:(a + 1) * 16], AfcT[a * 128:(a + 1) * 128, :])
        for a in range(NFB):
            b.ld(wstg[:, 128 + a * 16:128 + (a + 1) * 16], AprT[a * 128:(a + 1) * 128, :])
        b.ld(wstg[0:16, 1024:1024 + D], BprT)
        b.wait("vector", "ld")
        b.v("tensor_copy", AfcTb[:], wstg[:, 0:KD * 16])
        b.v("tensor_copy", AprTb[:], wstg[:, 128:128 + NFB * 16])
        b.v("tensor_copy", BprTb[:], wstg[0:16, 1024:1024 + D])
        b.wait("sync", "dve")
        b.ld(wstg[0:16, 0:FF], BfcT)
        b.wait("vector", "ld")
        b.v("tensor_copy", BfcTb[:], wstg[0:16, 0:FF])
        b.bar()
        smax2 = b.internal("FF_smax2", (128, 32), F32)
        for which, W, nk, width, dstn in ((0, WfcT, KD, FF, rfc), (1, WprT, NFB, D, rpr)):
            b.v("memset", permax[:], 0.0)
            for kb in range(nk):
                b.wait("sync", "dve")
                b.ld(wstg[:, 0:width], W[kb * 128:(kb + 1) * 128, :])
                b.wait("vector", "ld")
                b.v("reduce_max", permax[:, kb:kb + 1], wstg[:, 0:width], axis=AX.X,
                    apply_absolute_value=True)
            b.bar(("gpsimd",))
            b.st(smax2, permax[:])
            b.wait("sync", "st")
            b.ld(rowall[:], smax2.rearrange("p n -> (p n)").rearrange("(o f) -> o f", o=1))
            b.wait("vector", "ld")
            b.v("reduce_max", sval[:, which:which + 1], rowall[:], axis=AX.X)
        b.v("tensor_scalar", svals[:, 0:2], sval[:, 0:2], 1.0 / 127.0, None, op0=ALU.mult)
        b.v("reciprocal", rsval[:, 0:2], svals[:, 0:2])
        b.wait("tensor", "dve")
        for src_c, dstb in ((svals[0:1, 0:1], sfcb), (svals[0:1, 1:2], sprb),
                            (rsval[0:1, 0:1], rfcb), (rsval[0:1, 1:2], rprb)):
            b.pe("matmul", ps_bc[:, :], onesr[:], src_c, start=True, stop=True)
            b.wait("scalar", "pe")
            b.s("copy", dstb[:], ps_bc[:, :])
            b.wait("tensor", "act")
        b.bar()
        for which, W, nk, width, dstn, scol in (
                (0, WfcT, KD, FF, rfc, rfcb), (1, WprT, NFB, D, rpr, rprb)):
            for kb in range(nk):
                b.wait("sync", "dve")
                b.ld(wstg[:, 0:width], W[kb * 128:(kb + 1) * 128, :])
                b.wait("vector", "ld")
                b.v("tensor_scalar", wstg[:, 0:width], wstg[:, 0:width], scol[:], None,
                    op0=ALU.mult)
                b.v("tensor_scalar", wstg[:, 0:width], wstg[:, 0:width], MAGIC, None,
                    op0=ALU.add)
                b.v("tensor_scalar", wstg[:, 0:width], wstg[:, 0:width], MAGIC, None,
                    op0=ALU.subtract)
                b.v("tensor_scalar", wstg[:, 0:width], wstg[:, 0:width], 127.0, -128.0,
                    op0=ALU.min, op1=ALU.max)
                b.v("tensor_copy", dstn[:, kb * width:(kb + 1) * width], wstg[:, 0:width])
        b.bar()
        stk2.close()
        if DEBUG_FFN:
            for kb in range(KD):
                b.st(rfc_dbg[kb * 128:(kb + 1) * 128, :], rfc[:, kb * FF:(kb + 1) * FF])
            for _i, _t in enumerate((sfcb, sprb, rfcb, rprb)):
                with b.nc.allow_non_contiguous_dma(reason="debug tap"):
                    b.st(sc_dbg[:, _i:_i + 1], _t[:])
            b.bar()
        xnt = sb([128, LC], F32, "xnt")
        xnbf = sb([128, KD * LC], BF16, "xnbf")
        t2b = sb([16, LC], BF16, "t2b")
        t4b = sb([16, LC], BF16, "t4b")
        h1b = sb([128, NFB * LC], BF16, "h1b")
        x1t = sb([128, LC], F32, "x1t")
        ot = sb([128, LC], F32, "ot")
        ps_l = ps([16, LC], F32, "l")
        ps_m = [ps([128, LC], F32, f"m{i}") for i in range(2)]
        for c in range(NCH):
            sl = slice(c * LC, (c + 1) * LC)
            b.bar()
            for kb in range(KD):
                b.ld(xnt[:], xn0[kb * 128:(kb + 1) * 128, sl])
                b.wait("vector", "ld")
                b.v("tensor_copy", xnbf[:, kb * LC:(kb + 1) * LC], xnt[:])
                b.wait("sync", "dve")
            b.wait("tensor", "dve")
            for kb in range(KD):
                b.pe("matmul", ps_l[:, :], AfcTb[:, kb * 16:(kb + 1) * 16],
                     xnbf[:, kb * LC:(kb + 1) * LC], start=(kb == 0), stop=(kb == KD - 1))
            b.wait("vector", "pe")
            b.v("tensor_scalar", t2b[:], ps_l[:, :], rfcb[0:16, :], 2.0,
                op0=ALU.mult, op1=ALU.mult)
            for fb in range(NFB):
                pst = ps_m[fb % 2]
                b.wait("tensor", "act", "dve")
                for kb in range(KD):
                    b.pe("matmul", pst[:, :], rfc[:, kb * FF + fb * 128:kb * FF + (fb + 1) * 128],
                         xnbf[:, kb * LC:(kb + 1) * LC], start=(kb == 0), stop=False)
                b.pe("matmul", pst[:, :], BfcTb[:, fb * 128:(fb + 1) * 128], t2b[:],
                     start=False, stop=True)
                b.wait("scalar", "pe")
                b.s("activation", out=h1b[:, fb * LC:(fb + 1) * LC], in_=pst[:, :],
                    func=AF.Gelu, bias=bfct[:, fb:fb + 1], scale=sfcb[:])
            b.wait("tensor", "act")
            if DEBUG_FFN:
                b.bar()
                for fb in range(NFB):
                    b.st(h1_dbg[fb * 128:(fb + 1) * 128, sl], h1b[:, fb * LC:(fb + 1) * LC])
                b.bar()
            for fb in range(NFB):
                b.pe("matmul", ps_l[:, :], AprTb[:, fb * 16:(fb + 1) * 16],
                     h1b[:, fb * LC:(fb + 1) * LC], start=(fb == 0), stop=(fb == NFB - 1))
            b.wait("vector", "pe", "st")
            b.v("tensor_scalar", t4b[:], ps_l[:, :], rprb[0:16, :], 2.0,
                op0=ALU.mult, op1=ALU.mult)
            for ob in range(KD):
                pst = ps_m[ob % 2]
                b.wait("tensor", "act", "dve")
                for fb in range(NFB):
                    b.pe("matmul", pst[:, :], rpr[:, fb * D + ob * 128:fb * D + (ob + 1) * 128],
                         h1b[:, fb * LC:(fb + 1) * LC], start=(fb == 0), stop=False)
                b.pe("matmul", pst[:, :], BprTb[:, ob * 128:(ob + 1) * 128], t4b[:],
                     start=False, stop=True)
                b.wait("scalar", "pe", "st")
                b.s("activation", out=ot[:], in_=pst[:, :], func=AF.Identity,
                    bias=bprt[:, ob:ob + 1], scale=sprb[:])
                b.wait("sync", "dve")
                b.ld(x1t[:], x1p[ob * 128:(ob + 1) * 128, sl])
                b.wait("vector", "ld", "act", "st")
                b.v("tensor_tensor", ot[:], ot[:], x1t[:], ALU.add)
                b.bar(("gpsimd",))
                b.st(x1[ob * 128:(ob + 1) * 128, sl], ot[:])
        b.bar()


def emit_attkv(b):
    nc = b.nc
    xn1 = b.stream("xn1")
    kfm = b.stream("kfm")
    vtm = b.stream("vtm")
    qfm = b.stream("qfm")
    WqT = b.ext_in("WqT", (D, D), F32)
    WkT = b.ext_in("WkT", (D, D), F32)
    WvT = b.ext_in("WvT", (D, D), F32)
    bq = b.ext_in("bq", (128, KD), F32)
    bk = b.ext_in("bk", (128, KD), F32)

    with ExitStack() as stk:
        sb = lambda shape, dt, nm: stk.enter_context(nc.sbuf_tensor(f"KV_{nm}", list(shape), dt))
        ps = lambda shape, dt, nm: stk.enter_context(nc.psum_tensor(f"KVp_{nm}", list(shape), dt))
        xnbf = sb([128, KD * LC], BF16, "xnbf")
        xnt = sb([128, LC], F32, "xnt")
        bqt = sb([128, KD], F32, "bqt")
        bkt = sb([128, KD], F32, "bkt")
        ob_ = sb([128, LC], BF16, "ob")
        ps_m = [ps([128, LC], F32, f"m{i}") for i in range(2)]
        b.ld(bqt[:], bq)
        b.ld(bkt[:], bk)
        wqb = emit_w_to_bf16(b, "KVq", stk, WqT, KD, D)
        wkb = emit_w_to_bf16(b, "KVk", stk, WkT, KD, D)
        wvb = emit_w_to_bf16(b, "KVv", stk, WvT, KD, D)
        for c in range(NCH):
            sl = slice(c * LC, (c + 1) * LC)
            b.bar()
            for kb in range(KD):
                b.ld(xnt[:], xn1[kb * 128:(kb + 1) * 128, sl])
                b.wait("vector", "ld")
                b.v("tensor_copy", xnbf[:, kb * LC:(kb + 1) * LC], xnt[:])
                b.wait("sync", "dve")
            b.wait("tensor", "dve")
            for wb, bcol, dstm in ((wqb, bqt, qfm), (wkb, bkt, kfm)):
                for eb in range(KD):
                    pst = ps_m[eb % 2]
                    b.wait("tensor", "act", "st")
                    for kb in range(KD):
                        b.pe("matmul", pst[:, :], wb[:, kb * D + eb * 128:kb * D + (eb + 1) * 128],
                             xnbf[:, kb * LC:(kb + 1) * LC], start=(kb == 0), stop=(kb == KD - 1))
                    b.wait("scalar", "pe")
                    b.s("activation", out=ob_[:], in_=pst[:, :], func=AF.Identity,
                        bias=bcol[:, eb:eb + 1])
                    b.bar(("gpsimd",))
                    b.st(dstm[eb * 128:(eb + 1) * 128, sl], ob_[:])
                    b.wait("scalar", "st")
            for lb in range(LC // 128):
                labs = c * LC + lb * 128
                for ec in range(2):
                    pst = ps_m[ec % 2]
                    b.wait("tensor", "act", "st")
                    for kb in range(KD):
                        b.pe("matmul", pst[:, :],
                             xnbf[:, kb * LC + lb * 128:kb * LC + (lb + 1) * 128],
                             wvb[:, kb * D + ec * LC:kb * D + (ec + 1) * LC],
                             start=(kb == 0), stop=(kb == KD - 1))
                    b.wait("scalar", "pe")
                    b.s("copy", ob_[:], pst[:, :])
                    b.bar(("gpsimd",))
                    b.st(vtm[labs:labs + 128, ec * LC:(ec + 1) * LC], ob_[:])
                    b.wait("scalar", "st")
        b.bar()


def emit_attq(b):
    nc = b.nc
    kfm = b.stream("kfm")
    vtm = b.stream("vtm")
    qfm = b.stream("qfm")
    att = b.stream("att")
    bv = b.ext_in("bv", (128, KD), F32)
    amask = b.ext_in("amask", (4, 128 * LC), F32)
    identb = b.ext_in("identb", (128, 128), BF16)
    NLB = L // 128
    if DEBUG_ATT:
        p_dbg = b.ext_out("dbg_p", (L, L), BF16)
        rs_dbg = b.ext_out("dbg_rs", (L, 2), F32)
        sc_dbg2 = b.ext_out("dbg_sc2", (256, L), F32)

    with ExitStack() as stk:
        sb = lambda shape, dt, nm: stk.enter_context(nc.sbuf_tensor(f"AQ_{nm}", list(shape), dt))
        ps = lambda shape, dt, nm: stk.enter_context(nc.psum_tensor(f"AQp_{nm}", list(shape), dt))
        kf = sb([128, KD * L], BF16, "kf")
        vt = sb([128, NLB * D], BF16, "vt")
        qt = sb([128, KD * 128], BF16, "qt")
        sc = sb([128, L], F32, "sc")
        pb_ = sb([128, L], BF16, "pb")
        pT = sb([128, NLB * 128], BF16, "pT")
        mask = sb([128, 4 * LC], F32, "mask")
        idt = sb([128, 128], BF16, "idt")
        bvt = sb([128, KD], F32, "bvt")
        negm = sb([128, 1], F32, "negm")
        negm2 = sb([128, 1], F32, "negm2")
        rsum = sb([128, 1], F32, "rsum")
        rsum2 = sb([128, 1], F32, "rsum2")
        ot = sb([128, 128], F32, "ot")
        ps_s = [ps([128, LC], F32, f"s{i}") for i in range(2)]
        ps_t = ps([128, 128], BF16, "t")
        ps_av = [ps([128, 128], F32, f"av{i}") for i in range(2)]
        for kb in range(KD):
            b.ld(kf[:, kb * L:(kb + 1) * L], kfm[kb * 128:(kb + 1) * 128, :])
        for lb in range(NLB):
            b.ld(vt[:, lb * D:(lb + 1) * D], vtm[lb * 128:(lb + 1) * 128, :])
        am = amask.rearrange("a (p f) -> (a p) f", p=128)
        for w in range(4):
            b.ld(mask[:, w * LC:(w + 1) * LC], am[w * 128:(w + 1) * 128, :])
        b.ld(idt[:], identb)
        b.ld(bvt[:], bv)
        b.bar()
        for i in range(NLB):
            nchunks = i // 4 + 1
            kcols = nchunks * LC
            w = i % 4
            b.bar()
            for kb in range(KD):
                b.ld(qt[:, kb * 128:(kb + 1) * 128], qfm[kb * 128:(kb + 1) * 128, i * 128:(i + 1) * 128])
            b.wait("tensor", "ld")
            for j in range(nchunks):
                pst = ps_s[j % 2]
                b.wait("tensor", "act")
                for kb in range(KD):
                    b.pe("matmul", pst[:, :], qt[:, kb * 128:(kb + 1) * 128],
                         kf[:, kb * L + j * LC:kb * L + (j + 1) * LC],
                         start=(kb == 0), stop=(kb == KD - 1))
                b.wait("scalar", "pe", "dve")
                b.s("copy", sc[:, j * LC:(j + 1) * LC], pst[:, :])
            b.wait("vector", "act")
            b.v("tensor_tensor", sc[:, (nchunks - 1) * LC:kcols],
                sc[:, (nchunks - 1) * LC:kcols], mask[:, w * LC:(w + 1) * LC], ALU.add)
            b.v("reduce_max", negm[:], sc[:, 0:kcols], axis=AX.X, negate=True)
            b.v("tensor_scalar_mul", negm2[:], negm[:], 1.0 / 32.0)
            b.wait("scalar", "dve")
            b.s("activation", out=pb_[:, 0:kcols], in_=sc[:, 0:kcols], func=AF.Exp,
                scale=1.0 / 32.0, bias=negm2[:], accum_out=rsum[:])
            b.wait("vector", "act")
            if DEBUG_ATT:
                b.bar()
                with b.nc.allow_non_contiguous_dma(reason="debug tap"):
                    b.st(rs_dbg[i * 128:(i + 1) * 128, 0:1], rsum[:])
                    b.st(rs_dbg[i * 128:(i + 1) * 128, 1:2], negm[:])
                if i < 2:
                    b.st(sc_dbg2[i * 128:(i + 1) * 128, 0:kcols], sc[:, 0:kcols])
                b.bar()
            b.v("reciprocal", rsum2[:], rsum[:])
            b.v("tensor_scalar", pb_[:, 0:kcols], pb_[:, 0:kcols], rsum2[:], None,
                op0=ALU.mult)
            if DEBUG_ATT:
                b.bar()
                b.st(p_dbg[i * 128:(i + 1) * 128, 0:kcols], pb_[:, 0:kcols])
                b.bar()
            for kb2 in range(4 * nchunks):
                b.wait("tensor", "dve", "act")
                b.pe("transpose", ps_t[:, :], pb_[:, kb2 * 128:(kb2 + 1) * 128], idt[:])
                b.wait("scalar", "pe")
                b.s("copy", pT[:, kb2 * 128:(kb2 + 1) * 128], ps_t[:, :])
            for db in range(KD):
                pst = ps_av[db % 2]
                b.wait("tensor", "act", "st")
                for kb2 in range(4 * nchunks):
                    b.pe("matmul", pst[:, :], vt[:, kb2 * D + db * 128:kb2 * D + (db + 1) * 128],
                         pT[:, kb2 * 128:(kb2 + 1) * 128],
                         start=(kb2 == 0), stop=(kb2 == 4 * nchunks - 1))
                b.wait("scalar", "pe")
                b.s("activation", out=ot[:], in_=pst[:, :], func=AF.Identity,
                    bias=bvt[:, db:db + 1])
                b.bar(("gpsimd",))
                b.st(att[db * 128:(db + 1) * 128, i * 128:(i + 1) * 128], ot[:])
                b.wait("scalar", "st")
        b.bar()


def emit_rwkv(b):
    nc = b.nc
    xn1 = b.stream("xn1")
    x1 = b.stream("x1")
    att = b.stream("att")
    out = b.ext_out("out", (D, L), F32)
    WrT = b.ext_in("WrT", (D, D), F32)
    Wk2T = b.ext_in("Wk2T", (D, D), F32)
    Wv2T = b.ext_in("Wv2T", (D, D), F32)
    mur = b.ext_in("mur", (128, KD), F32)
    muk = b.ext_in("muk", (128, KD), F32)

    with ExitStack() as stk:
        sb = lambda shape, dt, nm: stk.enter_context(nc.sbuf_tensor(f"RW_{nm}", list(shape), dt))
        ps = lambda shape, dt, nm: stk.enter_context(nc.psum_tensor(f"RWp_{nm}", list(shape), dt))
        wrb = emit_w_to_bf16(b, "RWr", stk, WrT, KD, D)
        wkb = emit_w_to_bf16(b, "RWk", stk, Wk2T, KD, D)
        wvb = emit_w_to_bf16(b, "RWv", stk, Wv2T, KD, D)
        murt = sb([128, KD], F32, "murt")
        mukt = sb([128, KD], F32, "mukt")
        b.ld(murt[:], mur)
        b.ld(mukt[:], muk)
        xnt = sb([128, LC], F32, "xnt")
        xpt = sb([128, LC], F32, "xpt")
        dt_ = sb([128, LC], F32, "d")
        xrb = sb([128, KD * LC], BF16, "xrb")
        xkb = sb([128, KD * LC], BF16, "xkb")
        rsig = sb([128, KD * LC], F32, "rsig")
        katb = sb([128, KD * LC], BF16, "katb")
        t1 = sb([128, LC], F32, "t1")
        x1t = sb([128, LC], F32, "x1t")
        att_t = sb([128, LC], F32, "attt")
        ot = sb([128, LC], F32, "ot")
        ps_m = [ps([128, LC], F32, f"m{i}") for i in range(2)]
        b.bar()
        for c in range(NCH):
            sl = slice(c * LC, (c + 1) * LC)
            b.bar()
            for kb in range(KD):
                b.ld(xnt[:], xn1[kb * 128:(kb + 1) * 128, sl])
                if c == 0:
                    b.v("memset", xpt[:, 0:1], 0.0)
                    b.ld(xpt[:, 1:LC], xn1[kb * 128:(kb + 1) * 128, 0:LC - 1])
                else:
                    b.ld(xpt[:], xn1[kb * 128:(kb + 1) * 128, c * LC - 1:(c + 1) * LC - 1])
                b.wait("vector", "ld")
                b.v("tensor_tensor", dt_[:], xnt[:], xpt[:], ALU.subtract)
                b.v("scalar_tensor_tensor", out=xrb[:, kb * LC:(kb + 1) * LC], in0=dt_[:],
                    scalar=murt[:, kb:kb + 1], in1=xpt[:], op0=ALU.mult, op1=ALU.add)
                b.v("scalar_tensor_tensor", out=xkb[:, kb * LC:(kb + 1) * LC], in0=dt_[:],
                    scalar=mukt[:, kb:kb + 1], in1=xpt[:], op0=ALU.mult, op1=ALU.add)
                b.wait("sync", "dve")
            b.wait("tensor", "dve")
            for eb in range(KD):
                pst = ps_m[eb % 2]
                b.wait("tensor", "act")
                for kb in range(KD):
                    b.pe("matmul", pst[:, :], wrb[:, kb * D + eb * 128:kb * D + (eb + 1) * 128],
                         xrb[:, kb * LC:(kb + 1) * LC], start=(kb == 0), stop=(kb == KD - 1))
                b.wait("scalar", "pe")
                b.s("activation", out=rsig[:, eb * LC:(eb + 1) * LC], in_=pst[:, :],
                    func=AF.Sigmoid)
            for eb in range(KD):
                pst = ps_m[eb % 2]
                b.wait("tensor", "act")
                for kb in range(KD):
                    b.pe("matmul", pst[:, :], wkb[:, kb * D + eb * 128:kb * D + (eb + 1) * 128],
                         xkb[:, kb * LC:(kb + 1) * LC], start=(kb == 0), stop=(kb == KD - 1))
                b.wait("scalar", "pe")
                b.s("activation", out=t1[:], in_=pst[:, :], func=AF.Relu)
                b.s("activation", out=katb[:, eb * LC:(eb + 1) * LC], in_=t1[:],
                    func=AF.Square)
            b.wait("tensor", "act")
            for eb in range(KD):
                pst = ps_m[eb % 2]
                b.wait("tensor", "dve")
                for kb in range(KD):
                    b.pe("matmul", pst[:, :], wvb[:, kb * D + eb * 128:kb * D + (eb + 1) * 128],
                         katb[:, kb * LC:(kb + 1) * LC], start=(kb == 0), stop=(kb == KD - 1))
                b.wait("sync", "dve")
                b.ld(x1t[:], x1[eb * 128:(eb + 1) * 128, sl])
                b.ld(att_t[:], att[eb * 128:(eb + 1) * 128, sl])
                b.wait("vector", "pe", "ld", "st", "act")
                b.v("tensor_tensor", ot[:], pst[:, :], rsig[:, eb * LC:(eb + 1) * LC], ALU.mult)
                b.v("tensor_tensor", ot[:], ot[:], x1t[:], ALU.add)
                b.v("tensor_tensor", ot[:], ot[:], att_t[:], ALU.add)
                b.bar(("gpsimd",))
                b.st(out[eb * 128:(eb + 1) * 128, sl], ot[:])
        b.bar()


PHASE_FN = {
    "ln0": emit_ln0,
    "m0": lambda b: emit_mamba_layer(b, 0),
    "m1": lambda b: emit_mamba_layer(b, 1),
    "m2": lambda b: emit_mamba_layer(b, 2),
    "m3": lambda b: emit_mamba_layer(b, 3),
    "mfin": emit_mfin,
    "ffn": emit_ffn,
    "ln1": emit_ln1,
    "attkv": emit_attkv,
    "attq": emit_attq,
    "rwkv": emit_rwkv,
}


def build_launch(phases):
    b = Build(phases)
    b.ext_in("C_ones128", (128, 1), F32)
    b.ext_in("C_ones1x128", (1, 128), F32)
    for p in phases:
        b.begin_phase(p)
        PHASE_FN[p](b)
    b.finish()
    return b



def _prep_shared(I):
    asc = np.ascontiguousarray
    d = {}
    d["C_ones128"] = np.ones((128, 1), np.float32)
    d["C_ones1x128"] = np.ones((1, 128), np.float32)
    sel = np.zeros((32, 32 * 128), np.float32)
    for q in range(32):
        sel[q, q * 128:(q + 1) * 128] = 1.0
    d["selBC"] = sel.astype(ml_dtypes.bfloat16)
    d["identb"] = np.eye(128, dtype=np.float32).astype(ml_dtypes.bfloat16)
    d["identf"] = np.eye(128, dtype=np.float32)
    for k in range(NLAY):
        d[f"ipW{k}"] = asc(I["L0_attn_in_proj_w"][k].T)
        d[f"xpW{k}"] = asc(I["L0_attn_x_proj_w"][k].T)
        d[f"dpW{k}"] = asc(I["L0_attn_dt_proj_w"][k].T)
        d[f"opW{k}"] = asc(I["L0_attn_out_proj_w"][k].T)
        d[f"cw{k}"] = asc(I["L0_attn_conv_w"][k])
        d[f"cb{k}"] = asc(I["L0_attn_conv_b"][k].reshape(NDB, 128).T)
        d[f"nw{k}"] = asc(I["L0_attn_norm_w"][k].reshape(KD, 128).T)
        d[f"nb{k}"] = asc(I["L0_attn_norm_b"][k].reshape(KD, 128).T)
        d[f"dtb{k}"] = asc(I["L0_attn_dt_proj_b"][k].reshape(NDB, 128).T)
        d[f"alog{k}"] = asc(I["L0_attn_A_log"][k])
        d[f"dskip{k}"] = asc(I["L0_attn_D_skip"][k].reshape(NDB, 128).T)
    d["normf_w"] = asc(I["L0_attn_normf_w"].reshape(KD, 128).T)
    d["normf_b"] = asc(I["L0_attn_normf_b"].reshape(KD, 128).T)
    d["WfcT"] = asc(I["L0_ffn_W_fc"].T)
    d["WprT"] = asc(I["L0_ffn_W_proj"].T)
    d["AfcT"] = asc(I["L0_ffn_A_fc"].T)
    d["BfcT"] = asc(I["L0_ffn_B_fc"].T)
    d["AprT"] = asc(I["L0_ffn_A_proj"].T)
    d["BprT"] = asc(I["L0_ffn_B_proj"].T)
    d["bfc"] = asc(I["L0_ffn_b_fc"].reshape(NFB, 128).T)
    d["bpr"] = asc(I["L0_ffn_b_proj"].reshape(KD, 128).T)
    d["WqT"] = asc(I["L1_attn_Wq"].T)
    d["WkT"] = asc(I["L1_attn_Wk"].T)
    d["WvT"] = asc(I["L1_attn_Wv"].T)
    d["bq"] = asc(I["L1_attn_bq"].reshape(KD, 128).T)
    d["bk"] = asc(I["L1_attn_bk"].reshape(KD, 128).T)
    d["bv"] = asc(I["L1_attn_bv"].reshape(KD, 128).T)
    d["WrT"] = asc(I["L1_ffn_W_r"].T)
    d["Wk2T"] = asc(I["L1_ffn_W_k"].T)
    d["Wv2T"] = asc(I["L1_ffn_W_v"].T)
    d["mur"] = asc(I["L1_ffn_mu_r"].reshape(KD, 128).T)
    d["muk"] = asc(I["L1_ffn_mu_k"].reshape(KD, 128).T)
    w = np.arange(4)[:, None, None]
    r = np.arange(128)[None, :, None]
    cc = np.arange(LC)[None, None, :]
    d["amask"] = np.ascontiguousarray(
        np.where(cc <= 128 * w + r, 0.0, -1e30).astype(np.float32).reshape(4, 128 * LC))
    return d


FUSED = True
LAUNCH_PLAN_UNFUSED = [["ln0"], ["m0"], ["m1"], ["m2"], ["m3"], ["mfin"],
                       ["ffn"], ["ln1"], ["attkv"], ["attq"], ["rwkv"]]
LAUNCH_PLAN = [list(ALL_PHASES)] if FUSED else LAUNCH_PLAN_UNFUSED


def kernel(**inputs):
    shared = _prep_shared(inputs)
    x_all = inputs["L0_attn_hidden_states"]
    ncores = 8
    xs = [np.ascontiguousarray(x_all[i].T) for i in range(ncores)]
    state = [dict() for _ in range(ncores)]

    for phases in LAUNCH_PLAN:
        b = build_launch(phases)
        in_maps = []
        for i in range(ncores):
            m = {}
            for name in b.inputs:
                if name.startswith("S_"):
                    m[name] = state[i][name]
                elif name == "x":
                    m[name] = xs[i]
                else:
                    m[name] = shared[name]
            in_maps.append(m)
        res = run_bass_kernel_spmd(b.nc, in_maps, core_ids=list(range(ncores)))
        for i in range(ncores):
            for name in b.outputs:
                state[i][name] = np.asarray(res.results[i][name])

    outs = np.stack([state[i]["out"].astype(np.float32).T for i in range(ncores)], axis=0)
    return np.ascontiguousarray(outs)


# version 06
# speedup vs baseline: 1.8445x; 1.2288x over previous
import numpy as np
import ml_dtypes
from contextlib import ExitStack

import concourse.bass as bass
import concourse.mybir as mybir
from concourse.bass_utils import run_bass_kernel_spmd

F32 = mybir.dt.float32
BF16 = mybir.dt.bfloat16
AF = mybir.ActivationFunctionType
ALU = mybir.AluOpType
AX = mybir.AxisListType

D = 1024
L = 4096
DI = 2048
DS = 16
DTR = 64
NLAY = 4
FF = 4096
NFB = FF // 128
LC = 512
NCH = L // LC
NDB = DI // 128
KD = D // 128
EPS = 1e-5
MAGIC = float(np.float32(1.5 * 2 ** 23))

ALL_PHASES = ["ln0", "m0", "m1", "m2", "m3", "mfin", "ffn", "ln1", "attkv", "attq", "rwkv"]

STREAMS = {
    "xn0":  ((D, L), F32, "ln0", ["m0", "ffn"]),
    "R1":   ((D, L), F32, "m0", ["m1"]),
    "R2":   ((D, L), F32, "m1", ["m2"]),
    "R3":   ((D, L), F32, "m2", ["m3"]),
    "R4":   ((D, L), F32, "m3", ["mfin"]),
    "x1p":  ((D, L), F32, "mfin", ["ffn"]),
    "x1":   ((D, L), F32, "ffn", ["ln1", "rwkv"]),
    "xn1":  ((D, L), F32, "ln1", ["attkv", "attq", "rwkv"]),
    "kfm":  ((D, L), BF16, "attkv", ["attq"]),
    "vtm":  ((L, D), BF16, "attkv", ["attq"]),
    "qfm":  ((D, L), BF16, "attkv", ["attq"]),
    "att":  ((D, L), F32, "attq", ["rwkv"]),
}
DEBUG_SCRATCH = False
DEBUG_PHASE_A = False
DEBUG_FFN = False
DEBUG_ATT = False

MSCRATCH = {
    "xc_s": ((DI, L), BF16),
    "z_s":  ((DI, L), BF16),
    "dt_s": ((DI, L), BF16),
    "bc_s": ((32, L), BF16),
    "ym_s": ((DI, L), BF16),
}


class Build:
    def __init__(self, phases):
        self.phases = phases
        self.nc = bass.Bass("TRN2", target_bir_lowering=False)
        self.ex = ExitStack()
        self.sems = {}
        self.cnt = {}
        self._phase_idx = 0
        self._new_sems("g")
        self.inputs = {}
        self.outputs = {}
        self.dram = {}
        self.uid = 0

    def _new_sems(self, tag):
        for s in ["ld", "st", "pe", "act", "dve", "gp"]:
            self.sems[s] = self.ex.enter_context(self.nc.semaphore(f"{s}_{tag}"))
            self.cnt[s] = 0

    def begin_phase(self, name):
        if self._phase_idx > 0:
            self.bar()
        self._phase_idx += 1
        self._new_sems(f"p{self._phase_idx}_{name}")

    def ext_in(self, name, shape, dtype):
        if name not in self.dram:
            t = self.nc.dram_tensor(name, list(shape), dtype, kind="ExternalInput")
            self.inputs[name] = t
            self.dram[name] = t.ap()
        return self.dram[name]

    def ext_out(self, name, shape, dtype):
        if name not in self.dram:
            t = self.nc.dram_tensor(name, list(shape), dtype, kind="ExternalOutput")
            self.outputs[name] = t
            self.dram[name] = t.ap()
        return self.dram[name]

    def internal(self, name, shape, dtype):
        if name not in self.dram:
            t = self.nc.dram_tensor(name, list(shape), dtype, kind="Internal")
            self.dram[name] = t.ap()
        return self.dram[name]

    def stream(self, name):
        shape, dtype, prod, cons = STREAMS[name]
        key = "S_" + name
        if key in self.dram:
            return self.dram[key]
        if prod not in self.phases:
            return self.ext_in(key, shape, dtype)
        if any(c not in self.phases for c in cons):
            return self.ext_out(key, shape, dtype)
        return self.internal(key, shape, dtype)

    def mscratch(self, name):
        shape, dtype = MSCRATCH[name]
        if DEBUG_SCRATCH:
            return self.ext_out("M_" + name, shape, dtype)
        return self.internal("M_" + name, shape, dtype)

    def _bump(self, instr, sem, amt=1):
        instr.then_inc(self.sems[sem], amt)
        self.cnt[sem] += amt

    def ld(self, out_ap, in_ap):
        self._bump(self.nc.sync.dma_start(out=out_ap, in_=in_ap), "ld", 16)

    def st(self, out_ap, in_ap):
        self._bump(self.nc.gpsimd.dma_start(out=out_ap, in_=in_ap), "st", 16)

    def pe(self, meth, *a, **k):
        self._bump(getattr(self.nc.tensor, meth)(*a, **k), "pe")

    @staticmethod
    def _outsz(a, k):
        ap = k.get("out", a[0] if a else None)
        try:
            sz = 1
            for d in ap.shape:
                sz *= d
            return sz
        except Exception:
            return 1 << 30

    def v(self, meth, *a, **k):
        self._bump(getattr(self.nc.vector, meth)(*a, **k), "dve")
        if self._outsz(a, k) <= 8192:
            self.nc.vector.wait_ge(self.sems["dve"], self.cnt["dve"])

    def s(self, meth, *a, **k):
        self._bump(getattr(self.nc.scalar, meth)(*a, **k), "act")
        if self._outsz(a, k) <= 8192:
            self.nc.scalar.wait_ge(self.sems["act"], self.cnt["act"])

    def g(self, meth, *a, **k):
        self._bump(getattr(self.nc.gpsimd, meth)(*a, **k), "gp")
        if self._outsz(a, k) <= 8192:
            self.nc.gpsimd.wait_ge(self.sems["gp"], self.cnt["gp"])

    def wait(self, eng_name, *sems):
        eng = getattr(self.nc, eng_name)
        for s in sems:
            if self.cnt[s] > 0:
                eng.wait_ge(self.sems[s], self.cnt[s])

    def bar(self, engines=("tensor", "vector", "scalar", "gpsimd", "sync")):
        for e in engines:
            self.wait(e, "ld", "st", "pe", "act", "dve", "gp")

    def finish(self):
        self.nc.sync.wait_ge(self.sems["st"], self.cnt["st"])
        self.ex.close()



def emit_ln_stream(b, tag, src_ap, dst_ap, w_col=None, b_col=None, add_ap=None):
    nc = b.nc
    with ExitStack() as stk:
        sb = lambda shape, dt, nm: stk.enter_context(nc.sbuf_tensor(f"{tag}_{nm}", list(shape), dt))
        ps = lambda shape, dt, nm: stk.enter_context(nc.psum_tensor(f"{tag}p_{nm}", list(shape), dt))
        xt = [sb([128, LC], F32, f"x{i}") for i in range(KD)]
        sq = sb([128, LC], F32, "sq")
        ot = sb([128, LC], F32, "ot")
        addt = sb([128, LC], F32, "add") if add_ap is not None else None
        mu = sb([1, LC], F32, "mu")
        ex2 = sb([1, LC], F32, "ex2")
        rstd = sb([1, LC], F32, "rstd")
        onesc = sb([128, 1], F32, "onesc")
        onesr = sb([1, 128], F32, "onesr")
        epsT = sb([1, 1], F32, "eps")
        ps_sum = ps([1, LC], F32, "sum")
        ps_sq = ps([1, LC], F32, "sqs")
        ps_mu = ps([128, LC], F32, "mub")
        ps_rs = ps([128, LC], F32, "rsb")
        b.ld(onesc[:], b.dram["C_ones128"])
        b.ld(onesr[:], b.dram["C_ones1x128"])
        b.v("memset", epsT[:], EPS)
        b.bar()
        for c in range(NCH):
            sl = slice(c * LC, (c + 1) * LC)
            b.bar()
            for kb in range(KD):
                b.ld(xt[kb][:], src_ap[kb * 128:(kb + 1) * 128, sl])
            b.wait("tensor", "ld")
            b.wait("scalar", "ld")
            for kb in range(KD):
                b.pe("matmul", ps_sum[:, :], onesc[:], xt[kb][:],
                     start=(kb == 0), stop=(kb == KD - 1))
            for kb in range(KD):
                if kb > 0:
                    b.wait("scalar", "pe")
                b.s("activation", out=sq[:], in_=xt[kb][:], func=AF.Square)
                b.wait("tensor", "act")
                b.pe("matmul", ps_sq[:, :], onesc[:], sq[:],
                     start=(kb == 0), stop=(kb == KD - 1))
            b.wait("vector", "pe")
            b.v("tensor_scalar_mul", mu[:, :], ps_sum[:, :], 1.0 / D)
            b.v("tensor_scalar_mul", ex2[:, :], ps_sq[:, :], 1.0 / D)
            b.v("tensor_tensor", rstd[:, :], mu[:, :], mu[:, :], ALU.mult)
            b.v("tensor_tensor", rstd[:, :], ex2[:, :], rstd[:, :], ALU.subtract)
            b.wait("scalar", "dve")
            b.s("activation", out=rstd[:], in_=rstd[:], func=AF.Sqrt, bias=epsT[0:1, 0:1])
            b.wait("vector", "act")
            b.v("reciprocal", rstd[:], rstd[:])
            b.wait("tensor", "dve")
            b.pe("matmul", ps_mu[:, :], onesr[:], mu[:, :], start=True, stop=True)
            b.pe("matmul", ps_rs[:, :], onesr[:], rstd[:, :], start=True, stop=True)
            b.wait("vector", "pe")
            b.wait("sync", "dve", "act", "st")
            for kb in range(KD):
                if add_ap is not None:
                    b.ld(addt[:], add_ap[kb * 128:(kb + 1) * 128, sl])
                b.v("tensor_tensor", ot[:], xt[kb][:], ps_mu[:, :], ALU.subtract)
                b.v("tensor_tensor", ot[:], ot[:], ps_rs[:, :], ALU.mult)
                if w_col is not None:
                    b.wait("scalar", "dve")
                    b.s("activation", out=ot[:], in_=ot[:], func=AF.Identity,
                        bias=b_col[:, kb:kb + 1], scale=w_col[:, kb:kb + 1])
                if add_ap is not None:
                    b.wait("vector", "ld")
                    if w_col is not None:
                        b.wait("vector", "act")
                    b.v("tensor_tensor", ot[:], ot[:], addt[:], ALU.add)
                b.bar(("gpsimd",))
                b.st(dst_ap[kb * 128:(kb + 1) * 128, sl], ot[:])
                b.wait("vector", "st")
                b.wait("scalar", "st")
                b.wait("sync", "st")
        b.bar()


def emit_w_to_bf16(b, tag, stk, src_ap, nk, width, layout_width=None):
    nc = b.nc
    lw = layout_width or width
    dst = stk.enter_context(nc.sbuf_tensor(f"{tag}_wb", [128, nk * lw], BF16))
    stg = stk.enter_context(nc.sbuf_tensor(f"{tag}_wstg", [128, width], F32))
    for kb in range(nk):
        b.wait("sync", "dve")
        b.ld(stg[:], src_ap[kb * 128:(kb + 1) * 128, :])
        b.wait("vector", "ld")
        b.v("tensor_copy", dst[:, kb * lw:kb * lw + width], stg[:])
    b.bar()
    return dst



def emit_ln0(b):
    x = b.ext_in("x", (D, L), F32)
    emit_ln_stream(b, "ln0", x, b.stream("xn0"))


def emit_ln1(b):
    emit_ln_stream(b, "ln1", b.stream("x1"), b.stream("xn1"))


def emit_mfin(b):
    x = b.ext_in("x", (D, L), F32)
    with ExitStack() as stk:
        nwc = stk.enter_context(b.nc.sbuf_tensor("mfin_nw", [128, KD], F32))
        nbc = stk.enter_context(b.nc.sbuf_tensor("mfin_nb", [128, KD], F32))
        b.ld(nwc[:], b.ext_in("normf_w", (128, KD), F32))
        b.ld(nbc[:], b.ext_in("normf_b", (128, KD), F32))
        b.bar()
        emit_ln_stream(b, "mfin", b.stream("R4"), b.stream("x1p"),
                       w_col=nwc, b_col=nbc, add_ap=x)


def emit_mamba_layer(b, k):
    nc = b.nc
    rin = b.stream("xn0") if k == 0 else b.stream(f"R{k}")
    rout = b.stream(f"R{k + 1}")
    xc_s = b.mscratch("xc_s")
    z_s = b.mscratch("z_s")
    dt_s = b.mscratch("dt_s")
    bc_s = b.mscratch("bc_s")
    ym_s = b.mscratch("ym_s")

    ipW = b.ext_in(f"ipW{k}", (D, 2 * DI), F32)
    xpW = b.ext_in(f"xpW{k}", (DI, 96), F32)
    dpW = b.ext_in(f"dpW{k}", (DTR, DI), F32)
    opW = b.ext_in(f"opW{k}", (DI, D), F32)
    cw = b.ext_in(f"cw{k}", (DI, 4), F32)
    cb = b.ext_in(f"cb{k}", (128, NDB), F32)
    nw = b.ext_in(f"nw{k}", (128, KD), F32)
    nb = b.ext_in(f"nb{k}", (128, KD), F32)
    dtb = b.ext_in(f"dtb{k}", (128, NDB), F32)
    alog = b.ext_in(f"alog{k}", (DI, DS), F32)
    dskip = b.ext_in(f"dskip{k}", (128, NDB), F32)
    selBC = b.ext_in("selBC", (32, 32 * 128), BF16)
    if DEBUG_PHASE_A:
        xnd_dbg = b.ext_out(f"dbg_xn{k}", (D, L), BF16)
        xcpre_dbg = b.ext_out(f"dbg_xcpre{k}", (DI, L), F32)

    with ExitStack() as stk:
        sb = lambda shape, dt, nm: stk.enter_context(nc.sbuf_tensor(f"A{k}_{nm}", list(shape), dt))
        ps = lambda shape, dt, nm: stk.enter_context(nc.psum_tensor(f"A{k}p_{nm}", list(shape), dt))
        ipWb = emit_w_to_bf16(b, f"A{k}ip", stk, ipW, KD, 2 * DI)
        xpWb = emit_w_to_bf16(b, f"A{k}xp", stk, xpW, NDB, 96)
        dpWb = sb([64, DI], BF16, "dpWb")
        wstg = sb([64, DI], F32, "dpstg")
        b.ld(wstg[:], dpW)
        b.wait("vector", "ld")
        b.v("tensor_copy", dpWb[:], wstg[:])
        cwt = sb([128, NDB * 4], F32, "cwt")
        cbt = sb([128, NDB], F32, "cbt")
        nwt = sb([128, KD], F32, "nwt")
        nbt = sb([128, KD], F32, "nbt")
        dtbt = sb([128, NDB], F32, "dtbt")
        onesc = sb([128, 1], F32, "onesc")
        onesr = sb([1, 128], F32, "onesr")
        epsT = sb([1, 1], F32, "eps")
        b.v("memset", epsT[:], EPS)
        b.ld(onesc[:], b.dram["C_ones128"])
        b.ld(onesr[:], b.dram["C_ones1x128"])
        for db in range(NDB):
            b.ld(cwt[:, db * 4:(db + 1) * 4], cw[db * 128:(db + 1) * 128, :])
        b.ld(cbt[:], cb)
        b.ld(nwt[:], nw)
        b.ld(nbt[:], nb)
        b.ld(dtbt[:], dtb)

        rt = [sb([128, LC], F32, f"rt{i}") for i in range(KD)]
        xnb = sb([128, KD * LC], BF16, "xnb")
        sq = [sb([128, LC], F32, f"sq{i}") for i in range(2)]
        mu = sb([1, LC], F32, "mu")
        ex2 = sb([1, LC], F32, "ex2")
        rstd = sb([1, LC], F32, "rstd")
        ext = [sb([128, LC + 3], F32, f"ext{i}") for i in range(2)]
        tail = [sb([128, 4], F32, f"tail{i}") for i in range(NDB)]
        xcf = [sb([128, LC], F32, f"xcf{i}") for i in range(2)]
        xcb = sb([128, NDB * LC], BF16, "xcb")
        zt = [sb([128, LC], BF16, f"zt{i}") for i in range(2)]
        xdbb = sb([96, LC], BF16, "xdbb")
        t1 = [sb([128, LC], F32, f"t1{i}") for i in range(2)]
        t2 = [sb([128, LC], F32, f"t2{i}") for i in range(2)]
        t2b_ = [sb([128, LC], BF16, f"t2b{i}") for i in range(2)]
        ps_sum = ps([1, LC], F32, "sum")
        ps_sq = ps([1, LC], F32, "sqs")
        ps_mu = ps([128, LC], F32, "mub")
        ps_rs = ps([128, LC], F32, "rsb")
        ps_a = [ps([128, LC], F32, f"a{i}") for i in range(2)]
        ps_x = ps([96, LC], F32, "x96")

        for db in range(NDB):
            b.v("memset", tail[db][:], 0.0)
        b.bar()
        nct = b.cnt
        sem = b.sems
        rt_reader = 0
        rt_dve = 0
        xcbR_pe = 0
        
        ps_evac = [0, 0]
        extR = [0, 0]
        siluM = [0, 0]
        convM = [0, 0]
        castM = 0
        ztR = [0, 0]
        t1R = [0, 0]
        t2R = [0, 0]
        t2stR = [0, 0]
        stat_reader = 0
        xdbbR_pe = 0
        xdbbR_st = 0
        xnb_pe = 0
        for c in range(NCH):
            sl = slice(c * LC, (c + 1) * LC)
            nc.sync.wait_ge(sem["pe"], rt_reader)
            nc.sync.wait_ge(sem["dve"], rt_dve)
            for kb in range(KD):
                b.ld(rt[kb][:], rin[kb * 128:(kb + 1) * 128, sl])
            ld_rt = nct["ld"]
            nc.tensor.wait_ge(sem["ld"], ld_rt)
            nc.tensor.wait_ge(sem["dve"], stat_reader)
            for kb in range(KD):
                b.pe("matmul", ps_sum[:, :], onesc[:], rt[kb][:],
                     start=(kb == 0), stop=(kb == KD - 1))
            nc.scalar.wait_ge(sem["ld"], ld_rt)
            sq_pe = [0, 0]
            for kb in range(KD):
                par = kb % 2
                nc.scalar.wait_ge(sem["pe"], sq_pe[par])
                b.s("activation", out=sq[par][:], in_=rt[kb][:], func=AF.Square)
                sq_m = nct["act"]
                nc.tensor.wait_ge(sem["act"], sq_m)
                b.pe("matmul", ps_sq[:, :], onesc[:], sq[par][:],
                     start=(kb == 0), stop=(kb == KD - 1))
                sq_pe[par] = nct["pe"]
            pe_stats = nct["pe"]
            nc.vector.wait_ge(sem["pe"], pe_stats)
            b.v("tensor_scalar_mul", mu[:, :], ps_sum[:, :], 1.0 / D)
            b.v("tensor_scalar_mul", ex2[:, :], ps_sq[:, :], 1.0 / D)
            b.v("tensor_tensor", rstd[:, :], mu[:, :], mu[:, :], ALU.mult)
            b.v("tensor_tensor", rstd[:, :], ex2[:, :], rstd[:, :], ALU.subtract)
            dvem = nct["dve"]
            nc.scalar.wait_ge(sem["dve"], dvem)
            b.s("activation", out=rstd[:], in_=rstd[:], func=AF.Sqrt, bias=epsT[0:1, 0:1])
            nc.vector.wait_ge(sem["act"], nct["act"])
            b.v("reciprocal", rstd[:], rstd[:])
            dvem = nct["dve"]
            nc.tensor.wait_ge(sem["dve"], dvem)
            b.pe("matmul", ps_mu[:, :], onesr[:], mu[:, :], start=True, stop=True)
            b.pe("matmul", ps_rs[:, :], onesr[:], rstd[:, :], start=True, stop=True)
            bc_m = nct["pe"]
            nc.vector.wait_ge(sem["pe"], bc_m)
            for kb in range(KD):
                par = kb % 2
                nc.vector.wait_ge(sem["act"], nct["act"])
                b.v("tensor_tensor", sq[par][:], rt[kb][:], ps_mu[:, :], ALU.subtract)
                b.v("tensor_tensor", sq[par][:], sq[par][:], ps_rs[:, :], ALU.mult)
                ap_m = nct["dve"]
                nc.scalar.wait_ge(sem["dve"], ap_m)
                if kb == 0:
                    nc.scalar.wait_ge(sem["pe"], xnb_pe)
                b.s("activation", out=xnb[:, kb * LC:(kb + 1) * LC], in_=sq[par][:], func=AF.Identity,
                    bias=nbt[:, kb:kb + 1], scale=nwt[:, kb:kb + 1])
            xnb_ready = nct["act"]
            rt_dve = nct["dve"]
            rt_reader = nct["pe"]
            stat_reader = nct["dve"]
            nc.tensor.wait_ge(sem["act"], xnb_ready)
            for db in range(NDB):
                par = db % 2
                nc.tensor.wait_ge(sem["act"], ps_evac[par])
                for kb in range(KD):
                    b.pe("matmul", ps_a[par][:, :],
                         ipWb[:, kb * 2 * DI + db * 128: kb * 2 * DI + (db + 1) * 128],
                         xnb[:, kb * LC:(kb + 1) * LC],
                         start=(kb == 0), stop=(kb == KD - 1))
                mm_m = nct["pe"]
                nc.scalar.wait_ge(sem["pe"], mm_m)
                nc.scalar.wait_ge(sem["dve"], extR[par])
                b.s("copy", ext[par][:, 3:3 + LC], ps_a[par][:, :])
                ps_evac[par] = nct["act"]
                nc.vector.wait_ge(sem["act"], ps_evac[par])
                nc.vector.wait_ge(sem["act"], siluM[par])
                b.v("tensor_copy", ext[par][:, 0:3], tail[db][:, 0:3])
                b.v("tensor_scalar", xcf[par][:], ext[par][:, 0:LC],
                    cwt[:, db * 4:db * 4 + 1], cbt[:, db:db + 1], op0=ALU.mult, op1=ALU.add)
                for tap in range(1, 4):
                    b.v("scalar_tensor_tensor", out=xcf[par][:], in0=ext[par][:, tap:tap + LC],
                        scalar=cwt[:, db * 4 + tap:db * 4 + tap + 1], in1=xcf[par][:],
                        op0=ALU.mult, op1=ALU.add)
                b.v("tensor_copy", tail[db][:, 0:3], ext[par][:, LC:LC + 3])
                convM[par] = nct["dve"]
                extR[par] = nct["dve"]
                nc.scalar.wait_ge(sem["dve"], convM[par])
                b.s("activation", out=xcf[par][:], in_=xcf[par][:], func=AF.Silu)
                siluM[par] = nct["act"]
                nc.vector.wait_ge(sem["act"], siluM[par])
                if c > 0 and db == 0:
                    nc.vector.wait_ge(sem["pe"], xcbR_pe)
                b.v("tensor_copy", xcb[:, db * LC:(db + 1) * LC], xcf[par][:])
                cast_m = nct["dve"]
                nc.gpsimd.wait_ge(sem["dve"], cast_m)
                b.st(xc_s[db * 128:(db + 1) * 128, sl], xcb[:, db * LC:(db + 1) * LC])
            for db in range(NDB):
                par = db % 2
                nc.tensor.wait_ge(sem["act"], ps_evac[par])
                for kb in range(KD):
                    b.pe("matmul", ps_a[par][:, :],
                         ipWb[:, kb * 2 * DI + DI + db * 128: kb * 2 * DI + DI + (db + 1) * 128],
                         xnb[:, kb * LC:(kb + 1) * LC],
                         start=(kb == 0), stop=(kb == KD - 1))
                mm_m = nct["pe"]
                nc.scalar.wait_ge(sem["pe"], mm_m)
                nc.scalar.wait_ge(sem["st"], ztR[par])
                b.s("copy", zt[par][:], ps_a[par][:, :])
                ps_evac[par] = nct["act"]
                nc.gpsimd.wait_ge(sem["act"], ps_evac[par])
                b.st(z_s[db * 128:(db + 1) * 128, sl], zt[par][:])
                ztR[par] = nct["st"]
                if db == NDB - 1:
                    xnb_pe = nct["pe"]
            cast_all = nct["dve"]
            nc.tensor.wait_ge(sem["dve"], cast_all)
            for db in range(NDB):
                b.pe("matmul", ps_x[:, :], xpWb[:, db * 96:(db + 1) * 96],
                     xcb[:, db * LC:(db + 1) * LC],
                     start=(db == 0), stop=(db == NDB - 1))
            mm_m = nct["pe"]
            xcbR_pe = mm_m
            nc.scalar.wait_ge(sem["pe"], mm_m)
            nc.scalar.wait_ge(sem["pe"], xdbbR_pe)
            nc.scalar.wait_ge(sem["st"], xdbbR_st)
            b.s("copy", xdbb[:], ps_x[:, :])
            xdbb_ready = nct["act"]
            nc.gpsimd.wait_ge(sem["act"], xdbb_ready)
            b.st(bc_s[:, sl], xdbb[64:96, :])
            xdbbR_st = nct["st"]
            for db in range(NDB):
                par = db % 2
                nc.tensor.wait_ge(sem["act"], xdbb_ready)
                nc.tensor.wait_ge(sem["act"], ps_evac[par])
                nc.tensor.wait_ge(sem["dve"], t2R[par])
                b.pe("matmul", ps_a[par][:, :], dpWb[:, db * 128:(db + 1) * 128],
                     xdbb[0:64, :], start=True, stop=True)
                mm_m = nct["pe"]
                nc.scalar.wait_ge(sem["pe"], mm_m)
                nc.scalar.wait_ge(sem["dve"], t1R[par])
                b.s("activation", out=t1[par][:], in_=ps_a[par][:, :], func=AF.Abs,
                    bias=dtbt[:, db:db + 1])
                nc.vector.wait_ge(sem["act"], nct["act"])
                b.v("tensor_scalar", t2[par][:], ps_a[par][:, :], dtbt[:, db:db + 1], 0.0,
                    op0=ALU.add, op1=ALU.max)
                ps_evac[par] = nct["act"]
                b.s("activation", out=t1[par][:], in_=t1[par][:], func=AF.Exp, scale=-1.0)
                b.s("activation", out=t1[par][:], in_=t1[par][:], func=AF.Ln, bias=1.0)
                t1_m = nct["act"]
                nc.vector.wait_ge(sem["act"], t1_m)
                nc.vector.wait_ge(sem["st"], t2stR[par])
                b.v("tensor_tensor", t2b_[par][:], t2[par][:], t1[par][:], ALU.add)
                t2R[par] = nct["dve"]
                t1R[par] = nct["dve"]
                nc.gpsimd.wait_ge(sem["dve"], t2R[par])
                b.st(dt_s[db * 128:(db + 1) * 128, sl], t2b_[par][:])
                t2stR[par] = nct["st"]
            xdbbR_pe = nct["pe"]
        b.bar()

    with ExitStack() as stk:
        sb = lambda shape, dt, nm: stk.enter_context(nc.sbuf_tensor(f"B{k}_{nm}", list(shape), dt))
        ps = lambda shape, dt, nm: stk.enter_context(nc.psum_tensor(f"B{k}p_{nm}", list(shape), dt))
        dtb = [sb([128, L], BF16, f"dt{m}") for m in range(2)]
        ub = [sb([128, L], BF16, f"u{m}") for m in range(2)]
        ya = [sb([128, L], F32, f"ya{m}") for m in range(2)]
        dAt = [sb([128, L], F32, f"dA{m}") for m in range(2)]
        Gt = [sb([128, L], BF16, f"G{m}") for m in range(2)]
        ht = sb([128, L], BF16, "h")
        hct = sb([128, L], BF16, "hc")
        brep = [sb([128, L], BF16, f"br{p}") for p in range(2)]
        crep = [sb([128, L], BF16, f"cr{p}") for p in range(2)]
        xcz = sb([128, L], BF16, "xcz")
        zt = sb([128, L], BF16, "z")
        bct = sb([32, L], BF16, "bc")
        selt = sb([32, 32 * 128], BF16, "sel")
        at = sb([128, DS * NDB], F32, "A")
        dst_ = sb([128, NDB], F32, "dsk")
        psB = [ps([128, LC], F32, f"b{i}") for i in range(2)]
        psC = [ps([128, LC], F32, f"c{i}") for i in range(2)]
        b.ld(selt[:], selBC)
        b.ld(bct[:], bc_s)
        b.ld(dst_[:], dskip)
        for db in range(NDB):
            b.ld(at[:, db * DS:(db + 1) * DS], alog[db * 128:(db + 1) * 128, :])
        b.bar()
        b.s("activation", out=at[:], in_=at[:], func=AF.Exp)
        b.wait("vector", "act")
        b.v("tensor_scalar_mul", at[:], at[:], -1.0)
        b.bar()
        NJ = L // LC
        nct = b.cnt
        sem = b.sems
        for g in range(NDB // 2):
            b.bar()
            for m in range(2):
                db = 2 * g + m
                b.ld(dtb[m][:], dt_s[db * 128:(db + 1) * 128, :])
                b.ld(xcz[:] if m == 1 else zt[:], xc_s[db * 128:(db + 1) * 128, :])
                b.wait("vector", "ld")
                b.v("tensor_tensor", ub[m][:], dtb[m][:], zt[:] if m == 0 else xcz[:], ALU.mult)
                if m == 0:
                    b.wait("sync", "dve")
            b.v("memset", ya[0][:], 0.0)
            b.v("memset", ya[1][:], 0.0)
            umark = nct["dve"]
            scan_mark = [umark, umark]
            dA_mark = [0, 0]
            evB_bank = [0, 0]
            evC_bank = [0, 0]
            brepR = [umark, umark]
            crepR = [umark, umark]
            for n in range(DS):
                par = n % 2
                for m in range(2):
                    col = (2 * g + m) * DS + n
                    nc.scalar.wait_ge(sem["dve"], scan_mark[m])
                    b.s("activation", out=dAt[m][:], in_=dtb[m][:], func=AF.Exp,
                        scale=at[:, col:col + 1])
                    dA_mark[m] = nct["act"]
                for j in range(NJ):
                    bank = j % 2
                    nc.tensor.wait_ge(sem["act"], evB_bank[bank])
                    b.pe("matmul", psB[bank][:, :], selt[:, n * 128:(n + 1) * 128],
                         bct[:, j * LC:(j + 1) * LC], start=True, stop=True)
                    mmk = nct["pe"]
                    nc.scalar.wait_ge(sem["pe"], mmk)
                    nc.scalar.wait_ge(sem["dve"], brepR[par])
                    b.s("copy", brep[par][:, j * LC:(j + 1) * LC], psB[bank][:, :])
                    evB_bank[bank] = nct["act"]
                evB_all = nct["act"]
                for j in range(NJ):
                    bank = j % 2
                    nc.tensor.wait_ge(sem["act"], evC_bank[bank])
                    b.pe("matmul", psC[bank][:, :], selt[:, (DS + n) * 128:(DS + n + 1) * 128],
                         bct[:, j * LC:(j + 1) * LC], start=True, stop=True)
                    mmk = nct["pe"]
                    nc.scalar.wait_ge(sem["pe"], mmk)
                    nc.scalar.wait_ge(sem["dve"], crepR[par])
                    b.s("copy", crep[par][:, j * LC:(j + 1) * LC], psC[bank][:, :])
                    evC_bank[bank] = nct["act"]
                evC_all = nct["act"]
                for m in range(2):
                    nc.vector.wait_ge(sem["act"], evB_all)
                    nc.vector.wait_ge(sem["act"], dA_mark[m])
                    b.v("tensor_tensor", Gt[m][:], ub[m][:], brep[par][:], ALU.mult)
                    b.v("tensor_tensor_scan", out=ht[:], data0=dAt[m][:], data1=Gt[m][:],
                        initial=0.0, op0=ALU.mult, op1=ALU.add)
                    scan_mark[m] = nct["dve"]
                    nc.vector.wait_ge(sem["act"], evC_all)
                    b.v("tensor_tensor", hct[:], ht[:], crep[par][:], ALU.mult)
                    b.v("tensor_tensor", ya[m][:], ya[m][:], hct[:], ALU.add)
                brepR[par] = nct["dve"]
                crepR[par] = nct["dve"]
            for m in range(2):
                db = 2 * g + m
                nc.sync.wait_ge(sem["dve"], nct["dve"])
                b.ld(xcz[:], xc_s[db * 128:(db + 1) * 128, :])
                b.ld(zt[:], z_s[db * 128:(db + 1) * 128, :])
                b.wait("scalar", "ld")
                b.s("activation", out=zt[:], in_=zt[:], func=AF.Silu)
                b.wait("vector", "ld")
                ymf = dAt[0]
                b.v("scalar_tensor_tensor", out=ymf[:], in0=xcz[:], scalar=dst_[:, db:db + 1],
                    in1=ya[m][:], op0=ALU.mult, op1=ALU.add)
                b.wait("vector", "act")
                b.v("tensor_tensor", hct[:], ymf[:], zt[:], ALU.mult)
                b.bar(("gpsimd",))
                b.st(ym_s[db * 128:(db + 1) * 128, :], hct[:])
                b.wait("sync", "st")
                b.wait("vector", "st")
        b.bar()

    with ExitStack() as stk:
        sb = lambda shape, dt, nm: stk.enter_context(nc.sbuf_tensor(f"C{k}_{nm}", list(shape), dt))
        ps = lambda shape, dt, nm: stk.enter_context(nc.psum_tensor(f"C{k}p_{nm}", list(shape), dt))
        opWb = emit_w_to_bf16(b, f"C{k}op", stk, opW, NDB, D)
        ymt = sb([128, NDB * LC], BF16, "ym")
        rt = sb([128, LC], F32, "rt")
        ot = sb([128, LC], F32, "ot")
        pso = [ps([128, LC], F32, f"o{i}") for i in range(2)]
        for c in range(NCH):
            sl = slice(c * LC, (c + 1) * LC)
            b.bar()
            for db in range(NDB):
                b.ld(ymt[:, db * LC:(db + 1) * LC], ym_s[db * 128:(db + 1) * 128, sl])
            b.wait("tensor", "ld")
            for ob in range(KD):
                pst = pso[ob % 2]
                b.wait("tensor", "dve")
                for db in range(NDB):
                    b.pe("matmul", pst[:, :], opWb[:, db * D + ob * 128:db * D + (ob + 1) * 128],
                         ymt[:, db * LC:(db + 1) * LC],
                         start=(db == 0), stop=(db == NDB - 1))
                b.wait("sync", "dve")
                b.ld(rt[:], rin[ob * 128:(ob + 1) * 128, sl])
                b.wait("vector", "pe", "ld", "st")
                b.v("tensor_tensor", ot[:], pst[:, :], rt[:], ALU.add)
                b.bar(("gpsimd",))
                b.st(rout[ob * 128:(ob + 1) * 128, sl], ot[:])
        b.bar()


def emit_ffn(b):
    nc = b.nc
    xn0 = b.stream("xn0")
    x1p = b.stream("x1p")
    x1 = b.stream("x1")
    WfcT = b.ext_in("WfcT", (D, FF), F32)
    WprT = b.ext_in("WprT", (FF, D), F32)
    AfcT = b.ext_in("AfcT", (D, 16), F32)
    BfcT = b.ext_in("BfcT", (16, FF), F32)
    AprT = b.ext_in("AprT", (FF, 16), F32)
    BprT = b.ext_in("BprT", (16, D), F32)
    bfc = b.ext_in("bfc", (128, NFB), F32)
    bpr = b.ext_in("bpr", (128, KD), F32)
    smax_d = b.internal("FF_smax", (1, 128), F32)
    if DEBUG_FFN:
        rfc_dbg = b.ext_out("dbg_rfc", (D, FF), BF16)
        sc_dbg = b.ext_out("dbg_sc", (128, 4), F32)
        h1_dbg = b.ext_out("dbg_h1", (FF, L), BF16)

    with ExitStack() as stk:
        sb = lambda shape, dt, nm: stk.enter_context(nc.sbuf_tensor(f"F_{nm}", list(shape), dt))
        ps = lambda shape, dt, nm: stk.enter_context(nc.psum_tensor(f"Fp_{nm}", list(shape), dt))
        rfc = sb([128, KD * FF], BF16, "rfc")
        rpr = sb([128, NFB * D], BF16, "rpr")
        sval = sb([1, 4], F32, "sval")
        svals = sb([1, 2], F32, "svals")
        sfcb = sb([128, 1], F32, "sfcb")
        sprb = sb([128, 1], F32, "sprb")
        rfcb = sb([128, 1], F32, "rfcb")
        rprb = sb([128, 1], F32, "rprb")
        rsval = sb([1, 2], F32, "rsval")
        onesr = sb([1, 128], F32, "onesr")
        AfcTb = sb([128, KD * 16], BF16, "AfcTb")
        BfcTb = sb([16, FF], BF16, "BfcTb")
        AprTb = sb([128, NFB * 16], BF16, "AprTb")
        BprTb = sb([16, D], BF16, "BprTb")
        bfct = sb([128, NFB], F32, "bfct")
        bprt = sb([128, KD], F32, "bprt")
        ps_bc = ps([128, 1], F32, "bc")
        stk2 = ExitStack()
        sb2 = lambda shape, dt, nm: stk2.enter_context(nc.sbuf_tensor(f"F2_{nm}", list(shape), dt))
        wstg = sb2([128, FF], F32, "wstg")
        rmax = sb2([128, 2], F32, "rmax")
        permax = sb2([128, 32], F32, "permax")
        rowall = sb2([1, 128 * 32], F32, "rowall")
        b.ld(onesr[:], b.dram["C_ones1x128"])
        b.ld(bfct[:], bfc)
        b.ld(bprt[:], bpr)
        for a in range(KD):
            b.ld(wstg[:, a * 16:(a + 1) * 16], AfcT[a * 128:(a + 1) * 128, :])
        for a in range(NFB):
            b.ld(wstg[:, 128 + a * 16:128 + (a + 1) * 16], AprT[a * 128:(a + 1) * 128, :])
        b.ld(wstg[0:16, 1024:1024 + D], BprT)
        b.wait("vector", "ld")
        b.v("tensor_copy", AfcTb[:], wstg[:, 0:KD * 16])
        b.v("tensor_copy", AprTb[:], wstg[:, 128:128 + NFB * 16])
        b.v("tensor_copy", BprTb[:], wstg[0:16, 1024:1024 + D])
        b.wait("sync", "dve")
        b.ld(wstg[0:16, 0:FF], BfcT)
        b.wait("vector", "ld")
        b.v("tensor_copy", BfcTb[:], wstg[0:16, 0:FF])
        b.bar()
        smax2 = b.internal("FF_smax2", (128, 32), F32)
        for which, W, nk, width, dstn in ((0, WfcT, KD, FF, rfc), (1, WprT, NFB, D, rpr)):
            b.v("memset", permax[:], 0.0)
            for kb in range(nk):
                b.wait("sync", "dve")
                b.ld(wstg[:, 0:width], W[kb * 128:(kb + 1) * 128, :])
                b.wait("vector", "ld")
                b.v("reduce_max", permax[:, kb:kb + 1], wstg[:, 0:width], axis=AX.X,
                    apply_absolute_value=True)
            b.bar(("gpsimd",))
            b.st(smax2, permax[:])
            b.wait("sync", "st")
            b.ld(rowall[:], smax2.rearrange("p n -> (p n)").rearrange("(o f) -> o f", o=1))
            b.wait("vector", "ld")
            b.v("reduce_max", sval[:, which:which + 1], rowall[:], axis=AX.X)
        b.v("tensor_scalar", svals[:, 0:2], sval[:, 0:2], 1.0 / 127.0, None, op0=ALU.mult)
        b.v("reciprocal", rsval[:, 0:2], svals[:, 0:2])
        b.wait("tensor", "dve")
        for src_c, dstb in ((svals[0:1, 0:1], sfcb), (svals[0:1, 1:2], sprb),
                            (rsval[0:1, 0:1], rfcb), (rsval[0:1, 1:2], rprb)):
            b.pe("matmul", ps_bc[:, :], onesr[:], src_c, start=True, stop=True)
            b.wait("scalar", "pe")
            b.s("copy", dstb[:], ps_bc[:, :])
            b.wait("tensor", "act")
        b.bar()
        for which, W, nk, width, dstn, scol in (
                (0, WfcT, KD, FF, rfc, rfcb), (1, WprT, NFB, D, rpr, rprb)):
            for kb in range(nk):
                b.wait("sync", "dve")
                b.ld(wstg[:, 0:width], W[kb * 128:(kb + 1) * 128, :])
                b.wait("vector", "ld")
                b.v("tensor_scalar", wstg[:, 0:width], wstg[:, 0:width], scol[:], None,
                    op0=ALU.mult)
                b.v("tensor_scalar", wstg[:, 0:width], wstg[:, 0:width], MAGIC, None,
                    op0=ALU.add)
                b.v("tensor_scalar", wstg[:, 0:width], wstg[:, 0:width], MAGIC, None,
                    op0=ALU.subtract)
                b.v("tensor_scalar", wstg[:, 0:width], wstg[:, 0:width], 127.0, -128.0,
                    op0=ALU.min, op1=ALU.max)
                b.v("tensor_copy", dstn[:, kb * width:(kb + 1) * width], wstg[:, 0:width])
        b.bar()
        stk2.close()
        if DEBUG_FFN:
            for kb in range(KD):
                b.st(rfc_dbg[kb * 128:(kb + 1) * 128, :], rfc[:, kb * FF:(kb + 1) * FF])
            for _i, _t in enumerate((sfcb, sprb, rfcb, rprb)):
                with b.nc.allow_non_contiguous_dma(reason="debug tap"):
                    b.st(sc_dbg[:, _i:_i + 1], _t[:])
            b.bar()
        xnt = sb([128, LC], F32, "xnt")
        xnbf = sb([128, KD * LC], BF16, "xnbf")
        t2b = sb([16, LC], BF16, "t2b")
        t4b = sb([16, LC], BF16, "t4b")
        h1b = sb([128, NFB * LC], BF16, "h1b")
        x1t2 = [sb([128, LC], F32, f"x1t{i}") for i in range(2)]
        ot2 = [sb([128, LC], F32, f"ot{i}") for i in range(2)]
        ps_l = ps([16, LC], F32, "l")
        ps_m = [ps([128, LC], F32, f"m{i}") for i in range(2)]
        nct = b.cnt
        sem = b.sems
        psm_evac = [0, 0]
        h1_pe = 0
        otM = [0, 0]
        otS = [0, 0]
        x1tR = [0, 0]
        xnb_pe = 0
        for c in range(NCH):
            sl = slice(c * LC, (c + 1) * LC)
            for kb in range(KD):
                nc.sync.wait_ge(sem["dve"], nct["dve"])
                b.ld(xnt[:], xn0[kb * 128:(kb + 1) * 128, sl])
                nc.vector.wait_ge(sem["ld"], nct["ld"])
                if kb == 0:
                    nc.vector.wait_ge(sem["pe"], xnb_pe)
                b.v("tensor_copy", xnbf[:, kb * LC:(kb + 1) * LC], xnt[:])
            cast_m = nct["dve"]
            nc.tensor.wait_ge(sem["dve"], cast_m)
            for kb in range(KD):
                b.pe("matmul", ps_l[:, :], AfcTb[:, kb * 16:(kb + 1) * 16],
                     xnbf[:, kb * LC:(kb + 1) * LC], start=(kb == 0), stop=(kb == KD - 1))
            lora_pe = nct["pe"]
            nc.vector.wait_ge(sem["pe"], lora_pe)
            b.v("tensor_scalar", t2b[:], ps_l[:, :], rfcb[0:16, :], 2.0,
                op0=ALU.mult, op1=ALU.mult)
            t2b_m = nct["dve"]
            for fb in range(NFB):
                par = fb % 2
                pst = ps_m[par]
                nc.tensor.wait_ge(sem["act"], psm_evac[par])
                if fb == 0:
                    nc.tensor.wait_ge(sem["dve"], t2b_m)
                for kb in range(KD):
                    b.pe("matmul", pst[:, :], rfc[:, kb * FF + fb * 128:kb * FF + (fb + 1) * 128],
                         xnbf[:, kb * LC:(kb + 1) * LC], start=(kb == 0), stop=False)
                b.pe("matmul", pst[:, :], BfcTb[:, fb * 128:(fb + 1) * 128], t2b[:],
                     start=False, stop=True)
                mm_m = nct["pe"]
                nc.scalar.wait_ge(sem["pe"], mm_m)
                if fb == 0:
                    nc.scalar.wait_ge(sem["pe"], h1_pe)
                b.s("activation", out=h1b[:, fb * LC:(fb + 1) * LC], in_=pst[:, :],
                    func=AF.Gelu, bias=bfct[:, fb:fb + 1], scale=sfcb[:])
                psm_evac[par] = nct["act"]
            gelu_all = nct["act"]
            nc.tensor.wait_ge(sem["act"], gelu_all)
            for fb in range(NFB):
                b.pe("matmul", ps_l[:, :], AprTb[:, fb * 16:(fb + 1) * 16],
                     h1b[:, fb * LC:(fb + 1) * LC], start=(fb == 0), stop=(fb == NFB - 1))
            lora2_pe = nct["pe"]
            nc.vector.wait_ge(sem["pe"], lora2_pe)
            b.v("tensor_scalar", t4b[:], ps_l[:, :], rprb[0:16, :], 2.0,
                op0=ALU.mult, op1=ALU.mult)
            t4b_m = nct["dve"]
            for ob in range(KD):
                par = ob % 2
                pst = ps_m[par]
                nc.tensor.wait_ge(sem["act"], psm_evac[par])
                if ob == 0:
                    nc.tensor.wait_ge(sem["dve"], t4b_m)
                for fb in range(NFB):
                    b.pe("matmul", pst[:, :], rpr[:, fb * D + ob * 128:fb * D + (ob + 1) * 128],
                         h1b[:, fb * LC:(fb + 1) * LC], start=(fb == 0), stop=False)
                b.pe("matmul", pst[:, :], BprTb[:, ob * 128:(ob + 1) * 128], t4b[:],
                     start=False, stop=True)
                mm_m = nct["pe"]
                h1_pe = mm_m
                nc.scalar.wait_ge(sem["pe"], mm_m)
                nc.scalar.wait_ge(sem["dve"], otM[par])
                b.s("activation", out=ot2[par][:], in_=pst[:, :], func=AF.Identity,
                    bias=bprt[:, ob:ob + 1], scale=sprb[:])
                psm_evac[par] = nct["act"]
                nc.sync.wait_ge(sem["dve"], x1tR[par])
                b.ld(x1t2[par][:], x1p[ob * 128:(ob + 1) * 128, sl])
                ld_m = nct["ld"]
                nc.vector.wait_ge(sem["ld"], ld_m)
                nc.vector.wait_ge(sem["act"], psm_evac[par])
                nc.vector.wait_ge(sem["st"], otS[par])
                b.v("tensor_tensor", ot2[par][:], ot2[par][:], x1t2[par][:], ALU.add)
                otM[par] = nct["dve"]
                x1tR[par] = nct["dve"]
                nc.gpsimd.wait_ge(sem["dve"], otM[par])
                b.st(x1[ob * 128:(ob + 1) * 128, sl], ot2[par][:])
                otS[par] = nct["st"]
        b.bar()


def emit_attkv(b):
    nc = b.nc
    xn1 = b.stream("xn1")
    kfm = b.stream("kfm")
    vtm = b.stream("vtm")
    qfm = b.stream("qfm")
    WqT = b.ext_in("WqT", (D, D), F32)
    WkT = b.ext_in("WkT", (D, D), F32)
    WvT = b.ext_in("WvT", (D, D), F32)
    bq = b.ext_in("bq", (128, KD), F32)
    bk = b.ext_in("bk", (128, KD), F32)

    with ExitStack() as stk:
        sb = lambda shape, dt, nm: stk.enter_context(nc.sbuf_tensor(f"KV_{nm}", list(shape), dt))
        ps = lambda shape, dt, nm: stk.enter_context(nc.psum_tensor(f"KVp_{nm}", list(shape), dt))
        xnbf = sb([128, KD * LC], BF16, "xnbf")
        xnt = sb([128, LC], F32, "xnt")
        bqt = sb([128, KD], F32, "bqt")
        bkt = sb([128, KD], F32, "bkt")
        ob2 = [sb([128, LC], BF16, f"ob{i}") for i in range(2)]
        ps_m = [ps([128, LC], F32, f"m{i}") for i in range(2)]
        b.ld(bqt[:], bq)
        b.ld(bkt[:], bk)
        wqb = emit_w_to_bf16(b, "KVq", stk, WqT, KD, D)
        wkb = emit_w_to_bf16(b, "KVk", stk, WkT, KD, D)
        wvb = emit_w_to_bf16(b, "KVv", stk, WvT, KD, D)
        nct = b.cnt
        sem = b.sems
        ps_ev = [0, 0]
        obS = [0, 0]
        xnb_pe = 0
        for c in range(NCH):
            sl = slice(c * LC, (c + 1) * LC)
            for kb in range(KD):
                nc.sync.wait_ge(sem["dve"], nct["dve"])
                b.ld(xnt[:], xn1[kb * 128:(kb + 1) * 128, sl])
                nc.vector.wait_ge(sem["ld"], nct["ld"])
                if kb == 0:
                    nc.vector.wait_ge(sem["pe"], xnb_pe)
                b.v("tensor_copy", xnbf[:, kb * LC:(kb + 1) * LC], xnt[:])
            cast_m = nct["dve"]
            nc.tensor.wait_ge(sem["dve"], cast_m)
            groups = [(wb, bcol, dstm, False) for wb, bcol, dstm in
                      ((wqb, bqt, qfm), (wkb, bkt, kfm))]
            for wb, bcol, dstm, _ in groups:
                for eb in range(KD):
                    par = eb % 2
                    nc.tensor.wait_ge(sem["act"], ps_ev[par])
                    for kb in range(KD):
                        b.pe("matmul", ps_m[par][:, :], wb[:, kb * D + eb * 128:kb * D + (eb + 1) * 128],
                             xnbf[:, kb * LC:(kb + 1) * LC], start=(kb == 0), stop=(kb == KD - 1))
                    mm_m = nct["pe"]
                    nc.scalar.wait_ge(sem["pe"], mm_m)
                    nc.scalar.wait_ge(sem["st"], obS[par])
                    b.s("activation", out=ob2[par][:], in_=ps_m[par][:, :], func=AF.Identity,
                        bias=bcol[:, eb:eb + 1])
                    ps_ev[par] = nct["act"]
                    nc.gpsimd.wait_ge(sem["act"], ps_ev[par])
                    b.st(dstm[eb * 128:(eb + 1) * 128, sl], ob2[par][:])
                    obS[par] = nct["st"]
            for lb in range(LC // 128):
                labs = c * LC + lb * 128
                for ec in range(2):
                    par = ec % 2
                    nc.tensor.wait_ge(sem["act"], ps_ev[par])
                    for kb in range(KD):
                        b.pe("matmul", ps_m[par][:, :],
                             xnbf[:, kb * LC + lb * 128:kb * LC + (lb + 1) * 128],
                             wvb[:, kb * D + ec * LC:kb * D + (ec + 1) * LC],
                             start=(kb == 0), stop=(kb == KD - 1))
                    mm_m = nct["pe"]
                    nc.scalar.wait_ge(sem["pe"], mm_m)
                    nc.scalar.wait_ge(sem["st"], obS[par])
                    b.s("copy", ob2[par][:], ps_m[par][:, :])
                    ps_ev[par] = nct["act"]
                    nc.gpsimd.wait_ge(sem["act"], ps_ev[par])
                    b.st(vtm[labs:labs + 128, ec * LC:(ec + 1) * LC], ob2[par][:])
                    obS[par] = nct["st"]
            xnb_pe = nct["pe"]
        b.bar()


def emit_attq(b):
    nc = b.nc
    kfm = b.stream("kfm")
    vtm = b.stream("vtm")
    qfm = b.stream("qfm")
    att = b.stream("att")
    bv = b.ext_in("bv", (128, KD), F32)
    amask = b.ext_in("amask", (4, 128 * LC), F32)
    identb = b.ext_in("identb", (128, 128), BF16)
    NLB = L // 128

    with ExitStack() as stk:
        sb = lambda shape, dt, nm: stk.enter_context(nc.sbuf_tensor(f"AQ_{nm}", list(shape), dt))
        ps = lambda shape, dt, nm: stk.enter_context(nc.psum_tensor(f"AQp_{nm}", list(shape), dt))
        kf = sb([128, KD * L], BF16, "kf")
        vt = sb([128, NLB * D], BF16, "vt")
        qt = [sb([128, KD * 128], BF16, f"qt{i}") for i in range(2)]
        sc = [sb([128, L], F32, f"sc{i}") for i in range(2)]
        pb_ = sb([128, L], BF16, "pb")
        pT = sb([128, NLB * 128], BF16, "pT")
        mask = sb([128, 4 * LC], F32, "mask")
        idt = sb([128, 128], BF16, "idt")
        bvt = sb([128, KD], F32, "bvt")
        negm = sb([128, 1], F32, "negm")
        negm2 = sb([128, 1], F32, "negm2")
        rsum = sb([128, 1], F32, "rsum")
        rsum2 = sb([128, 1], F32, "rsum2")
        ot = [sb([128, 128], F32, f"ot{i}") for i in range(2)]
        ps_s = [ps([128, LC], F32, f"s{i}") for i in range(2)]
        ps_t = [ps([128, 128], BF16, f"t{i}") for i in range(2)]
        ps_av = [ps([128, 128], F32, f"av{i}") for i in range(2)]
        for kb in range(KD):
            b.ld(kf[:, kb * L:(kb + 1) * L], kfm[kb * 128:(kb + 1) * 128, :])
        for lb in range(NLB):
            b.ld(vt[:, lb * D:(lb + 1) * D], vtm[lb * 128:(lb + 1) * 128, :])
        am = amask.rearrange("a (p f) -> (a p) f", p=128)
        for w in range(4):
            b.ld(mask[:, w * LC:(w + 1) * LC], am[w * 128:(w + 1) * 128, :])
        b.ld(idt[:], identb)
        b.ld(bvt[:], bv)
        b.bar()
        nct = b.cnt
        sem = b.sems
        ps_s_ev = [0, 0]
        ps_t_ev = [0, 0]
        ps_av_ev = [0, 0]
        scR = [0, 0]
        sc_ready = [0, 0]
        qtR = [0, 0]
        qld = [0, 0]
        pbR_pe = 0
        pb_act = 0
        otS = [0, 0]
        for t in range(NLB + 1):
            if t < NLB:
                i = t
                par = i % 2
                nchunks = i // 4 + 1
                nc.sync.wait_ge(sem["pe"], qtR[par])
                for kb in range(KD):
                    b.ld(qt[par][:, kb * 128:(kb + 1) * 128],
                         qfm[kb * 128:(kb + 1) * 128, i * 128:(i + 1) * 128])
                qld[par] = nct["ld"]
                for j in range(nchunks):
                    bank = j % 2
                    nc.tensor.wait_ge(sem["act"], ps_s_ev[bank])
                    if j == 0:
                        nc.tensor.wait_ge(sem["ld"], qld[par])
                    for kb in range(KD):
                        b.pe("matmul", ps_s[bank][:, :], qt[par][:, kb * 128:(kb + 1) * 128],
                             kf[:, kb * L + j * LC:kb * L + (j + 1) * LC],
                             start=(kb == 0), stop=(kb == KD - 1))
                    mm_m = nct["pe"]
                    nc.scalar.wait_ge(sem["pe"], mm_m)
                    if j == 0:
                        nc.scalar.wait_ge(sem["dve"], scR[par])
                    b.s("copy", sc[par][:, j * LC:(j + 1) * LC], ps_s[bank][:, :])
                    ps_s_ev[bank] = nct["act"]
                sc_ready[par] = nct["act"]
            if t > 0:
                i = t - 1
                par = i % 2
                nchunks = i // 4 + 1
                kcols = nchunks * LC
                w = i % 4
                nc.vector.wait_ge(sem["act"], sc_ready[par])
                b.v("tensor_tensor", sc[par][:, (nchunks - 1) * LC:kcols],
                    sc[par][:, (nchunks - 1) * LC:kcols], mask[:, w * LC:(w + 1) * LC], ALU.add)
                b.v("reduce_max", negm[:], sc[par][:, 0:kcols], axis=AX.X, negate=True)
                b.v("tensor_scalar_mul", negm2[:], negm[:], 1.0 / 32.0)
                nm_m = nct["dve"]
                nc.scalar.wait_ge(sem["dve"], nm_m)
                nc.scalar.wait_ge(sem["pe"], pbR_pe)
                b.s("activation", out=pb_[:, 0:kcols], in_=sc[par][:, 0:kcols], func=AF.Exp,
                    scale=1.0 / 32.0, bias=negm2[:], accum_out=rsum[:])
                pb_act = nct["act"]
                nc.vector.wait_ge(sem["act"], pb_act)
                b.v("reciprocal", rsum2[:], rsum[:])
                b.v("tensor_scalar", pb_[:, 0:kcols], pb_[:, 0:kcols], rsum2[:], None,
                    op0=ALU.mult)
                pn_m = nct["dve"]
                scR[par] = pn_m
                for kb2 in range(4 * nchunks):
                    bank = kb2 % 2
                    nc.tensor.wait_ge(sem["act"], ps_t_ev[bank])
                    if kb2 == 0:
                        nc.tensor.wait_ge(sem["dve"], pn_m)
                    b.pe("transpose", ps_t[bank][:, :], pb_[:, kb2 * 128:(kb2 + 1) * 128], idt[:])
                    mm_m = nct["pe"]
                    nc.scalar.wait_ge(sem["pe"], mm_m)
                    b.s("copy", pT[:, kb2 * 128:(kb2 + 1) * 128], ps_t[bank][:, :])
                    ps_t_ev[bank] = nct["act"]
                pbR_pe = nct["pe"]
                pT_ready = nct["act"]
                for db in range(KD):
                    bank = db % 2
                    nc.tensor.wait_ge(sem["act"], ps_av_ev[bank])
                    if db == 0:
                        nc.tensor.wait_ge(sem["act"], pT_ready)
                    for kb2 in range(4 * nchunks):
                        b.pe("matmul", ps_av[bank][:, :],
                             vt[:, kb2 * D + db * 128:kb2 * D + (db + 1) * 128],
                             pT[:, kb2 * 128:(kb2 + 1) * 128],
                             start=(kb2 == 0), stop=(kb2 == 4 * nchunks - 1))
                    mm_m = nct["pe"]
                    nc.scalar.wait_ge(sem["pe"], mm_m)
                    nc.scalar.wait_ge(sem["st"], otS[bank])
                    b.s("activation", out=ot[bank][:], in_=ps_av[bank][:, :], func=AF.Identity,
                        bias=bvt[:, db:db + 1])
                    ps_av_ev[bank] = nct["act"]
                    nc.gpsimd.wait_ge(sem["act"], ps_av_ev[bank])
                    b.st(att[db * 128:(db + 1) * 128, i * 128:(i + 1) * 128], ot[bank][:])
                    otS[bank] = nct["st"]
                qtR[par] = nct["pe"]
        b.bar()


def emit_rwkv(b):
    nc = b.nc
    xn1 = b.stream("xn1")
    x1 = b.stream("x1")
    att = b.stream("att")
    out = b.ext_out("out", (D, L), F32)
    WrT = b.ext_in("WrT", (D, D), F32)
    Wk2T = b.ext_in("Wk2T", (D, D), F32)
    Wv2T = b.ext_in("Wv2T", (D, D), F32)
    mur = b.ext_in("mur", (128, KD), F32)
    muk = b.ext_in("muk", (128, KD), F32)

    with ExitStack() as stk:
        sb = lambda shape, dt, nm: stk.enter_context(nc.sbuf_tensor(f"RW_{nm}", list(shape), dt))
        ps = lambda shape, dt, nm: stk.enter_context(nc.psum_tensor(f"RWp_{nm}", list(shape), dt))
        wrb = emit_w_to_bf16(b, "RWr", stk, WrT, KD, D)
        wkb = emit_w_to_bf16(b, "RWk", stk, Wk2T, KD, D)
        wvb = emit_w_to_bf16(b, "RWv", stk, Wv2T, KD, D)
        murt = sb([128, KD], F32, "murt")
        mukt = sb([128, KD], F32, "mukt")
        b.ld(murt[:], mur)
        b.ld(mukt[:], muk)
        xnt = sb([128, LC], F32, "xnt")
        xpt = sb([128, LC], F32, "xpt")
        dt_ = sb([128, LC], F32, "d")
        xrb = sb([128, KD * LC], BF16, "xrb")
        xkb = sb([128, KD * LC], BF16, "xkb")
        rsig = sb([128, KD * LC], F32, "rsig")
        katb = sb([128, KD * LC], BF16, "katb")
        t12 = [sb([128, LC], F32, f"t1{i}") for i in range(2)]
        x1t2 = [sb([128, LC], F32, f"x1t{i}") for i in range(2)]
        att2 = [sb([128, LC], F32, f"attt{i}") for i in range(2)]
        ot2 = [sb([128, LC], F32, f"ot{i}") for i in range(2)]
        ps_m = [ps([128, LC], F32, f"m{i}") for i in range(2)]
        b.bar()
        nct = b.cnt
        sem = b.sems
        ps_ev = [0, 0]
        ps_dve = [0, 0]
        t1M = [0, 0]
        otS = [0, 0]
        xr_pe = 0
        for c in range(NCH):
            sl = slice(c * LC, (c + 1) * LC)
            for kb in range(KD):
                nc.sync.wait_ge(sem["dve"], nct["dve"])
                b.ld(xnt[:], xn1[kb * 128:(kb + 1) * 128, sl])
                if c == 0:
                    nc.vector.wait_ge(sem["ld"], nct["ld"])
                    b.v("memset", xpt[:, 0:1], 0.0)
                    b.ld(xpt[:, 1:LC], xn1[kb * 128:(kb + 1) * 128, 0:LC - 1])
                else:
                    b.ld(xpt[:], xn1[kb * 128:(kb + 1) * 128, c * LC - 1:(c + 1) * LC - 1])
                nc.vector.wait_ge(sem["ld"], nct["ld"])
                if kb == 0:
                    nc.vector.wait_ge(sem["pe"], xr_pe)
                b.v("tensor_tensor", dt_[:], xnt[:], xpt[:], ALU.subtract)
                b.v("scalar_tensor_tensor", out=xrb[:, kb * LC:(kb + 1) * LC], in0=dt_[:],
                    scalar=murt[:, kb:kb + 1], in1=xpt[:], op0=ALU.mult, op1=ALU.add)
                b.v("scalar_tensor_tensor", out=xkb[:, kb * LC:(kb + 1) * LC], in0=dt_[:],
                    scalar=mukt[:, kb:kb + 1], in1=xpt[:], op0=ALU.mult, op1=ALU.add)
            shift_m = nct["dve"]
            nc.tensor.wait_ge(sem["dve"], shift_m)
            for eb in range(KD):
                par = eb % 2
                nc.tensor.wait_ge(sem["act"], ps_ev[par])
                for kb in range(KD):
                    b.pe("matmul", ps_m[par][:, :], wrb[:, kb * D + eb * 128:kb * D + (eb + 1) * 128],
                         xrb[:, kb * LC:(kb + 1) * LC], start=(kb == 0), stop=(kb == KD - 1))
                mm_m = nct["pe"]
                nc.scalar.wait_ge(sem["pe"], mm_m)
                b.s("activation", out=rsig[:, eb * LC:(eb + 1) * LC], in_=ps_m[par][:, :],
                    func=AF.Sigmoid)
                ps_ev[par] = nct["act"]
            for eb in range(KD):
                par = eb % 2
                nc.tensor.wait_ge(sem["act"], ps_ev[par])
                for kb in range(KD):
                    b.pe("matmul", ps_m[par][:, :], wkb[:, kb * D + eb * 128:kb * D + (eb + 1) * 128],
                         xkb[:, kb * LC:(kb + 1) * LC], start=(kb == 0), stop=(kb == KD - 1))
                mm_m = nct["pe"]
                nc.scalar.wait_ge(sem["pe"], mm_m)
                b.s("activation", out=t12[par][:], in_=ps_m[par][:, :], func=AF.Relu)
                ps_ev[par] = nct["act"]
                b.s("activation", out=katb[:, eb * LC:(eb + 1) * LC], in_=t12[par][:],
                    func=AF.Square)
            kat_m = nct["act"]
            nc.tensor.wait_ge(sem["act"], kat_m)
            for eb in range(KD):
                par = eb % 2
                nc.tensor.wait_ge(sem["dve"], ps_dve[par])
                for kb in range(KD):
                    b.pe("matmul", ps_m[par][:, :], wvb[:, kb * D + eb * 128:kb * D + (eb + 1) * 128],
                         katb[:, kb * LC:(kb + 1) * LC], start=(kb == 0), stop=(kb == KD - 1))
                mm_m = nct["pe"]
                nc.sync.wait_ge(sem["dve"], ps_dve[par])
                b.ld(x1t2[par][:], x1[eb * 128:(eb + 1) * 128, sl])
                b.ld(att2[par][:], att[eb * 128:(eb + 1) * 128, sl])
                ld_m = nct["ld"]
                nc.vector.wait_ge(sem["pe"], mm_m)
                nc.vector.wait_ge(sem["ld"], ld_m)
                nc.vector.wait_ge(sem["st"], otS[par])
                b.v("tensor_tensor", ot2[par][:], ps_m[par][:, :], rsig[:, eb * LC:(eb + 1) * LC], ALU.mult)
                b.v("tensor_tensor", ot2[par][:], ot2[par][:], x1t2[par][:], ALU.add)
                b.v("tensor_tensor", ot2[par][:], ot2[par][:], att2[par][:], ALU.add)
                ps_dve[par] = nct["dve"]
                nc.gpsimd.wait_ge(sem["dve"], ps_dve[par])
                b.st(out[eb * 128:(eb + 1) * 128, sl], ot2[par][:])
                otS[par] = nct["st"]
            xr_pe = nct["pe"]
        b.bar()


PHASE_FN = {
    "ln0": emit_ln0,
    "m0": lambda b: emit_mamba_layer(b, 0),
    "m1": lambda b: emit_mamba_layer(b, 1),
    "m2": lambda b: emit_mamba_layer(b, 2),
    "m3": lambda b: emit_mamba_layer(b, 3),
    "mfin": emit_mfin,
    "ffn": emit_ffn,
    "ln1": emit_ln1,
    "attkv": emit_attkv,
    "attq": emit_attq,
    "rwkv": emit_rwkv,
}


def build_launch(phases):
    b = Build(phases)
    b.ext_in("C_ones128", (128, 1), F32)
    b.ext_in("C_ones1x128", (1, 128), F32)
    for p in phases:
        b.begin_phase(p)
        PHASE_FN[p](b)
    b.finish()
    return b



def _prep_shared(I):
    asc = np.ascontiguousarray
    d = {}
    d["C_ones128"] = np.ones((128, 1), np.float32)
    d["C_ones1x128"] = np.ones((1, 128), np.float32)
    sel = np.zeros((32, 32 * 128), np.float32)
    for q in range(32):
        sel[q, q * 128:(q + 1) * 128] = 1.0
    d["selBC"] = sel.astype(ml_dtypes.bfloat16)
    d["identb"] = np.eye(128, dtype=np.float32).astype(ml_dtypes.bfloat16)
    d["identf"] = np.eye(128, dtype=np.float32)
    for k in range(NLAY):
        d[f"ipW{k}"] = asc(I["L0_attn_in_proj_w"][k].T)
        d[f"xpW{k}"] = asc(I["L0_attn_x_proj_w"][k].T)
        d[f"dpW{k}"] = asc(I["L0_attn_dt_proj_w"][k].T)
        d[f"opW{k}"] = asc(I["L0_attn_out_proj_w"][k].T)
        d[f"cw{k}"] = asc(I["L0_attn_conv_w"][k])
        d[f"cb{k}"] = asc(I["L0_attn_conv_b"][k].reshape(NDB, 128).T)
        d[f"nw{k}"] = asc(I["L0_attn_norm_w"][k].reshape(KD, 128).T)
        d[f"nb{k}"] = asc(I["L0_attn_norm_b"][k].reshape(KD, 128).T)
        d[f"dtb{k}"] = asc(I["L0_attn_dt_proj_b"][k].reshape(NDB, 128).T)
        d[f"alog{k}"] = asc(I["L0_attn_A_log"][k])
        d[f"dskip{k}"] = asc(I["L0_attn_D_skip"][k].reshape(NDB, 128).T)
    d["normf_w"] = asc(I["L0_attn_normf_w"].reshape(KD, 128).T)
    d["normf_b"] = asc(I["L0_attn_normf_b"].reshape(KD, 128).T)
    d["WfcT"] = asc(I["L0_ffn_W_fc"].T)
    d["WprT"] = asc(I["L0_ffn_W_proj"].T)
    d["AfcT"] = asc(I["L0_ffn_A_fc"].T)
    d["BfcT"] = asc(I["L0_ffn_B_fc"].T)
    d["AprT"] = asc(I["L0_ffn_A_proj"].T)
    d["BprT"] = asc(I["L0_ffn_B_proj"].T)
    d["bfc"] = asc(I["L0_ffn_b_fc"].reshape(NFB, 128).T)
    d["bpr"] = asc(I["L0_ffn_b_proj"].reshape(KD, 128).T)
    d["WqT"] = asc(I["L1_attn_Wq"].T)
    d["WkT"] = asc(I["L1_attn_Wk"].T)
    d["WvT"] = asc(I["L1_attn_Wv"].T)
    d["bq"] = asc(I["L1_attn_bq"].reshape(KD, 128).T)
    d["bk"] = asc(I["L1_attn_bk"].reshape(KD, 128).T)
    d["bv"] = asc(I["L1_attn_bv"].reshape(KD, 128).T)
    d["WrT"] = asc(I["L1_ffn_W_r"].T)
    d["Wk2T"] = asc(I["L1_ffn_W_k"].T)
    d["Wv2T"] = asc(I["L1_ffn_W_v"].T)
    d["mur"] = asc(I["L1_ffn_mu_r"].reshape(KD, 128).T)
    d["muk"] = asc(I["L1_ffn_mu_k"].reshape(KD, 128).T)
    w = np.arange(4)[:, None, None]
    r = np.arange(128)[None, :, None]
    cc = np.arange(LC)[None, None, :]
    d["amask"] = np.ascontiguousarray(
        np.where(cc <= 128 * w + r, 0.0, -1e30).astype(np.float32).reshape(4, 128 * LC))
    return d


FUSED = True
LAUNCH_PLAN_UNFUSED = [["ln0"], ["m0"], ["m1"], ["m2"], ["m3"], ["mfin"],
                       ["ffn"], ["ln1"], ["attkv"], ["attq"], ["rwkv"]]
LAUNCH_PLAN = [list(ALL_PHASES)] if FUSED else LAUNCH_PLAN_UNFUSED


def kernel(**inputs):
    shared = _prep_shared(inputs)
    x_all = inputs["L0_attn_hidden_states"]
    ncores = 8
    xs = [np.ascontiguousarray(x_all[i].T) for i in range(ncores)]
    state = [dict() for _ in range(ncores)]

    for phases in LAUNCH_PLAN:
        b = build_launch(phases)
        in_maps = []
        for i in range(ncores):
            m = {}
            for name in b.inputs:
                if name.startswith("S_"):
                    m[name] = state[i][name]
                elif name == "x":
                    m[name] = xs[i]
                else:
                    m[name] = shared[name]
            in_maps.append(m)
        res = run_bass_kernel_spmd(b.nc, in_maps, core_ids=list(range(ncores)))
        for i in range(ncores):
            for name in b.outputs:
                state[i][name] = np.asarray(res.results[i][name])

    outs = np.stack([state[i]["out"].astype(np.float32).T for i in range(ncores)], axis=0)
    return np.ascontiguousarray(outs)
